# version 11
# baseline (speedup 1.0000x reference)
import sys
sys.path.insert(0, '/opt/trn_rl_repo')
from contextlib import ExitStack
import numpy as np
import concourse.bass as bass
import concourse.mybir as mybir
from concourse.bass_utils import run_bass_kernel_spmd

F32 = mybir.dt.float32
BF16 = mybir.dt.bfloat16
AF = mybir.ActivationFunctionType
ALU = mybir.AluOpType
AX = mybir.AxisListType

ENGINES = ("tensor", "vector", "scalar", "gpsimd", "sync")


class Buf:
    __slots__ = ("name", "last_writer", "readers", "psum")

    def __init__(self, name, psum=False):
        self.name = name
        self.last_writer = None
        self.readers = []
        self.psum = psum


class DmaSem:
    __slots__ = ("handle", "count", "name")

    def __init__(self, handle, name):
        self.handle = handle
        self.count = 0
        self.name = name


class Op:
    __slots__ = ("engine", "fn", "seq", "inc", "sem_val", "deps", "dma_sem", "tag")

    def __init__(self, engine, fn, seq, dma_sem, tag):
        self.engine = engine
        self.fn = fn
        self.seq = seq
        self.inc = dma_sem is not None
        self.sem_val = None
        self.deps = []
        self.dma_sem = dma_sem
        self.tag = tag


class Tok:
    __slots__ = ("kind", "engine", "idx", "sem")

    def __init__(self, kind, engine=None, idx=None, sem=None):
        self.kind = kind
        self.engine = engine
        self.idx = idx
        self.sem = sem


class Prog:
    def __init__(self, nc, stack):
        self.nc = nc
        self.stack = stack
        self.ops = {e: [] for e in ENGINES}
        self.sems = {e: stack.enter_context(nc.semaphore(f"pg_{e}")) for e in ENGINES}
        self.n_bufs = 0
        self.n_dsems = 0

    def buf(self, name=None, psum=False):
        self.n_bufs += 1
        return Buf(name or f"b{self.n_bufs}", psum)

    def dma_sem(self, name=None):
        self.n_dsems += 1
        name = name or f"ds{self.n_dsems}"
        return DmaSem(self.stack.enter_context(self.nc.semaphore(name)), name)

    def _collect_deps(self, o, tok, reads, writes, war_free):
        deps = []

        def add(d):
            if d is tok:
                return
            if d.kind == "eng":
                deps.append(("eng", d.engine, d.idx))
            else:
                deps.append(("dma", d.sem, 16 * d.sem.count))

        for b in reads:
            if b.last_writer is not None:
                add(b.last_writer)
            if b.psum:
                for r in b.readers:
                    if r.kind == "eng" and r.engine != tok.engine:
                        add(r)
            b.readers.append(tok)
        for b in writes:
            if not war_free:
                if b.last_writer is not None:
                    add(b.last_writer)
                for r in b.readers:
                    add(r)
            b.readers = []
            b.last_writer = tok
        o.deps = deps

    def op(self, engine, fn, reads=(), writes=(), war_free=False, tag=""):
        seq = len(self.ops[engine])
        tok = Tok("eng", engine=engine, idx=seq)
        o = Op(engine, fn, seq, None, tag)
        self._collect_deps(o, tok, reads, writes, war_free)
        self.ops[engine].append(o)
        return tok

    def dma(self, qengine, fn, sem, reads=(), writes=(), war_free=False, tag=""):
        seq = len(self.ops[qengine])
        tok = Tok("dma", engine=qengine, idx=None, sem=sem)
        o = Op(qengine, fn, seq, sem, tag)
        self._collect_deps(o, tok, reads, writes, war_free)
        sem.count += 1
        tok.idx = sem.count
        self.ops[qengine].append(o)
        return tok

    def barrier(self, engine, toks, tag="barrier"):
        seq = len(self.ops[engine])
        o = Op(engine, lambda eng: eng.nop(), seq, None, tag)
        deps = []
        for d in toks:
            if d.kind == "eng":
                deps.append(("eng", d.engine, d.idx))
            else:
                deps.append(("dma", d.sem, 16 * d.sem.count))
        o.deps = deps
        self.ops[engine].append(o)
        return Tok("eng", engine=engine, idx=seq)

    def _resolve(self):
        for e in ENGINES:
            for o in self.ops[e]:
                kept = []
                for d in o.deps:
                    if d[0] == "eng":
                        if d[1] == "tensor" and e == "tensor":
                            continue
                        self.ops[d[1]][d[2]].inc = True
                    kept.append(d)
                o.deps = kept
        for e in ENGINES:
            v = 0
            for o in self.ops[e]:
                if o.dma_sem is not None:
                    continue
                if o.inc:
                    v += 1
                    o.sem_val = v

    def finalize(self, block):
        self._resolve()
        prog = self

        def section(e):
            def body(eng):
                watermark = {}
                for o in prog.ops[e]:
                    for d in o.deps:
                        if d[0] == "eng":
                            sem = prog.sems[d[1]]
                            val = prog.ops[d[1]][d[2]].sem_val
                            assert val is not None, (d, o.tag)
                        else:
                            sem = d[1].handle
                            val = d[2]
                        key = id(sem)
                        if watermark.get(key, -1) >= val:
                            continue
                        watermark[key] = val
                        eng.wait_ge(sem, val)
                    inst = o.fn(eng)
                    if o.dma_sem is not None:
                        inst.then_inc(o.dma_sem.handle, 16)
                    elif o.inc:
                        inst.then_inc(prog.sems[e], 1)
            return body

        for e in ENGINES:
            if not self.ops[e]:
                continue
            getattr(block, e)(section(e))


class Pool:

    def __init__(self, nc, stack, name, dtype, elems):
        self.t = stack.enter_context(nc.sbuf_tensor(name, [128, elems], dtype))
        self.elems = elems
        self.off = 0

    def alloc(self, n, align=64):
        o = -(-self.off // align) * align
        assert o + n <= self.elems, f"pool overflow: {o}+{n} > {self.elems}"
        self.off = o + n
        return o

    def view(self, off, n, p0=0, p1=128):
        return self.t[p0:p1, off:off + n]


CC = {}
_o = 0
for _nm, _n in [("bq", 8), ("bk", 8), ("bo", 8), ("lng", 8), ("lnb", 8),
                ("w1s", 32), ("w2s", 32), ("h1s", 32), ("h2s", 32), ("w3s", 8),
                ("eps", 1)]:
    CC[_nm] = _o
    _o += _n
NCC = _o


def default_cfg():
    return dict(D=1024, H=16, DH=4096, To=2048, Tf=4096)


def build_program(cfg):
    D, H, DH, To, Tf = cfg["D"], cfg["H"], cfg["DH"], cfg["To"], cfg["Tf"]
    ND = D // 128
    NP = H // 2
    NJ = DH // 128
    NKC = Tf // 128
    NSEG = Tf // 512
    NQT = To // 512
    NTC_F = Tf // 512
    NTC_O = To // 512
    TB = min(1024, To)
    NFT = To // TB
    NSB = TB // 512
    assert NP == ND and D == 1024

    nc = bass.Bass("TRN2", target_bir_lowering=False, debug=False,
                   dynamic_dma_scratch_size=16384)
    di = lambda n, s: nc.dram_tensor(n, s, F32, kind="ExternalInput").ap()
    xT = di("xT", [D, Tf])
    wq_d, wk_d, wv_d, wo_d = di("wq", [D, D]), di("wk", [D, D]), di("wv", [D, D]), di("wo", [D, D])
    w1t_d, w2t_d = di("w1t", [D, DH]), di("w2t", [D, DH])
    h1t_d, h2t_d = di("h1t", [DH, DH]), di("h2t", [DH, DH])
    w3t_d = di("w3t", [DH, D])
    ccol_d = di("ccol", [128, NCC])
    crow_d = di("crow", [1, D + 128])
    cones_d = di("cones", [128, 128])
    x1d = nc.dram_tensor("x1d", [D, To], F32, kind="Internal").ap()
    outT = nc.dram_tensor("outT", [D, To], F32, kind="ExternalOutput").ap()

    def fm(dram_ap, t0, t1, n_last=None):
        r = dram_ap.rearrange("(n p) t -> p n t", p=128)
        return r[:, :, t0:t1]

    stack = ExitStack()
    with stack:
        E = stack.enter_context
        p = Prog(nc, stack)

        _o_sc = 0
        _o_ws = _o_sc + 4 * 512
        _main = _o_ws + 2 * ND * 512
        _a_end = _main + ND * Tf + 2 * ND * To + Tf + NKC * 130 + 2 * 1024
        _b_end = _main + 2 * NJ * TB + ND * TB
        pb_elems = max(_a_end, _b_end) + 64
        pf_elems = (3 * 512 + 2 * ND * 512)
        PB = Pool(nc, stack, "PB", BF16, pb_elems)
        PF = Pool(nc, stack, "PF", F32, pf_elems)

        ccol_t = E(nc.sbuf_tensor("ccol_t", [128, NCC], F32))
        onesf_t = E(nc.sbuf_tensor("onesf_t", [128, 128], F32))
        ones_b = E(nc.sbuf_tensor("ones_b", [128, 128], BF16))
        bks_t = E(nc.sbuf_tensor("bks_t", [128, 8], F32))

        psP1a = E(nc.psum_tensor("psP1a", [128, 512], F32))
        psP1b = E(nc.psum_tensor("psP1b", [128, 512], F32))
        psST0 = E(nc.psum_tensor("psST0", [128, 1024], F32))
        psST1 = E(nc.psum_tensor("psST1", [128, 1024], F32))
        psPVa = E(nc.psum_tensor("psPVa", [65, 512], F32))
        psPVb = E(nc.psum_tensor("psPVb", [65, 512], F32))
        block = E(nc.Block())

        bP1a, bP1b = p.buf("psP1a", True), p.buf("psP1b", True)
        bST = [p.buf(f"psST{i}", True) for i in range(4)]
        bPVa, bPVb = p.buf("psPVa", True), p.buf("psPVb", True)
        _slots = [(psP1a, 0, bP1a), (psP1b, 0, bP1b),
                  (psST0, 0, bST[0]), (psST0, 512, bST[1]),
                  (psST1, 0, bST[2]), (psST1, 512, bST[3])]
        _si = [0]

        _slot_pool = [list(range(6))]

        def slot():
            pool = _slot_pool[0]
            h, lo, b = _slots[pool[_si[0] % len(pool)]]
            _si[0] += 1
            return h[:, lo:lo + 512], b

        bccol, bonesf, bonesb, bbks = (p.buf(x) for x in
            ["ccol", "onesf", "onesb", "bks"])

        sem_c = p.dma_sem("c")
        sem_cg = p.dma_sem("cg")
        sem_x = [p.dma_sem("x0"), p.dma_sem("x1")]
        sem_w = [p.dma_sem(f"w{i}") for i in range(4)]
        sem_bv = p.dma_sem("bv")
        sem_qg = [p.dma_sem("qg0"), p.dma_sem("qg1")]
        sem_gg = [p.dma_sem("gg0"), p.dma_sem("gg1")]
        sem_big = p.dma_sem("big")
        sem_st = p.dma_sem("st")

        p.dma("sync", lambda s: s.dma_start(ccol_t[:, :], ccol_d), sem_c, writes=[bccol])
        p.dma("sync", lambda s: s.dma_start(onesf_t[:, :], cones_d), sem_c, writes=[bonesf])
        p.dma("gpsimd", lambda g: g.dma_start(ones_b[:, :], cones_d), sem_cg, writes=[bonesb])
        p.op("vector", lambda v: v.tensor_scalar_mul(
            bks_t[:, :], ccol_t[:, CC["bk"]:CC["bk"] + 8], 0.125),
            reads=[bccol], writes=[bbks])

        def cc(nm, i=0):
            return ccol_t[:, CC[nm] + i:CC[nm] + i + 1]

        eps_ap = ccol_t[:, CC["eps"]:CC["eps"] + 1]

        o_sc = _o_sc
        o_ws = _o_ws
        o_xln = _main
        o_qs = o_xln + ND * Tf
        o_ao = o_qs + ND * To
        o_k = o_ao + ND * To
        o_v = o_k + Tf
        o_pt_m = o_v + NKC * 130
        o_scpy = _o_ws + 2048
        o_pt0 = _o_ws + 3072
        o_qg = _o_ws + 4096
        o_kb2 = _o_ws + 6144
        o_kb1 = _o_sc
        assert Tf <= 4096
        o_h0 = _main
        o_h1 = o_h0 + NJ * TB
        o_xln2 = o_h1 + NJ * TB

        xln = [PB.view(o_xln + i * Tf, Tf) for i in range(ND)]
        q_s = [PB.view(o_qs + i * To, To) for i in range(ND)]
        ao = [PB.view(o_ao + i * To, To) for i in range(ND)]
        K_hp = PB.view(o_k, Tf)
        V_hp = [PB.view(o_v + k * 130, 130) for k in range(NKC)]
        sc16 = [PB.view(o_sc + i * 512, 512) for i in range(4)]
        b_xln = [p.buf(f"xln{i}") for i in range(ND)]
        b_qs = [p.buf(f"qs{i}") for i in range(ND)]
        b_ao = [p.buf(f"ao{i}") for i in range(ND)]
        b_k = p.buf("K_hp")
        b_v = [p.buf(f"V{k}") for k in range(NKC)]
        b_pt = [p.buf("pt0"), p.buf("pt1"), p.buf("pt2")]

        def PTof(par):
            return o_pt0 if par == 0 else o_pt_m + (par - 1) * 1024
        scpy = [PB.view(o_scpy + i * 512, 512) for i in range(2)]
        b_scpy = [p.buf("scpy0"), p.buf("scpy1")]
        b_qg = [p.buf("qg0"), p.buf("qg1")]
        b_qgg = [p.buf("qgg0"), p.buf("qgg1")]
        b_kb1, b_kb2 = p.buf("kb1"), p.buf("kb2")
        qgv = [PB.view(o_qg + i * 512, 512) for i in range(4)]
        b_sc16 = [p.buf(f"sc16_{i}") for i in range(4)]
        b_ws = p.buf("wslab")
        b_wsV = p.buf("wslabV")
        b_xbs = [p.buf("xb0"), p.buf("xb1")]

        o_st = PF.alloc(3 * 512)
        mean_b = PF.view(o_st, 512)
        rstd_b = PF.view(o_st + 512, 512)
        s1f = PF.view(o_st + 1024, 512)
        b_stats = p.buf("stats")
        asc = [s1f, s1f]
        b_asc = [b_stats, b_stats]

        o_big1 = PF.alloc(ND * 512)
        o_rd = o_big1
        o_rb = o_big1 + 1024
        o_bv = o_big1 + 1536
        o_brow = o_big1 + 1664
        o_cm = o_big1 + 1792
        o_gc = o_big1 + 1920
        o_big2 = PF.alloc(ND * 512)
        big1 = [PF.view(o_big1 + i * 512, 512) for i in range(ND)]
        big2 = [PF.view(o_big2 + i * 512, 512) for i in range(ND)]
        b_stats = p.buf("stats")
        b_cm = [p.buf("cm0"), p.buf("cm1")]
        b_gc = [p.buf("gc0"), p.buf("gc1")]
        b_rd, b_rb, b_bv, b_brow = p.buf("rd"), p.buf("rb"), p.buf("bv"), p.buf("brow")
        b_big1, b_big2 = p.buf("big1"), p.buf("big2")
        b_x1d = p.buf("x1d_dram")

        def emit_stats(bf_chunks):
            n = len(bf_chunks)
            for i, (kind, xap, xbufs) in enumerate(bf_chunks):
                if kind == "f32":
                    dst, db = sc16[i % 2], b_sc16[i % 2]
                    p.op("vector", lambda v, xap=xap, dst=dst: v.tensor_copy(
                        dst[:, :], xap), reads=list(xbufs), writes=[db, b_kb1], tag="cast")
                    xap, xbufs = dst[:, :], [db]
                sq, bsq = sc16[2 + (i % 2)], b_sc16[2 + (i % 2)]
                p.op("vector", lambda v, xap=xap, sq=sq: v.tensor_mul(sq[:, :], xap, xap),
                     reads=list(xbufs), writes=[bsq, b_kb1], tag="sq")
                p.op("tensor", lambda t, xap=xap, i=i, n=n: t.matmul(
                    psP1a[:, :], ones_b[:, :], xap, start=(i == 0), stop=(i == n - 1)),
                    reads=[bonesb] + list(xbufs), writes=[bP1a], war_free=(i > 0), tag="sMM")
                p.op("tensor", lambda t, sq=sq, i=i, n=n: t.matmul(
                    psP1b[:, :], ones_b[:, :], sq[:, :], start=(i == 0), stop=(i == n - 1)),
                    reads=[bonesb, bsq], writes=[bP1b], war_free=(i > 0), tag="qMM")
            inv = 1.0 / (n * 128)
            p.op("vector", lambda v: v.tensor_scalar_mul(mean_b, psP1a[:, :], inv),
                 reads=[bP1a], writes=[b_stats])
            p.op("vector", lambda v: v.tensor_scalar_mul(s1f, psP1b[:, :], inv),
                 reads=[bP1b], writes=[b_stats])
            p.op("vector", lambda v: v.scalar_tensor_tensor(
                rstd_b, mean_b, 1.0, mean_b, op0=ALU.mult, op1=ALU.mult),
                reads=[b_stats], writes=[b_stats])
            p.op("vector", lambda v: v.tensor_sub(s1f, s1f, rstd_b),
                 reads=[b_stats], writes=[b_stats])
            p.op("scalar", lambda s: s.activation(s1f, s1f, AF.Sqrt, bias=eps_ap, scale=1.0),
                 reads=[b_stats, bccol], writes=[b_stats])
            p.op("vector", lambda v: v.reciprocal(rstd_b, s1f),
                 reads=[b_stats], writes=[b_stats])

        def cast_f32(srcs, bufs_read):
            return [("f32", ap, list(bufs_read)) for ap in srcs]

        xb = [[PB.view(o_ws + sl * ND * 512 + i * 512, 512) for i in range(ND)]
              for sl in range(2)]
        for tc in range(NTC_F if not cfg.get("skip_a1") else 0):
            sl = tc % 2
            dst = PB.view(o_ws + sl * ND * 512, ND * 512).rearrange(
                "p (n t) -> p n t", t=512)
            p.dma("gpsimd", lambda g, tc=tc, dst=dst: g.dma_start(
                dst, fm(xT, tc * 512, (tc + 1) * 512)),
                sem_x[sl], writes=[b_xbs[sl]], tag="xload")
            emit_stats([("bf", xb[sl][i][:, :], [b_xbs[sl]]) for i in range(ND)])
            for i in range(ND):
                a, ab = asc[i % 2], b_asc[i % 2]
                p.op("vector", lambda v, i=i, sl=sl, a=a: v.tensor_sub(
                    a, xb[sl][i][:, :], mean_b), reads=[b_xbs[sl], b_stats], writes=[ab])
                p.op("vector", lambda v, i=i, tc=tc, a=a: v.tensor_mul(
                    xln[i][:, tc * 512:(tc + 1) * 512], a, rstd_b),
                    reads=[ab, b_stats], writes=[b_xln[i]], tag="ln1ap")

        wsl_v = PB.view(o_ws, ND * 1024).rearrange("p (n o) -> p n o", o=1024)
        wq_sl = [PB.view(o_ws + i * 1024, 1024) for i in range(ND)]
        p.dma("gpsimd", lambda g: g.dma_start(
            wsl_v, wq_d.rearrange("(n p) o -> p n o", p=128)),
            sem_w[0], writes=[b_ws, b_wsV, b_xbs[0], b_xbs[1], b_kb2] + b_pt + b_qg + b_qgg, tag="wq")
        for tt in range(NQT if not cfg.get("skip_a2") else 0):
            t0 = tt * 512
            for o in range(ND):
                ps, bps = slot()
                for i in range(ND):
                    p.op("tensor", lambda t, i=i, o=o, t0=t0, ps=ps: t.matmul(
                        ps, wq_sl[i][:, o * 128:(o + 1) * 128],
                        xln[i][:, t0:t0 + 512], start=(i == 0), stop=(i == ND - 1)),
                        reads=[b_ws, b_xln[i]], writes=[bps], war_free=(i > 0), tag="qpMM")
                p.op("scalar", lambda s, o=o, ps=ps: s.activation(
                    big1[o], ps, AF.Identity, bias=cc("bq", o), scale=1.0),
                    reads=[bps, bccol], writes=[b_big1], tag="qevac")
            emit_stats(cast_f32([big1[o] for o in range(ND)], [b_big1]))
            for o in range(ND):
                a, ab = asc[o % 2], b_asc[o % 2]
                p.op("vector", lambda v, o=o, a=a: v.tensor_sub(a, big1[o], mean_b),
                     reads=[b_big1, b_stats], writes=[ab])
                p.op("vector", lambda v, a=a: v.tensor_mul(a, a, rstd_b),
                     reads=[ab, b_stats], writes=[ab])
                p.op("scalar", lambda s, o=o, t0=t0, a=a: s.activation(
                    q_s[o][:, t0:t0 + 512], a, AF.Identity,
                    bias=cc("lnb", o), scale=cc("lng", o)),
                    reads=[ab, bccol], writes=[b_qs[o]], tag="qln")

        if cfg.get("stop_after_A2"):
            p.barrier("sync", [Tok("dma", sem=sem_c, idx=sem_c.count)])
            p.finalize(block)
            nc._layout = dict(o_xln=o_xln, o_qs=o_qs, o_ao=o_ao, o_k=o_k, o_v=o_v,
                              o_qg=o_qg, o_sc=o_sc, o_ws=o_ws,
                              o_st=o_st, o_big1=o_big1, o_big2=o_big2)
            return nc

        vvíew = PB.t[:, o_v:o_v + NKC * 130].rearrange("p (k c) -> p k c", c=130)
        p.op("vector", lambda v: v.memset(vvíew[:, :, 64:65], 1.0),
             writes=list(b_v), tag="vones")
        p.op("vector", lambda v: v.memset(vvíew[:, :, 129:130], 1.0),
             writes=list(b_v), war_free=True, tag="vones")

        p.op("vector", lambda v: v.memset(PB.t[64:65, o_k:o_k + Tf], 1.0),
             writes=[b_k], tag="kones")
        p.op("vector", lambda v: v.memset(PB.t[64:65, o_kb1:o_kb1 + 2048], 1.0),
             writes=[b_kb1] + b_sc16, tag="kones")
        p.op("vector", lambda v: v.memset(PB.t[64:65, o_kb2:o_kb2 + 2048], 1.0),
             writes=[b_kb2], tag="kones")
        p.op("vector", lambda v: v.memset(PF.view(o_gc, 256), 0.0),
             writes=[b_gc[0], b_gc[1], b_big1], tag="gcinit")
        kvK = [PB.view(o_ws + i * 128, 128) for i in range(ND)]
        kvV = [PB.view(o_ws + 1024 + i * 128, 128) for i in range(ND)]
        bvrow = PF.view(o_brow, 128, 0, 1)

        def emit_kv(hp):
            c0 = hp * 128
            p.dma("gpsimd", lambda g, c0=c0: g.dma_start(
                PB.view(o_ws, ND * 128).rearrange("p (n o) -> p n o", o=128),
                wk_d.rearrange("(n p) o -> p n o", p=128)[:, :, c0:c0 + 128]),
                sem_w[0], writes=[b_ws], tag="wk")
            p.dma("gpsimd", lambda g, c0=c0: g.dma_start(
                PB.view(o_ws + 1024, ND * 128).rearrange("p (n o) -> p n o", o=128),
                wv_d.rearrange("(n p) o -> p n o", p=128)[:, :, c0:c0 + 128]),
                sem_w[1], writes=[b_wsV], tag="wv")
            p.dma("sync", lambda s, c0=c0: s.dma_start(
                bvrow, crow_d[0:1, c0:c0 + 128]), sem_bv, writes=[b_brow, b_big1], tag="bvrow")
            p.op("tensor", lambda t: t.matmul(
                psP1a[:, 0:128], onesf_t[0:1, :], bvrow,
                start=True, stop=True), reads=[bonesf, b_brow], writes=[bP1a], tag="bvMM")
            p.op("vector", lambda v: v.tensor_copy(PF.view(o_bv, 128), psP1a[:, 0:128]),
                 reads=[bP1a], writes=[b_bv, b_big1])
            for tc in range(NTC_F):
                for h in range(2):
                    ps, bps = slot()
                    for i in range(ND):
                        p.op("tensor", lambda t, i=i, tc=tc, h=h, ps=ps: t.matmul(
                            ps[0:64, :], kvK[i][:, h * 64:(h + 1) * 64],
                            xln[i][:, tc * 512:(tc + 1) * 512],
                            start=(i == 0), stop=(i == ND - 1)),
                            reads=[b_ws, b_xln[i]], writes=[bps], war_free=(i > 0),
                            tag="kMM")
                    if h == 0:
                        dst, dbuf = PB.t[0:64, o_k + tc * 512:o_k + (tc + 1) * 512], [b_k]
                    elif tc * 512 < 2048:
                        dst = PB.t[0:64, o_kb1 + tc * 512:o_kb1 + (tc + 1) * 512]
                        dbuf = [b_kb1] + b_sc16
                    else:
                        dst = PB.t[0:64, o_kb2 + tc * 512 - 2048:o_kb2 + (tc + 1) * 512 - 2048]
                        dbuf = [b_kb2]
                    p.op("scalar", lambda s, hp=hp, h=h, ps=ps, dst=dst: s.activation(
                        dst, ps[0:64, :], AF.Identity,
                        bias=bks_t[h * 64:(h + 1) * 64, hp:hp + 1], scale=0.125),
                        reads=[bps, bbks], writes=dbuf, tag="kevac")
            for kc in range(NKC):
                psh, lo, bps = _slots[_si[0] % len(_slots)]
                _si[0] += 1
                for i in range(ND):
                    p.op("tensor", lambda t, i=i, kc=kc, psh=psh, lo=lo: t.matmul(
                        psh[:, lo:lo + 128], xln[i][:, kc * 128:(kc + 1) * 128],
                        kvV[i][:, :], start=(i == 0), stop=(i == ND - 1)),
                        reads=[b_wsV, b_xln[i]], writes=[bps], war_free=(i > 0), tag="vMM")
                for h in range(2):
                    p.op("vector", lambda v, kc=kc, h=h, psh=psh, lo=lo: v.scalar_tensor_tensor(
                        V_hp[kc][:, h * 65:h * 65 + 64], psh[:, lo + h * 64:lo + h * 64 + 64],
                        1.0, PF.view(o_bv, 128)[:, h * 64:(h + 1) * 64],
                        op0=ALU.mult, op1=ALU.add),
                        reads=[bps, b_bv], writes=[b_v[kc]], tag="vevac")

        def k65(h, kc):
            if h == 0:
                return PB.t[0:65, o_k + kc * 128:o_k + (kc + 1) * 128], [b_k]
            if kc * 128 < 2048:
                return PB.t[0:65, o_kb1 + kc * 128:o_kb1 + (kc + 1) * 128], [b_kb1]
            return PB.t[0:65, o_kb2 + kc * 128 - 2048:o_kb2 + (kc + 1) * 128 - 2048], [b_kb2]

        def kseg(h, seg):
            if h == 0:
                return PB.t[0:64, o_k + seg * 512:o_k + (seg + 1) * 512], [b_k]
            if seg * 512 < 2048:
                return PB.t[0:64, o_kb1 + seg * 512:o_kb1 + (seg + 1) * 512], [b_kb1]
            return PB.t[0:64, o_kb2 + seg * 512 - 2048:o_kb2 + (seg + 1) * 512 - 2048], [b_kb2]

        def pass1_steps(hp, qt):
            c = hp
            t0 = qt * 512
            gsl = qt % 2
            steps = []

            def th0():
                p.op("vector", lambda v: v.tensor_copy(
                    qgv[gsl * 2 + 0][0:64, :], q_s[c][0:64, t0:t0 + 512]),
                    reads=[b_qs[c]], writes=[b_qg[gsl]], tag="qgA")
                p.dma("sync", lambda sy: sy.dma_start(
                    qgv[gsl * 2 + 1][0:64, :], q_s[c][64:128, t0:t0 + 512]),
                    sem_qg[gsl], reads=[b_qs[c]], writes=[b_qg[gsl]], tag="qgB")
            steps.append(th0)
            for qs in range(4):
                for seg in range(NSEG):
                    for h in range(2):
                        def th(qs=qs, h=h, seg=seg):
                            k = (qs * NSEG * 2 + seg * 2 + h) % 2
                            pa, pb = (psP1a, bP1a) if k == 0 else (psP1b, bP1b)
                            sc_, bsc_ = scpy[k], b_scpy[k]
                            rhs, rbuf = kseg(h, seg)
                            p.op("tensor", lambda t, rhs=rhs, qs=qs, h=h: t.matmul(
                                pa[:, :],
                                qgv[gsl * 2 + h][0:64, qs * 128:(qs + 1) * 128],
                                rhs, start=True, stop=True),
                                reads=[b_qg[gsl]] + rbuf, writes=[pb], tag="s1MM")
                            p.op("vector", lambda v, sc_=sc_: v.tensor_copy(
                                sc_[:, :], pa[:, :]),
                                reads=[pb], writes=[bsc_], tag="s1cp")
                            col = o_cm + gsl * 64 + qs * 16 + h * 8 + seg
                            p.op("vector", lambda v, sc_=sc_, col=col: v.tensor_scalar(
                                sc_[:, :], sc_[:, :], -1.0, None,
                                op0=ALU.mult, op1=ALU.min,
                                accum_out=PF.view(col, 1)),
                                reads=[bsc_], writes=[bsc_, b_cm[gsl], b_big1],
                                tag="s1max")
                        steps.append(th)

                def th2(qs=qs):
                    gc0 = o_gc + gsl * 128
                    base = o_cm + gsl * 64 + qs * 16
                    for h in range(2):
                        col = gc0 + qs * 2 + h
                        p.op("vector", lambda v, h=h, col=col: v.tensor_reduce(
                            PF.view(col, 1), PF.view(base + h * 8, NSEG),
                            axis=AX.X, op=ALU.min),
                            reads=[b_cm[gsl]], writes=[b_gc[gsl], b_big1], tag="gmin")
                        p.dma("gpsimd", lambda g, h=h, col=col: g.dma_start(
                            PB.t[64:65, o_qg + (gsl * 2 + h) * 512 + qs * 128:
                                 o_qg + (gsl * 2 + h) * 512 + qs * 128 + 128],
                            PF.t[0:128, col:col + 1]),
                            sem_gg[gsl], reads=[b_gc[gsl]], writes=[b_qgg[gsl]],
                            tag="grow")
                steps.append(th2)
            return steps

        def emit_pass2(hp, qt, nxt):
            c = hp
            t0 = qt * 512
            gsl = qt % 2

            def emit_pv(kc):
                par = kc % 3
                for h in range(2):
                    psv, bpv = (psPVa, bPVa) if h == 0 else (psPVb, bPVb)
                    p.op("tensor", lambda t, kc=kc, h=h, par=par, psv=psv: t.matmul(
                        psv[:, :], V_hp[kc][:, h * 65:h * 65 + 65],
                        PB.t[:, PTof(par) + h * 512:
                             PTof(par) + (h + 1) * 512],
                        start=(kc == 0), stop=(kc == NKC - 1)),
                        reads=[b_v[kc], b_pt[par]], writes=[bpv],
                        war_free=(kc > 0), tag="pvMM")

            for b2 in range(NKC // 2):
                kcs = (2 * b2, 2 * b2 + 1)
                for kc in kcs:
                    pst = psST0 if kc % 2 == 0 else psST1
                    bsth = [bST[2 * (kc % 2)], bST[2 * (kc % 2) + 1]]
                    for h in range(2):
                        lhsT, lbuf = k65(h, kc)
                        p.op("tensor", lambda t, h=h, pst=pst, lhsT=lhsT: t.matmul(
                            pst[:, h * 512:(h + 1) * 512], lhsT,
                            qgv[gsl * 2 + h][0:65, :],
                            start=True, stop=True),
                            reads=lbuf + [b_qg[gsl], b_qgg[gsl]], writes=[bsth[h]],
                            tag="stMM")
                for _ in range(4):
                    if nxt:
                        nxt.pop(0)()
                for kc in kcs:
                    if kc - 4 >= 0:
                        emit_pv(kc - 4)
                for kc in kcs:
                    pst = psST0 if kc % 2 == 0 else psST1
                    bsth = [bST[2 * (kc % 2)], bST[2 * (kc % 2) + 1]]
                    par3 = kc % 3
                    p.op("scalar", lambda s, par3=par3, pst=pst: s.activation(
                        PB.t[:, PTof(par3):PTof(par3) + 1024],
                        pst[:, :], AF.Exp), reads=bsth, writes=[b_pt[par3]], tag="exp")
            for kc in range(max(0, NKC - 4), NKC):
                emit_pv(kc)
            rd = PF.view(o_rd, 1024, 64, 65)
            p.op("vector", lambda v: v.reciprocal(rd[:, 0:512], psPVa[64:65, :]),
                 reads=[bPVa], writes=[b_rd, b_big1], tag="recipA")
            p.op("vector", lambda v: v.reciprocal(rd[:, 512:1024], psPVb[64:65, :]),
                 reads=[bPVb], writes=[b_rd], war_free=True, tag="recipB")
            p.op("tensor", lambda t: t.matmul(
                psP1a[0:64, :], onesf_t[64:65, 0:64], rd[:, 0:512],
                start=True, stop=True), reads=[bonesf, b_rd], writes=[bP1a], tag="rbA")
            p.op("tensor", lambda t: t.matmul(
                psP1b[64:128, :], onesf_t[64:65, 0:64], rd[:, 512:1024],
                start=True, stop=True, tile_position=(64, 64)),
                reads=[bonesf, b_rd], writes=[bP1b], tag="rbB")
            p.op("vector", lambda v: v.tensor_copy(PF.view(o_rb, 512, 0, 64),
                 psP1a[0:64, :]), reads=[bP1a], writes=[b_rb, b_big1])
            p.op("vector", lambda v: v.tensor_copy(PF.view(o_rb, 512, 64, 128),
                 psP1b[64:128, :]), reads=[bP1b], writes=[b_rb], war_free=True)
            for h in range(2):
                psv, bpv = (psPVa, bPVa) if h == 0 else (psPVb, bPVb)
                p.op("vector", lambda v, h=h, psv=psv: v.scalar_tensor_tensor(
                    PF.view(o_st + 1024, 512, h * 64, (h + 1) * 64), psv[0:64, :], 1.0,
                    PF.view(o_rb, 512, h * 64, (h + 1) * 64),
                    op0=ALU.mult, op1=ALU.mult),
                    reads=[bpv, b_rb], writes=[b_stats],
                    war_free=(h == 1), tag="numdiv")
                p.op("vector", lambda v, h=h: v.tensor_add(
                    ao[c][h * 64:(h + 1) * 64, t0:t0 + 512],
                    PF.view(o_st + 1024, 512, h * 64, (h + 1) * 64),
                    q_s[c][h * 64:(h + 1) * 64, t0:t0 + 512]),
                    reads=[b_stats, b_qs[c]], writes=[b_ao[c]],
                    war_free=(h == 1), tag="aoadd")

        for hp in range(NP if not cfg.get("skip_attn") else 0):
            emit_kv(hp)
            for th in pass1_steps(hp, 0):
                th()
            for qt in range(NQT):
                nxt = pass1_steps(hp, qt + 1) if qt + 1 < NQT else []
                emit_pass2(hp, qt, nxt)
                for th in nxt:
                    th()

        p.dma("gpsimd", lambda g: g.dma_start(
            wsl_v, wo_d.rearrange("(n p) o -> p n o", p=128)),
            sem_w[0], writes=[b_ws, b_wsV, b_kb2] + b_pt + b_qg + b_qgg, tag="wo")
        for tt in range(NTC_O if not cfg.get("skip_attn") else 0):
            t0 = tt * 512
            p.dma("sync", lambda s, t0=t0: s.dma_start(
                PF.view(o_big2, ND * 512).rearrange("p (n t) -> p n t", t=512),
                fm(xT, t0, t0 + 512)), sem_big, writes=[b_big2], tag="xown")
            for o in range(ND):
                ps, bps = slot()
                for i in range(ND):
                    p.op("tensor", lambda t, i=i, o=o, t0=t0, ps=ps: t.matmul(
                        ps, wq_sl[i][:, o * 128:(o + 1) * 128], ao[i][:, t0:t0 + 512],
                        start=(i == 0), stop=(i == ND - 1)),
                        reads=[b_ws, b_ao[i]], writes=[bps], war_free=(i > 0), tag="woMM")
                p.op("scalar", lambda s, o=o, ps=ps: s.activation(
                    big1[o], ps, AF.Identity, bias=cc("bo", o), scale=1.0),
                    reads=[bps, bccol], writes=[b_big1], tag="woevac")
            emit_stats(cast_f32([big1[o] for o in range(ND)], [b_big1]))
            for o in range(ND):
                a, ab = asc[o % 2], b_asc[o % 2]
                p.op("vector", lambda v, o=o, a=a: v.tensor_sub(a, big1[o], mean_b),
                     reads=[b_big1, b_stats], writes=[ab])
                p.op("vector", lambda v, a=a: v.tensor_mul(a, a, rstd_b),
                     reads=[ab, b_stats], writes=[ab])
                p.op("vector", lambda v, o=o, a=a: v.tensor_add(big2[o], big2[o], a),
                     reads=[ab, b_big2], writes=[b_big2], tag="x1add")
            p.dma("sync", lambda s, t0=t0: s.dma_start(
                fm(x1d, t0, t0 + 512),
                PF.view(o_big2, ND * 512).rearrange("p (n t) -> p n t", t=512)),
                sem_big, reads=[b_big2], writes=[b_x1d], tag="x1st")

        if cfg.get("stop_after_A"):
            p.barrier("sync", [Tok("dma", sem=sem_big, idx=sem_big.count)])
            p.finalize(block)
            nc._layout = dict(o_xln=o_xln, o_qs=o_qs, o_ao=o_ao, o_k=o_k, o_v=o_v,
                              o_qg=o_qg, o_sc=o_sc, o_ws=o_ws,
                              o_st=o_st, o_big1=o_big1, o_big2=o_big2)
            return nc

        a_bufs = (b_xln + b_qs + b_ao + [b_k, b_kb1, b_kb2] + b_v + b_pt + b_qg + b_qgg +
                  [b_ws, b_wsV, b_xbs[0], b_xbs[1]])
        fence = p.op("vector", lambda v: v.engine_nop(), writes=a_bufs, tag="fence")

        h0 = [PB.view(o_h0 + j * TB, TB) for j in range(NJ)]
        h1v = [PB.view(o_h1 + j * TB, TB) for j in range(NJ)]
        xln2 = [PB.view(o_xln2 + i * TB, TB) for i in range(ND)]
        b_h0 = [p.buf(f"h0_{j}") for j in range(NJ)]
        b_h1 = [p.buf(f"h1_{j}") for j in range(NJ)]
        b_xln2 = [p.buf(f"xln2_{i}") for i in range(ND)]
        fsl = [PB.view(o_ws + s * 2048, 2048) for s in range(4)]
        b_fsl = [p.buf(f"fsl{s}") for s in range(4)]
        for b in b_h0 + b_h1 + b_xln2 + b_fsl:
            b.last_writer = fence
        _slot_pool[0] = [2, 3, 4, 5]

        def _stop(bufs):
            t = p.op("vector", lambda v: v.engine_nop(), reads=bufs, tag="stopper")
            p.barrier("sync", [t, Tok("dma", sem=sem_big, idx=sem_big.count)])
            p.finalize(block)
            nc._layout = dict(o_xln=o_xln, o_qs=o_qs, o_ao=o_ao, o_k=o_k, o_v=o_v,
                              o_qg=o_qg, o_sc=o_sc, o_ws=o_ws,
                              o_h0=o_h0, o_h1=o_h1, o_xln2=o_xln2,
                              o_st=o_st, o_big1=o_big1, o_big2=o_big2)
            return nc

        def emit_ln2(ft):
            tb0 = ft * TB
            for sb in range(NSB if not cfg.get("skip_ln2") else 0):
                t0 = tb0 + sb * 512
                p.dma("sync", lambda s, t0=t0: s.dma_start(
                    PF.view(o_big2, ND * 512).rearrange("p (n t) -> p n t", t=512),
                    fm(x1d, t0, t0 + 512)), sem_big, reads=[b_x1d],
                    writes=[b_big2], tag="x1ld")
                emit_stats(cast_f32([big2[i] for i in range(ND)], [b_big2]))
                for i in range(ND):
                    a, ab = asc[i % 2], b_asc[i % 2]
                    p.op("vector", lambda v, i=i, a=a: v.tensor_sub(a, big2[i], mean_b),
                         reads=[b_big2, b_stats], writes=[ab])
                    p.op("vector", lambda v, i=i, sb=sb, a=a: v.tensor_mul(
                        xln2[i][:, sb * 512:(sb + 1) * 512], a, rstd_b),
                        reads=[ab, b_stats], writes=[b_xln2[i]], tag="ln2ap")

        emit_ln2(0)
        for ft in range(NFT):
            tb0 = ft * TB
            if cfg.get("stop_after_LN2"):
                return _stop(b_xln2)
            for jg in range(DH // 256 if not cfg.get("s1_one_jg") else 1):
                wi = jg % 2
                p.dma("gpsimd", lambda g, jg=jg, wi=wi: g.dma_start(
                    fsl[wi].rearrange("p (n o) -> p n o", o=256),
                    w1t_d.rearrange("(n p) o -> p n o", p=128)[:, :, jg * 256:(jg + 1) * 256]),
                    sem_w[wi], writes=[b_fsl[wi]], tag="w1l")
                p.dma("gpsimd", lambda g, jg=jg, wi=wi: g.dma_start(
                    fsl[2 + wi].rearrange("p (n o) -> p n o", o=256),
                    w2t_d.rearrange("(n p) o -> p n o", p=128)[:, :, jg * 256:(jg + 1) * 256]),
                    sem_w[2 + wi], writes=[b_fsl[2 + wi]], tag="w2l")
                for jj in range(2):
                    j = jg * 2 + jj
                    for sb in range(NSB):
                        psa, bpsa = slot()
                        psb2, bpsb2 = slot()
                        for i in range(ND):
                            p.op("tensor", lambda t, i=i, jj=jj, sb=sb, psa=psa, wi=wi: t.matmul(
                                psa, fsl[wi][:, i * 256 + jj * 128:i * 256 + (jj + 1) * 128],
                                xln2[i][:, sb * 512:(sb + 1) * 512],
                                start=(i == 0), stop=(i == ND - 1)),
                                reads=[b_fsl[wi], b_xln2[i]], writes=[bpsa],
                                war_free=(i > 0), tag="w1MM")
                        for i in range(ND):
                            p.op("tensor", lambda t, i=i, jj=jj, sb=sb, psb2=psb2, wi=wi: t.matmul(
                                psb2, fsl[2 + wi][:, i * 256 + jj * 128:i * 256 + (jj + 1) * 128],
                                xln2[i][:, sb * 512:(sb + 1) * 512],
                                start=(i == 0), stop=(i == ND - 1)),
                                reads=[b_fsl[2 + wi], b_xln2[i]], writes=[bpsb2],
                                war_free=(i > 0), tag="w2MM")
                        _af1 = AF.Identity if cfg.get("no_silu") else (AF.Gelu if cfg.get("silu_as_gelu") else (AF.Sigmoid if cfg.get("silu_as_sigmoid") else AF.Silu))
                        if cfg.get("one_silu") and (jg, jj, sb) != (0, 0, 0):
                            _af1 = AF.Identity
                        _sc1 = 0.5 if cfg.get("silu_const_scale") else cc("w1s", j)
                        if cfg.get("silu_from_sbuf"):
                            p.op("scalar", lambda s, psa=psa: s.activation(
                                sc16[2][:, :], psa, AF.Identity, bias=0.0, scale=1.0),
                                reads=[bpsa], writes=[b_sc16[2], b_kb1], tag="pre")
                            p.op("scalar", lambda s, j=j, _af1=_af1, _sc1=_sc1: s.activation(
                                sc16[0][:, :], sc16[2][:, :], _af1, bias=0.0, scale=_sc1),
                                reads=[b_sc16[2], bccol], writes=[b_sc16[0], b_kb1], tag="silu")
                        else:
                            p.op("scalar", lambda s, j=j, psa=psa, _af1=_af1, _sc1=_sc1: s.activation(
                                sc16[0][:, :], psa, _af1, bias=0.0, scale=_sc1),
                                reads=[bpsa, bccol], writes=[b_sc16[0], b_kb1], tag="silu")
                        _af2 = AF.Silu if cfg.get("w2_as_silu") else AF.Identity
                        p.op("scalar", lambda s, j=j, psb2=psb2, _af2=_af2: s.activation(
                            sc16[1][:, :], psb2, _af2, bias=0.0, scale=cc("w2s", j)),
                            reads=[bpsb2, bccol], writes=[b_sc16[1], b_kb1], tag="w2sc")
                        p.op("vector", lambda v, j=j, sb=sb: v.tensor_mul(
                            h0[j][:, sb * 512:(sb + 1) * 512], sc16[0][:, :], sc16[1][:, :]),
                            reads=[b_sc16[0], b_sc16[1]], writes=[b_h0[j]], tag="h0mul")
            if cfg.get("stop_after_S1"):
                return _stop(b_h0)
            if ft + 1 < NFT:
                emit_ln2(ft + 1)
            for (wt_d, scn, src, bsrc, dst, bdst) in (
                    (h1t_d, "h1s", h0, b_h0, h1v, b_h1),
                    (h2t_d, "h2s", h1v, b_h1, h0, b_h0)):
                for j2 in range(NJ):
                    wi = j2 % 2
                    for qd in range(4):
                        jq0 = qd * NJ // 4
                        p.dma("gpsimd", lambda g, j2=j2, wi=wi, wt_d=wt_d, jq0=jq0: g.dma_start(
                            PB.view(o_ws + wi * 4096 + jq0 * 128, NJ * 32).rearrange(
                                "p (n o) -> p n o", o=128),
                            wt_d.rearrange("(n p) o -> p n o", p=128)[
                                :, jq0:jq0 + NJ // 4, j2 * 128:(j2 + 1) * 128]),
                            sem_w[wi], writes=[b_fsl[2 * wi], b_fsl[2 * wi + 1]],
                            war_free=(qd > 0), tag="hwl")
                    for sb in range(NSB):
                        ps, bps = slot()
                        for j in range(NJ):
                            p.op("tensor", lambda t, j=j, sb=sb, ps=ps, wi=wi, src=src: t.matmul(
                                ps, PB.view(o_ws + wi * 4096 + j * 128, 128),
                                src[j][:, sb * 512:(sb + 1) * 512],
                                start=(j == 0), stop=(j == NJ - 1)),
                                reads=[b_fsl[2 * wi], b_fsl[2 * wi + 1], bsrc[j]],
                                writes=[bps], war_free=(j > 0), tag="hMM")
                        p.op("scalar", lambda s, j2=j2, sb=sb, ps=ps, scn=scn, dst=dst,
                             bdst=bdst: s.activation(
                            dst[j2][:, sb * 512:(sb + 1) * 512], ps, AF.Gelu,
                            bias=0.0, scale=cc(scn, j2)),
                            reads=[bps, bccol], writes=[bdst[j2]], tag="gelu")
            if cfg.get("stop_after_S3"):
                return _stop(b_h0 + b_h1)
            for sb in range(NSB):
                t0 = tb0 + sb * 512
                for o in range(ND):
                    wi = o % 2
                    for qd in range(4):
                        jq0 = qd * NJ // 4
                        p.dma("gpsimd", lambda g, o=o, wi=wi, jq0=jq0: g.dma_start(
                            PB.view(o_ws + wi * 4096 + jq0 * 128, NJ * 32).rearrange(
                                "p (n o2) -> p n o2", o2=128),
                            w3t_d.rearrange("(n p) o2 -> p n o2", p=128)[
                                :, jq0:jq0 + NJ // 4, o * 128:(o + 1) * 128]),
                            sem_w[wi], writes=[b_fsl[2 * wi], b_fsl[2 * wi + 1]],
                            war_free=(qd > 0), tag="w3l")
                    ps, bps = slot()
                    for j in range(NJ):
                        p.op("tensor", lambda t, j=j, sb=sb, ps=ps, wi=wi: t.matmul(
                            ps, PB.view(o_ws + wi * 4096 + j * 128, 128),
                            h0[j][:, sb * 512:(sb + 1) * 512],
                            start=(j == 0), stop=(j == NJ - 1)),
                            reads=[b_fsl[2 * wi], b_fsl[2 * wi + 1], b_h0[j]],
                            writes=[bps], war_free=(j > 0), tag="w3MM")
                    p.op("scalar", lambda s, o=o, ps=ps: s.activation(
                        big1[o], ps, AF.Identity, bias=0.0, scale=cc("w3s", o)),
                        reads=[bps, bccol], writes=[b_big1], tag="w3evac")
                p.dma("sync", lambda s, t0=t0: s.dma_start(
                    PF.view(o_big2, ND * 512).rearrange("p (n t) -> p n t", t=512),
                    fm(x1d, t0, t0 + 512)), sem_big, reads=[b_x1d],
                    writes=[b_big2], tag="x1ld2")
                emit_stats(cast_f32([big1[o] for o in range(ND)], [b_big1]))
                for o in range(ND):
                    a, ab = asc[o % 2], b_asc[o % 2]
                    p.op("vector", lambda v, o=o, a=a: v.tensor_sub(a, big1[o], mean_b),
                         reads=[b_big1, b_stats], writes=[ab])
                    p.op("vector", lambda v, a=a: v.tensor_mul(a, a, rstd_b),
                         reads=[ab, b_stats], writes=[ab])
                    p.op("vector", lambda v, o=o, a=a: v.tensor_add(big2[o], big2[o], a),
                         reads=[ab, b_big2], writes=[b_big2], tag="x2add")
                p.dma("sync", lambda s, t0=t0: s.dma_start(
                    fm(outT, t0, t0 + 512),
                    PF.view(o_big2, ND * 512).rearrange("p (n t) -> p n t", t=512)),
                    sem_st, reads=[b_big2], tag="outst")

        p.barrier("sync", [Tok("dma", sem=sem_st, idx=sem_st.count)])
        p.finalize(block)
    nc._layout = dict(o_xln=o_xln, o_qs=o_qs, o_ao=o_ao, o_k=o_k, o_v=o_v,
                      o_qg=o_qg, o_sc=o_sc, o_ws=o_ws,
                      o_h0=o_h0, o_h1=o_h1, o_xln2=o_xln2,
                      o_st=o_st, o_big1=o_big1, o_big2=o_big2)
    return nc


_CACHE = {}


def _get_nc(cfg_key, cfg):
    if cfg_key not in _CACHE:
        _CACHE[cfg_key] = build_program(cfg)
    return _CACHE[cfg_key]


def make_host_inputs(cfg, x, Wq, bq, Wk, bk, Wv, bv, Wo, bo, ln_g, ln_b,
                     w1_w, w1_s, w2_w, w2_s, h1_w, h1_s, h2_w, h2_s, w3_w, w3_s):
    D, To, Tf = cfg["D"], cfg["To"], cfg["Tf"]
    B = x.shape[0]
    f = np.float32
    ac = np.ascontiguousarray

    def col8(vec, n):
        return vec.reshape(n, 128).T.astype(f)

    ccol = np.zeros((128, NCC), f)
    ccol[:, CC["bq"]:CC["bq"] + 8] = col8(bq, 8)
    ccol[:, CC["bk"]:CC["bk"] + 8] = col8(bk, 8)
    ccol[:, CC["bo"]:CC["bo"] + 8] = col8(bo, 8)
    ccol[:, CC["lng"]:CC["lng"] + 8] = col8(ln_g, 8)
    ccol[:, CC["lnb"]:CC["lnb"] + 8] = col8(ln_b, 8)
    ccol[:, CC["w1s"]:CC["w1s"] + 32] = col8(w1_s, 32)
    ccol[:, CC["w2s"]:CC["w2s"] + 32] = col8(w2_s, 32)
    ccol[:, CC["h1s"]:CC["h1s"] + 32] = col8(h1_s, 32)
    ccol[:, CC["h2s"]:CC["h2s"] + 32] = col8(h2_s, 32)
    ccol[:, CC["w3s"]:CC["w3s"] + 8] = col8(w3_s, 8)
    ccol[:, CC["eps"]] = 1e-5
    crow = np.concatenate([bv.astype(f), np.full(128, -1.0, f)])[None, :]
    shared = dict(
        wq=ac(Wq.astype(f)), wk=ac(Wk.astype(f)), wv=ac(Wv.astype(f)),
        wo=ac(Wo.astype(f)),
        w1t=ac(w1_w.T.astype(f)), w2t=ac(w2_w.T.astype(f)),
        h1t=ac(h1_w.T.astype(f)), h2t=ac(h2_w.T.astype(f)),
        w3t=ac(w3_w.T.astype(f)),
        ccol=ccol, crow=ac(crow.astype(f)),
        cones=np.ones((128, 128), f),
    )
    in_maps = []
    for b in range(B):
        xt = ac(x[b].T.astype(f))
        for hh in range(2):
            if hh == 0:
                xcore = xt
            else:
                xcore = ac(np.concatenate([xt[:, To:], xt[:, :To]], axis=1))
            in_maps.append(dict(shared, xT=xcore))
    return in_maps


def kernel(L0_attn_x, L0_attn_Wq, L0_attn_bq, L0_attn_Wk, L0_attn_bk,
           L0_attn_Wv, L0_attn_bv, L0_attn_Wo, L0_attn_bo,
           L0_attn_ln_g, L0_attn_ln_b,
           L0_ffn_w1_w, L0_ffn_w1_s, L0_ffn_w2_w, L0_ffn_w2_s,
           L0_ffn_h1_w, L0_ffn_h1_s, L0_ffn_h2_w, L0_ffn_h2_s,
           L0_ffn_w3_w, L0_ffn_w3_s):
    cfg = default_cfg()
    x = np.asarray(L0_attn_x)
    B, S, D = x.shape
    To = cfg["To"]
    nc = _get_nc("full", cfg)
    in_maps = make_host_inputs(
        cfg, x, L0_attn_Wq, L0_attn_bq, L0_attn_Wk, L0_attn_bk,
        L0_attn_Wv, L0_attn_bv, L0_attn_Wo, L0_attn_bo,
        L0_attn_ln_g, L0_attn_ln_b,
        L0_ffn_w1_w, L0_ffn_w1_s, L0_ffn_w2_w, L0_ffn_w2_s,
        L0_ffn_h1_w, L0_ffn_h1_s, L0_ffn_h2_w, L0_ffn_h2_s,
        L0_ffn_w3_w, L0_ffn_w3_s)
    res = run_bass_kernel_spmd(nc, in_maps, core_ids=list(range(8)))
    out = np.empty((B, S, D), np.float32)
    for b in range(B):
        out[b, :To, :] = res.results[2 * b]["outT"].T
        out[b, To:, :] = res.results[2 * b + 1]["outT"].T
    return out


# version 13
# speedup vs baseline: 1.3483x; 1.3483x over previous
import sys
sys.path.insert(0, '/opt/trn_rl_repo')
from contextlib import ExitStack
import numpy as np
import concourse.bass as bass
import concourse.mybir as mybir
from concourse.bass_utils import run_bass_kernel_spmd

F32 = mybir.dt.float32
BF16 = mybir.dt.bfloat16
AF = mybir.ActivationFunctionType
ALU = mybir.AluOpType
AX = mybir.AxisListType

ENGINES = ("tensor", "vector", "scalar", "gpsimd", "sync")


class Buf:
    __slots__ = ("name", "last_writer", "readers", "psum")

    def __init__(self, name, psum=False):
        self.name = name
        self.last_writer = None
        self.readers = []
        self.psum = psum


class DmaSem:
    __slots__ = ("handle", "count", "name")

    def __init__(self, handle, name):
        self.handle = handle
        self.count = 0
        self.name = name


class Op:
    __slots__ = ("engine", "fn", "seq", "inc", "sem_val", "deps", "dma_sem", "tag")

    def __init__(self, engine, fn, seq, dma_sem, tag):
        self.engine = engine
        self.fn = fn
        self.seq = seq
        self.inc = dma_sem is not None
        self.sem_val = None
        self.deps = []
        self.dma_sem = dma_sem
        self.tag = tag


class Tok:
    __slots__ = ("kind", "engine", "idx", "sem")

    def __init__(self, kind, engine=None, idx=None, sem=None):
        self.kind = kind
        self.engine = engine
        self.idx = idx
        self.sem = sem


class Prog:
    def __init__(self, nc, stack):
        self.nc = nc
        self.stack = stack
        self.ops = {e: [] for e in ENGINES}
        self.sems = {e: stack.enter_context(nc.semaphore(f"pg_{e}")) for e in ENGINES}
        self.n_bufs = 0
        self.n_dsems = 0

    def buf(self, name=None, psum=False):
        self.n_bufs += 1
        return Buf(name or f"b{self.n_bufs}", psum)

    def dma_sem(self, name=None):
        self.n_dsems += 1
        name = name or f"ds{self.n_dsems}"
        return DmaSem(self.stack.enter_context(self.nc.semaphore(name)), name)

    def _collect_deps(self, o, tok, reads, writes, war_free):
        deps = []

        def add(d):
            if d is tok:
                return
            if d.kind == "eng":
                deps.append(("eng", d.engine, d.idx))
            else:
                deps.append(("dma", d.sem, 16 * d.sem.count))

        for b in reads:
            if b.last_writer is not None:
                add(b.last_writer)
            if b.psum:
                for r in b.readers:
                    if r.kind == "eng" and r.engine != tok.engine:
                        add(r)
            b.readers.append(tok)
        for b in writes:
            if not war_free:
                if b.last_writer is not None:
                    add(b.last_writer)
                for r in b.readers:
                    add(r)
            b.readers = []
            b.last_writer = tok
        o.deps = deps

    def op(self, engine, fn, reads=(), writes=(), war_free=False, tag=""):
        seq = len(self.ops[engine])
        tok = Tok("eng", engine=engine, idx=seq)
        o = Op(engine, fn, seq, None, tag)
        self._collect_deps(o, tok, reads, writes, war_free)
        self.ops[engine].append(o)
        return tok

    def dma(self, qengine, fn, sem, reads=(), writes=(), war_free=False, tag=""):
        seq = len(self.ops[qengine])
        tok = Tok("dma", engine=qengine, idx=None, sem=sem)
        o = Op(qengine, fn, seq, sem, tag)
        self._collect_deps(o, tok, reads, writes, war_free)
        sem.count += 1
        tok.idx = sem.count
        self.ops[qengine].append(o)
        return tok

    def barrier(self, engine, toks, tag="barrier"):
        seq = len(self.ops[engine])
        o = Op(engine, lambda eng: eng.nop(), seq, None, tag)
        deps = []
        for d in toks:
            if d.kind == "eng":
                deps.append(("eng", d.engine, d.idx))
            else:
                deps.append(("dma", d.sem, 16 * d.sem.count))
        o.deps = deps
        self.ops[engine].append(o)
        return Tok("eng", engine=engine, idx=seq)

    def _resolve(self):
        for e in ENGINES:
            for o in self.ops[e]:
                kept = []
                for d in o.deps:
                    if d[0] == "eng":
                        if d[1] == "tensor" and e == "tensor":
                            continue
                        self.ops[d[1]][d[2]].inc = True
                    kept.append(d)
                o.deps = kept
        for e in ENGINES:
            v = 0
            for o in self.ops[e]:
                if o.dma_sem is not None:
                    continue
                if o.inc:
                    v += 1
                    o.sem_val = v

    def finalize(self, block):
        self._resolve()
        prog = self

        def section(e):
            def body(eng):
                watermark = {}
                for o in prog.ops[e]:
                    for d in o.deps:
                        if d[0] == "eng":
                            sem = prog.sems[d[1]]
                            val = prog.ops[d[1]][d[2]].sem_val
                            assert val is not None, (d, o.tag)
                        else:
                            sem = d[1].handle
                            val = d[2]
                        key = id(sem)
                        if watermark.get(key, -1) >= val:
                            continue
                        watermark[key] = val
                        eng.wait_ge(sem, val)
                    inst = o.fn(eng)
                    if o.dma_sem is not None:
                        inst.then_inc(o.dma_sem.handle, 16)
                    elif o.inc:
                        inst.then_inc(prog.sems[e], 1)
            return body

        for e in ENGINES:
            if not self.ops[e]:
                continue
            getattr(block, e)(section(e))


class Pool:

    def __init__(self, nc, stack, name, dtype, elems):
        self.t = stack.enter_context(nc.sbuf_tensor(name, [128, elems], dtype))
        self.elems = elems
        self.off = 0

    def alloc(self, n, align=64):
        o = -(-self.off // align) * align
        assert o + n <= self.elems, f"pool overflow: {o}+{n} > {self.elems}"
        self.off = o + n
        return o

    def view(self, off, n, p0=0, p1=128):
        return self.t[p0:p1, off:off + n]


CC = {}
_o = 0
for _nm, _n in [("bq", 8), ("bk", 8), ("bo", 8), ("lng", 8), ("lnb", 8),
                ("w1s", 32), ("w2s", 32), ("h1s", 32), ("h2s", 32), ("w3s", 8),
                ("eps", 1)]:
    CC[_nm] = _o
    _o += _n
NCC = _o


def default_cfg():
    return dict(D=1024, H=16, DH=4096, To=2048, Tf=4096)


def build_program(cfg):
    D, H, DH, To, Tf = cfg["D"], cfg["H"], cfg["DH"], cfg["To"], cfg["Tf"]
    ND = D // 128
    NP = H // 2
    NJ = DH // 128
    NKC = Tf // 128
    NSEG = Tf // 512
    NQT = To // 512
    NTC_F = Tf // 512
    NTC_O = To // 512
    TB = min(1024, To)
    NFT = To // TB
    NSB = TB // 512
    assert NP == ND and D == 1024

    nc = bass.Bass("TRN2", target_bir_lowering=False, debug=False,
                   dynamic_dma_scratch_size=16384)
    di = lambda n, s: nc.dram_tensor(n, s, F32, kind="ExternalInput").ap()
    xT = di("xT", [D, Tf])
    wq_d, wk_d, wv_d, wo_d = di("wq", [D, D]), di("wk", [D, D]), di("wv", [D, D]), di("wo", [D, D])
    w1t_d, w2t_d = di("w1t", [D, DH]), di("w2t", [D, DH])
    h1t_d, h2t_d = di("h1t", [DH, DH]), di("h2t", [DH, DH])
    w3t_d = di("w3t", [DH, D])
    ccol_d = di("ccol", [128, NCC])
    crow_d = di("crow", [1, D + 128])
    cones_d = di("cones", [128, 128])
    x1d = nc.dram_tensor("x1d", [D, To], F32, kind="Internal").ap()
    outT = nc.dram_tensor("outT", [D, To], F32, kind="ExternalOutput").ap()

    def fm(dram_ap, t0, t1, n_last=None):
        r = dram_ap.rearrange("(n p) t -> p n t", p=128)
        return r[:, :, t0:t1]

    stack = ExitStack()
    with stack:
        E = stack.enter_context
        p = Prog(nc, stack)

        _o_sc = 0
        _o_ws = _o_sc + 4 * 512
        _main = _o_ws + 2 * ND * 512
        _a_end = _main + ND * Tf + 2 * ND * To + Tf + NKC * 130
        _b_end = _main + 2 * NJ * TB + ND * TB
        pb_elems = max(_a_end, _b_end) + 64
        pf_elems = (3 * 512 + 2 * ND * 512)
        PB = Pool(nc, stack, "PB", BF16, pb_elems)
        PF = Pool(nc, stack, "PF", F32, pf_elems)

        ccol_t = E(nc.sbuf_tensor("ccol_t", [128, NCC], F32))
        onesf_t = E(nc.sbuf_tensor("onesf_t", [128, 128], F32))
        ones_b = E(nc.sbuf_tensor("ones_b", [128, 128], BF16))
        neg_b = E(nc.sbuf_tensor("neg_b", [128, 128], BF16))
        bks_t = E(nc.sbuf_tensor("bks_t", [128, 8], F32))

        psP1a = E(nc.psum_tensor("psP1a", [128, 512], F32))
        psP1b = E(nc.psum_tensor("psP1b", [128, 512], F32))
        psST0 = E(nc.psum_tensor("psST0", [128, 1024], F32))
        psST1 = E(nc.psum_tensor("psST1", [128, 1024], F32))
        psPVa = E(nc.psum_tensor("psPVa", [65, 512], F32))
        psPVb = E(nc.psum_tensor("psPVb", [65, 512], F32))
        block = E(nc.Block())

        bP1a, bP1b = p.buf("psP1a", True), p.buf("psP1b", True)
        bST = [p.buf(f"psST{i}", True) for i in range(4)]
        bPVa, bPVb = p.buf("psPVa", True), p.buf("psPVb", True)
        _slots = [(psP1a, 0, bP1a), (psP1b, 0, bP1b),
                  (psST0, 0, bST[0]), (psST0, 512, bST[1]),
                  (psST1, 0, bST[2]), (psST1, 512, bST[3])]
        _si = [0]

        _slot_pool = [list(range(6))]

        def slot():
            pool = _slot_pool[0]
            h, lo, b = _slots[pool[_si[0] % len(pool)]]
            _si[0] += 1
            return h[:, lo:lo + 512], b

        bccol, bonesf, bonesb, bnegb, bbks = (p.buf(x) for x in
            ["ccol", "onesf", "onesb", "negb", "bks"])

        sem_c = p.dma_sem("c")
        sem_cg = p.dma_sem("cg")
        sem_x = [p.dma_sem("x0"), p.dma_sem("x1")]
        sem_w = [p.dma_sem(f"w{i}") for i in range(4)]
        sem_bv = p.dma_sem("bv")
        sem_qg = [p.dma_sem("qg0"), p.dma_sem("qg1")]
        sem_gg = [p.dma_sem("gg0"), p.dma_sem("gg1")]
        sem_big = p.dma_sem("big")
        sem_st = p.dma_sem("st")

        p.dma("sync", lambda s: s.dma_start(ccol_t[:, :], ccol_d), sem_c, writes=[bccol])
        p.dma("sync", lambda s: s.dma_start(onesf_t[:, :], cones_d), sem_c, writes=[bonesf])
        p.dma("gpsimd", lambda g: g.dma_start(ones_b[:, :], cones_d), sem_cg, writes=[bonesb])
        p.dma("gpsimd", lambda g: g.dma_start(neg_b[0:1, :], crow_d[0:1, D:D + 128]),
              sem_cg, writes=[bnegb])
        p.dma("gpsimd", lambda g: g.dma_start(neg_b[64:65, :], crow_d[0:1, D:D + 128]),
              sem_cg, writes=[bnegb], war_free=True)
        p.op("vector", lambda v: v.tensor_scalar_mul(
            bks_t[:, :], ccol_t[:, CC["bk"]:CC["bk"] + 8], 0.125),
            reads=[bccol], writes=[bbks])

        def cc(nm, i=0):
            return ccol_t[:, CC[nm] + i:CC[nm] + i + 1]

        eps_ap = ccol_t[:, CC["eps"]:CC["eps"] + 1]

        o_sc = _o_sc
        o_ws = _o_ws
        o_xln = _main
        o_qs = o_xln + ND * Tf
        o_ao = o_qs + ND * To
        o_k = o_ao + ND * To
        o_v = o_k + Tf
        o_pt_m = _o_ws + 5120
        o_scpy = _o_ws + 2048
        o_pt0 = _o_ws + 3072
        o_grow = _o_ws + 4096
        assert Tf <= 4096
        o_h0 = _main
        o_h1 = o_h0 + NJ * TB
        o_xln2 = o_h1 + NJ * TB

        xln = [PB.view(o_xln + i * Tf, Tf) for i in range(ND)]
        q_s = [PB.view(o_qs + i * To, To) for i in range(ND)]
        ao = [PB.view(o_ao + i * To, To) for i in range(ND)]
        K_hp = PB.view(o_k, Tf)
        V_hp = [PB.view(o_v + k * 130, 130) for k in range(NKC)]
        sc16 = [PB.view(o_sc + i * 512, 512) for i in range(4)]
        b_xln = [p.buf(f"xln{i}") for i in range(ND)]
        b_qs = [p.buf(f"qs{i}") for i in range(ND)]
        b_ao = [p.buf(f"ao{i}") for i in range(ND)]
        b_k = p.buf("K_hp")
        b_v = [p.buf(f"V{k}") for k in range(NKC)]
        b_pt = [p.buf("pt0"), p.buf("pt1"), p.buf("pt2")]

        def PTof(par):
            return o_pt0 if par == 0 else o_pt_m + (par - 1) * 1024
        scpy = [PB.view(o_scpy + i * 512, 512) for i in range(2)]
        b_scpy = [p.buf("scpy0"), p.buf("scpy1")]
        b_grow = [p.buf("grow0"), p.buf("grow1")]
        b_kb1 = p.buf("kb1")
        b_sc16 = [p.buf(f"sc16_{i}") for i in range(4)]
        b_ws = p.buf("wslab")
        b_wsV = p.buf("wslabV")
        b_xbs = [p.buf("xb0"), p.buf("xb1")]

        o_st = PF.alloc(3 * 512)
        mean_b = PF.view(o_st, 512)
        rstd_b = PF.view(o_st + 512, 512)
        s1f = PF.view(o_st + 1024, 512)
        b_stats = p.buf("stats")
        asc = [s1f, s1f]
        b_asc = [b_stats, b_stats]

        o_big1 = PF.alloc(ND * 512)
        o_rd = o_big1
        o_rb = o_big1 + 1024
        o_bv = o_big1 + 1536
        o_brow = o_big1 + 1664
        o_cm = o_big1 + 1792
        o_gc = o_big1 + 1920
        o_big2 = PF.alloc(ND * 512)
        big1 = [PF.view(o_big1 + i * 512, 512) for i in range(ND)]
        big2 = [PF.view(o_big2 + i * 512, 512) for i in range(ND)]
        b_stats = p.buf("stats")
        b_cm = [p.buf("cm0"), p.buf("cm1")]
        b_gc = [p.buf("gc0"), p.buf("gc1")]
        b_rd, b_rb, b_bv, b_brow = p.buf("rd"), p.buf("rb"), p.buf("bv"), p.buf("brow")
        b_big1, b_big2 = p.buf("big1"), p.buf("big2")
        b_x1d = p.buf("x1d_dram")

        def emit_stats(bf_chunks):
            n = len(bf_chunks)
            for i, (kind, xap, xbufs) in enumerate(bf_chunks):
                if kind == "f32":
                    dst, db = sc16[i % 2], b_sc16[i % 2]
                    p.op("vector", lambda v, xap=xap, dst=dst: v.tensor_copy(
                        dst[:, :], xap), reads=list(xbufs), writes=[db, b_kb1], tag="cast")
                    xap, xbufs = dst[:, :], [db]
                sq, bsq = sc16[2 + (i % 2)], b_sc16[2 + (i % 2)]
                p.op("vector", lambda v, xap=xap, sq=sq: v.tensor_mul(sq[:, :], xap, xap),
                     reads=list(xbufs), writes=[bsq, b_kb1], tag="sq")
                p.op("tensor", lambda t, xap=xap, i=i, n=n: t.matmul(
                    psP1a[:, :], ones_b[:, :], xap, start=(i == 0), stop=(i == n - 1)),
                    reads=[bonesb] + list(xbufs), writes=[bP1a], war_free=(i > 0), tag="sMM")
                p.op("tensor", lambda t, sq=sq, i=i, n=n: t.matmul(
                    psP1b[:, :], ones_b[:, :], sq[:, :], start=(i == 0), stop=(i == n - 1)),
                    reads=[bonesb, bsq], writes=[bP1b], war_free=(i > 0), tag="qMM")
            inv = 1.0 / (n * 128)
            p.op("vector", lambda v: v.tensor_scalar_mul(mean_b, psP1a[:, :], inv),
                 reads=[bP1a], writes=[b_stats])
            p.op("vector", lambda v: v.tensor_scalar_mul(s1f, psP1b[:, :], inv),
                 reads=[bP1b], writes=[b_stats])
            p.op("vector", lambda v: v.scalar_tensor_tensor(
                rstd_b, mean_b, 1.0, mean_b, op0=ALU.mult, op1=ALU.mult),
                reads=[b_stats], writes=[b_stats])
            p.op("vector", lambda v: v.tensor_sub(s1f, s1f, rstd_b),
                 reads=[b_stats], writes=[b_stats])
            p.op("scalar", lambda s: s.activation(s1f, s1f, AF.Sqrt, bias=eps_ap, scale=1.0),
                 reads=[b_stats, bccol], writes=[b_stats])
            p.op("vector", lambda v: v.reciprocal(rstd_b, s1f),
                 reads=[b_stats], writes=[b_stats])

        def cast_f32(srcs, bufs_read):
            return [("f32", ap, list(bufs_read)) for ap in srcs]

        xb = [[PB.view(o_ws + sl * ND * 512 + i * 512, 512) for i in range(ND)]
              for sl in range(2)]
        for tc in range(NTC_F if not cfg.get("skip_a1") else 0):
            sl = tc % 2
            dst = PB.view(o_ws + sl * ND * 512, ND * 512).rearrange(
                "p (n t) -> p n t", t=512)
            p.dma("gpsimd", lambda g, tc=tc, dst=dst: g.dma_start(
                dst, fm(xT, tc * 512, (tc + 1) * 512)),
                sem_x[sl], writes=[b_xbs[sl]], tag="xload")
            emit_stats([("bf", xb[sl][i][:, :], [b_xbs[sl]]) for i in range(ND)])
            for i in range(ND):
                a, ab = asc[i % 2], b_asc[i % 2]
                p.op("vector", lambda v, i=i, sl=sl, a=a: v.tensor_sub(
                    a, xb[sl][i][:, :], mean_b), reads=[b_xbs[sl], b_stats], writes=[ab])
                p.op("vector", lambda v, i=i, tc=tc, a=a: v.tensor_mul(
                    xln[i][:, tc * 512:(tc + 1) * 512], a, rstd_b),
                    reads=[ab, b_stats], writes=[b_xln[i]], tag="ln1ap")

        wsl_v = PB.view(o_ws, ND * 1024).rearrange("p (n o) -> p n o", o=1024)
        wq_sl = [PB.view(o_ws + i * 1024, 1024) for i in range(ND)]
        p.dma("gpsimd", lambda g: g.dma_start(
            wsl_v, wq_d.rearrange("(n p) o -> p n o", p=128)),
            sem_w[0], writes=[b_ws, b_wsV, b_xbs[0], b_xbs[1]] + b_pt + b_grow, tag="wq")
        for tt in range(NQT if not cfg.get("skip_a2") else 0):
            t0 = tt * 512
            for o in range(ND):
                ps, bps = slot()
                for i in range(ND):
                    p.op("tensor", lambda t, i=i, o=o, t0=t0, ps=ps: t.matmul(
                        ps, wq_sl[i][:, o * 128:(o + 1) * 128],
                        xln[i][:, t0:t0 + 512], start=(i == 0), stop=(i == ND - 1)),
                        reads=[b_ws, b_xln[i]], writes=[bps], war_free=(i > 0), tag="qpMM")
                p.op("scalar", lambda s, o=o, ps=ps: s.activation(
                    big1[o], ps, AF.Identity, bias=cc("bq", o), scale=1.0),
                    reads=[bps, bccol], writes=[b_big1], tag="qevac")
            emit_stats(cast_f32([big1[o] for o in range(ND)], [b_big1]))
            for o in range(ND):
                a, ab = asc[o % 2], b_asc[o % 2]
                p.op("vector", lambda v, o=o, a=a: v.tensor_sub(a, big1[o], mean_b),
                     reads=[b_big1, b_stats], writes=[ab])
                p.op("vector", lambda v, a=a: v.tensor_mul(a, a, rstd_b),
                     reads=[ab, b_stats], writes=[ab])
                p.op("scalar", lambda s, o=o, t0=t0, a=a: s.activation(
                    q_s[o][:, t0:t0 + 512], a, AF.Identity,
                    bias=cc("lnb", o), scale=cc("lng", o)),
                    reads=[ab, bccol], writes=[b_qs[o]], tag="qln")

        if cfg.get("stop_after_A2"):
            p.barrier("sync", [Tok("dma", sem=sem_c, idx=sem_c.count)])
            p.finalize(block)
            nc._layout = dict(o_xln=o_xln, o_qs=o_qs, o_ao=o_ao, o_k=o_k, o_v=o_v,
                              o_sc=o_sc, o_ws=o_ws,
                              o_st=o_st, o_big1=o_big1, o_big2=o_big2)
            return nc

        vvíew = PB.t[:, o_v:o_v + NKC * 130].rearrange("p (k c) -> p k c", c=130)
        p.op("vector", lambda v: v.memset(vvíew[:, :, 64:65], 1.0),
             writes=list(b_v), tag="vones")
        p.op("vector", lambda v: v.memset(vvíew[:, :, 129:130], 1.0),
             writes=list(b_v), war_free=True, tag="vones")

        p.op("vector", lambda v: v.memset(PF.view(o_gc, 256), 0.0),
             writes=[b_gc[0], b_gc[1], b_big1], tag="gcinit")
        kvK = [PB.view(o_ws + i * 128, 128) for i in range(ND)]
        kvV = [PB.view(o_ws + 1024 + i * 128, 128) for i in range(ND)]
        bvrow = PF.view(o_brow, 128, 0, 1)

        def emit_kv(hp):
            c0 = hp * 128
            p.dma("gpsimd", lambda g, c0=c0: g.dma_start(
                PB.view(o_ws, ND * 128).rearrange("p (n o) -> p n o", o=128),
                wk_d.rearrange("(n p) o -> p n o", p=128)[:, :, c0:c0 + 128]),
                sem_w[0], writes=[b_ws], tag="wk")
            p.dma("gpsimd", lambda g, c0=c0: g.dma_start(
                PB.view(o_ws + 1024, ND * 128).rearrange("p (n o) -> p n o", o=128),
                wv_d.rearrange("(n p) o -> p n o", p=128)[:, :, c0:c0 + 128]),
                sem_w[1], writes=[b_wsV], tag="wv")
            p.dma("sync", lambda s, c0=c0: s.dma_start(
                bvrow, crow_d[0:1, c0:c0 + 128]), sem_bv, writes=[b_brow, b_big1], tag="bvrow")
            p.op("tensor", lambda t: t.matmul(
                psP1a[:, 0:128], onesf_t[0:1, :], bvrow,
                start=True, stop=True), reads=[bonesf, b_brow], writes=[bP1a], tag="bvMM")
            p.op("vector", lambda v: v.tensor_copy(PF.view(o_bv, 128), psP1a[:, 0:128]),
                 reads=[bP1a], writes=[b_bv, b_big1])
            for tc in range(NTC_F):
                ps, bps = slot()
                for i in range(ND):
                    p.op("tensor", lambda t, i=i, tc=tc, ps=ps: t.matmul(
                        ps, kvK[i][:, :], xln[i][:, tc * 512:(tc + 1) * 512],
                        start=(i == 0), stop=(i == ND - 1)),
                        reads=[b_ws, b_xln[i]], writes=[bps], war_free=(i > 0),
                        tag="kMM")
                p.op("scalar", lambda s, hp=hp, tc=tc, ps=ps: s.activation(
                    K_hp[:, tc * 512:(tc + 1) * 512], ps, AF.Identity,
                    bias=bks_t[:, hp:hp + 1], scale=0.125),
                    reads=[bps, bbks], writes=[b_k], tag="kevac")
            for kc in range(NKC):
                psh, lo, bps = _slots[_si[0] % len(_slots)]
                _si[0] += 1
                for i in range(ND):
                    p.op("tensor", lambda t, i=i, kc=kc, psh=psh, lo=lo: t.matmul(
                        psh[:, lo:lo + 128], xln[i][:, kc * 128:(kc + 1) * 128],
                        kvV[i][:, :], start=(i == 0), stop=(i == ND - 1)),
                        reads=[b_wsV, b_xln[i]], writes=[bps], war_free=(i > 0), tag="vMM")
                for h in range(2):
                    p.op("vector", lambda v, kc=kc, h=h, psh=psh, lo=lo: v.scalar_tensor_tensor(
                        V_hp[kc][:, h * 65:h * 65 + 64], psh[:, lo + h * 64:lo + h * 64 + 64],
                        1.0, PF.view(o_bv, 128)[:, h * 64:(h + 1) * 64],
                        op0=ALU.mult, op1=ALU.add),
                        reads=[bps, b_bv], writes=[b_v[kc]], tag="vevac")

        def pass1_steps(hp, qt):
            c = hp
            t0 = qt * 512
            gsl = qt % 2
            steps = []

            for qs in range(4):
                for seg in range(NSEG):
                    for h in range(2):
                        def th(qs=qs, h=h, seg=seg):
                            k = (qs * NSEG * 2 + seg * 2 + h) % 2
                            pa, pb = (psP1a, bP1a) if k == 0 else (psP1b, bP1b)
                            sc_, bsc_ = scpy[k], b_scpy[k]
                            p.op("tensor", lambda t, qs=qs, h=h, seg=seg: t.matmul(
                                pa[:, :],
                                q_s[c][h * 64:h * 64 + 64,
                                       t0 + qs * 128:t0 + (qs + 1) * 128],
                                K_hp[h * 64:h * 64 + 64, seg * 512:(seg + 1) * 512],
                                start=True, stop=True),
                                reads=[b_qs[c], b_k], writes=[pb], tag="s1MM")
                            p.op("vector", lambda v, sc_=sc_: v.tensor_copy(
                                sc_[:, :], pa[:, :]),
                                reads=[pb], writes=[bsc_], tag="s1cp")
                            col = o_cm + gsl * 64 + qs * 16 + h * 8 + seg
                            p.op("vector", lambda v, sc_=sc_, col=col: v.tensor_scalar(
                                sc_[:, :], sc_[:, :], -1.0, None,
                                op0=ALU.mult, op1=ALU.min,
                                accum_out=PF.view(col, 1)),
                                reads=[bsc_], writes=[bsc_, b_cm[gsl], b_big1],
                                tag="s1max")
                        steps.append(th)

                def th2(qs=qs):
                    gc0 = o_gc + gsl * 128
                    base = o_cm + gsl * 64 + qs * 16
                    for h in range(2):
                        col = gc0 + qs * 2 + h
                        p.op("vector", lambda v, h=h, col=col: v.tensor_reduce(
                            PF.view(col, 1), PF.view(base + h * 8, NSEG),
                            axis=AX.X, op=ALU.min),
                            reads=[b_cm[gsl]], writes=[b_gc[gsl], b_big1], tag="gmin")
                        p.dma("gpsimd", lambda g, h=h, col=col: g.dma_start(
                            PB.t[h * 64:h * 64 + 1,
                                 o_grow + gsl * 512 + qs * 128:
                                 o_grow + gsl * 512 + qs * 128 + 128],
                            PF.t[0:128, col:col + 1]),
                            sem_gg[gsl], reads=[b_gc[gsl]], writes=[b_grow[gsl]],
                            tag="grow")
                steps.append(th2)
            return steps

        def emit_pass2(hp, qt, nxt):
            c = hp
            t0 = qt * 512
            gsl = qt % 2

            def emit_pv(kc):
                par = kc % 3
                for h in range(2):
                    psv, bpv = (psPVa, bPVa) if h == 0 else (psPVb, bPVb)
                    p.op("tensor", lambda t, kc=kc, h=h, par=par, psv=psv: t.matmul(
                        psv[:, :], V_hp[kc][:, h * 65:h * 65 + 65],
                        PB.t[:, PTof(par) + h * 512:
                             PTof(par) + (h + 1) * 512],
                        start=(kc == 0), stop=(kc == NKC - 1)),
                        reads=[b_v[kc], b_pt[par]], writes=[bpv],
                        war_free=(kc > 0), tag="pvMM")

            g0 = o_grow + gsl * 512
            for kc in range(NKC):
                pst = psST0 if kc % 2 == 0 else psST1
                bsth = [bST[2 * (kc % 2)], bST[2 * (kc % 2) + 1]]
                for h in range(2):
                    p.op("tensor", lambda t, kc=kc, h=h, pst=pst: t.matmul(
                        pst[:, h * 512:(h + 1) * 512],
                        K_hp[h * 64:h * 64 + 64, kc * 128:(kc + 1) * 128],
                        q_s[c][h * 64:h * 64 + 64, t0:t0 + 512],
                        start=True, stop=False, tile_position=(h * 64, 0)),
                        reads=[b_k, b_qs[c]], writes=[bsth[h]], tag="stMM")
                for h in range(2):
                    p.op("tensor", lambda t, h=h, pst=pst: t.matmul(
                        pst[:, h * 512:(h + 1) * 512], neg_b[h * 64:h * 64 + 1, :],
                        PB.t[h * 64:h * 64 + 1, g0:g0 + 512],
                        start=False, stop=True, tile_position=(h * 64, 0)),
                        reads=[bnegb, b_grow[gsl]], writes=[bsth[h]], war_free=True,
                        tag="subMM")
                for _ in range(2):
                    if nxt:
                        nxt.pop(0)()
                if kc > 1:
                    emit_pv(kc - 2)
                par3 = kc % 3
                p.op("scalar", lambda s, par3=par3, pst=pst: s.activation(
                    PB.t[:, PTof(par3):PTof(par3) + 1024],
                    pst[:, :], AF.Exp), reads=bsth, writes=[b_pt[par3]], tag="exp")
            emit_pv(NKC - 2)
            emit_pv(NKC - 1)
            rd = PF.view(o_rd, 1024, 64, 65)
            p.op("vector", lambda v: v.reciprocal(rd[:, 0:512], psPVa[64:65, :]),
                 reads=[bPVa], writes=[b_rd, b_big1], tag="recipA")
            p.op("vector", lambda v: v.reciprocal(rd[:, 512:1024], psPVb[64:65, :]),
                 reads=[bPVb], writes=[b_rd], war_free=True, tag="recipB")
            p.op("tensor", lambda t: t.matmul(
                psP1a[0:64, :], onesf_t[64:65, 0:64], rd[:, 0:512],
                start=True, stop=True), reads=[bonesf, b_rd], writes=[bP1a], tag="rbA")
            p.op("tensor", lambda t: t.matmul(
                psP1b[64:128, :], onesf_t[64:65, 0:64], rd[:, 512:1024],
                start=True, stop=True, tile_position=(64, 64)),
                reads=[bonesf, b_rd], writes=[bP1b], tag="rbB")
            p.op("vector", lambda v: v.tensor_copy(PF.view(o_rb, 512, 0, 64),
                 psP1a[0:64, :]), reads=[bP1a], writes=[b_rb, b_big1])
            p.op("vector", lambda v: v.tensor_copy(PF.view(o_rb, 512, 64, 128),
                 psP1b[64:128, :]), reads=[bP1b], writes=[b_rb], war_free=True)
            for h in range(2):
                psv, bpv = (psPVa, bPVa) if h == 0 else (psPVb, bPVb)
                p.op("vector", lambda v, h=h, psv=psv: v.scalar_tensor_tensor(
                    PF.view(o_st + 1024, 512, h * 64, (h + 1) * 64), psv[0:64, :], 1.0,
                    PF.view(o_rb, 512, h * 64, (h + 1) * 64),
                    op0=ALU.mult, op1=ALU.mult),
                    reads=[bpv, b_rb], writes=[b_stats],
                    war_free=(h == 1), tag="numdiv")
                p.op("vector", lambda v, h=h: v.tensor_add(
                    ao[c][h * 64:(h + 1) * 64, t0:t0 + 512],
                    PF.view(o_st + 1024, 512, h * 64, (h + 1) * 64),
                    q_s[c][h * 64:(h + 1) * 64, t0:t0 + 512]),
                    reads=[b_stats, b_qs[c]], writes=[b_ao[c]],
                    war_free=(h == 1), tag="aoadd")

        for hp in range(NP if not cfg.get("skip_attn") else 0):
            emit_kv(hp)
            for th in pass1_steps(hp, 0):
                th()
            for qt in range(NQT):
                nxt = pass1_steps(hp, qt + 1) if qt + 1 < NQT else []
                emit_pass2(hp, qt, nxt)
                for th in nxt:
                    th()

        p.dma("gpsimd", lambda g: g.dma_start(
            wsl_v, wo_d.rearrange("(n p) o -> p n o", p=128)),
            sem_w[0], writes=[b_ws, b_wsV] + b_pt + b_grow, tag="wo")
        for tt in range(NTC_O if not cfg.get("skip_attn") else 0):
            t0 = tt * 512
            p.dma("sync", lambda s, t0=t0: s.dma_start(
                PF.view(o_big2, ND * 512).rearrange("p (n t) -> p n t", t=512),
                fm(xT, t0, t0 + 512)), sem_big, writes=[b_big2], tag="xown")
            for o in range(ND):
                ps, bps = slot()
                for i in range(ND):
                    p.op("tensor", lambda t, i=i, o=o, t0=t0, ps=ps: t.matmul(
                        ps, wq_sl[i][:, o * 128:(o + 1) * 128], ao[i][:, t0:t0 + 512],
                        start=(i == 0), stop=(i == ND - 1)),
                        reads=[b_ws, b_ao[i]], writes=[bps], war_free=(i > 0), tag="woMM")
                p.op("scalar", lambda s, o=o, ps=ps: s.activation(
                    big1[o], ps, AF.Identity, bias=cc("bo", o), scale=1.0),
                    reads=[bps, bccol], writes=[b_big1], tag="woevac")
            emit_stats(cast_f32([big1[o] for o in range(ND)], [b_big1]))
            for o in range(ND):
                a, ab = asc[o % 2], b_asc[o % 2]
                p.op("vector", lambda v, o=o, a=a: v.tensor_sub(a, big1[o], mean_b),
                     reads=[b_big1, b_stats], writes=[ab])
                p.op("vector", lambda v, a=a: v.tensor_mul(a, a, rstd_b),
                     reads=[ab, b_stats], writes=[ab])
                p.op("vector", lambda v, o=o, a=a: v.tensor_add(big2[o], big2[o], a),
                     reads=[ab, b_big2], writes=[b_big2], tag="x1add")
            p.dma("sync", lambda s, t0=t0: s.dma_start(
                fm(x1d, t0, t0 + 512),
                PF.view(o_big2, ND * 512).rearrange("p (n t) -> p n t", t=512)),
                sem_big, reads=[b_big2], writes=[b_x1d], tag="x1st")

        if cfg.get("stop_after_A"):
            p.barrier("sync", [Tok("dma", sem=sem_big, idx=sem_big.count)])
            p.finalize(block)
            nc._layout = dict(o_xln=o_xln, o_qs=o_qs, o_ao=o_ao, o_k=o_k, o_v=o_v,
                              o_sc=o_sc, o_ws=o_ws,
                              o_st=o_st, o_big1=o_big1, o_big2=o_big2)
            return nc

        a_bufs = (b_xln + b_qs + b_ao + [b_k, b_kb1] + b_v + b_pt + b_grow +
                  [b_ws, b_wsV, b_xbs[0], b_xbs[1]])
        fence = p.op("vector", lambda v: v.engine_nop(), writes=a_bufs, tag="fence")

        h0 = [PB.view(o_h0 + j * TB, TB) for j in range(NJ)]
        h1v = [PB.view(o_h1 + j * TB, TB) for j in range(NJ)]
        xln2 = [PB.view(o_xln2 + i * TB, TB) for i in range(ND)]
        b_h0 = [p.buf(f"h0_{j}") for j in range(NJ)]
        b_h1 = [p.buf(f"h1_{j}") for j in range(NJ)]
        b_xln2 = [p.buf(f"xln2_{i}") for i in range(ND)]
        fsl = [PB.view(o_ws + s * 2048, 2048) for s in range(4)]
        b_fsl = [p.buf(f"fsl{s}") for s in range(4)]
        for b in b_h0 + b_h1 + b_xln2 + b_fsl:
            b.last_writer = fence
        _slot_pool[0] = [2, 3, 4, 5]

        def _stop(bufs):
            t = p.op("vector", lambda v: v.engine_nop(), reads=bufs, tag="stopper")
            p.barrier("sync", [t, Tok("dma", sem=sem_big, idx=sem_big.count)])
            p.finalize(block)
            nc._layout = dict(o_xln=o_xln, o_qs=o_qs, o_ao=o_ao, o_k=o_k, o_v=o_v,
                              o_sc=o_sc, o_ws=o_ws,
                              o_h0=o_h0, o_h1=o_h1, o_xln2=o_xln2,
                              o_st=o_st, o_big1=o_big1, o_big2=o_big2)
            return nc

        def emit_ln2(ft):
            tb0 = ft * TB
            for sb in range(NSB if not cfg.get("skip_ln2") else 0):
                t0 = tb0 + sb * 512
                p.dma("sync", lambda s, t0=t0: s.dma_start(
                    PF.view(o_big2, ND * 512).rearrange("p (n t) -> p n t", t=512),
                    fm(x1d, t0, t0 + 512)), sem_big, reads=[b_x1d],
                    writes=[b_big2], tag="x1ld")
                emit_stats(cast_f32([big2[i] for i in range(ND)], [b_big2]))
                for i in range(ND):
                    a, ab = asc[i % 2], b_asc[i % 2]
                    p.op("vector", lambda v, i=i, a=a: v.tensor_sub(a, big2[i], mean_b),
                         reads=[b_big2, b_stats], writes=[ab])
                    p.op("vector", lambda v, i=i, sb=sb, a=a: v.tensor_mul(
                        xln2[i][:, sb * 512:(sb + 1) * 512], a, rstd_b),
                        reads=[ab, b_stats], writes=[b_xln2[i]], tag="ln2ap")

        emit_ln2(0)
        for ft in range(NFT):
            tb0 = ft * TB
            if cfg.get("stop_after_LN2"):
                return _stop(b_xln2)
            for jg in range(DH // 256 if not cfg.get("s1_one_jg") else 1):
                wi = jg % 2
                p.dma("gpsimd", lambda g, jg=jg, wi=wi: g.dma_start(
                    fsl[wi].rearrange("p (n o) -> p n o", o=256),
                    w1t_d.rearrange("(n p) o -> p n o", p=128)[:, :, jg * 256:(jg + 1) * 256]),
                    sem_w[wi], writes=[b_fsl[wi]], tag="w1l")
                p.dma("gpsimd", lambda g, jg=jg, wi=wi: g.dma_start(
                    fsl[2 + wi].rearrange("p (n o) -> p n o", o=256),
                    w2t_d.rearrange("(n p) o -> p n o", p=128)[:, :, jg * 256:(jg + 1) * 256]),
                    sem_w[2 + wi], writes=[b_fsl[2 + wi]], tag="w2l")
                for jj in range(2):
                    j = jg * 2 + jj
                    for sb in range(NSB):
                        psa, bpsa = slot()
                        psb2, bpsb2 = slot()
                        for i in range(ND):
                            p.op("tensor", lambda t, i=i, jj=jj, sb=sb, psa=psa, wi=wi: t.matmul(
                                psa, fsl[wi][:, i * 256 + jj * 128:i * 256 + (jj + 1) * 128],
                                xln2[i][:, sb * 512:(sb + 1) * 512],
                                start=(i == 0), stop=(i == ND - 1)),
                                reads=[b_fsl[wi], b_xln2[i]], writes=[bpsa],
                                war_free=(i > 0), tag="w1MM")
                        for i in range(ND):
                            p.op("tensor", lambda t, i=i, jj=jj, sb=sb, psb2=psb2, wi=wi: t.matmul(
                                psb2, fsl[2 + wi][:, i * 256 + jj * 128:i * 256 + (jj + 1) * 128],
                                xln2[i][:, sb * 512:(sb + 1) * 512],
                                start=(i == 0), stop=(i == ND - 1)),
                                reads=[b_fsl[2 + wi], b_xln2[i]], writes=[bpsb2],
                                war_free=(i > 0), tag="w2MM")
                        _af1 = AF.Identity if cfg.get("no_silu") else (AF.Gelu if cfg.get("silu_as_gelu") else (AF.Sigmoid if cfg.get("silu_as_sigmoid") else AF.Silu))
                        if cfg.get("one_silu") and (jg, jj, sb) != (0, 0, 0):
                            _af1 = AF.Identity
                        _sc1 = 0.5 if cfg.get("silu_const_scale") else cc("w1s", j)
                        if cfg.get("silu_from_sbuf"):
                            p.op("scalar", lambda s, psa=psa: s.activation(
                                sc16[2][:, :], psa, AF.Identity, bias=0.0, scale=1.0),
                                reads=[bpsa], writes=[b_sc16[2], b_kb1], tag="pre")
                            p.op("scalar", lambda s, j=j, _af1=_af1, _sc1=_sc1: s.activation(
                                sc16[0][:, :], sc16[2][:, :], _af1, bias=0.0, scale=_sc1),
                                reads=[b_sc16[2], bccol], writes=[b_sc16[0], b_kb1], tag="silu")
                        else:
                            p.op("scalar", lambda s, j=j, psa=psa, _af1=_af1, _sc1=_sc1: s.activation(
                                sc16[0][:, :], psa, _af1, bias=0.0, scale=_sc1),
                                reads=[bpsa, bccol], writes=[b_sc16[0], b_kb1], tag="silu")
                        _af2 = AF.Silu if cfg.get("w2_as_silu") else AF.Identity
                        p.op("scalar", lambda s, j=j, psb2=psb2, _af2=_af2: s.activation(
                            sc16[1][:, :], psb2, _af2, bias=0.0, scale=cc("w2s", j)),
                            reads=[bpsb2, bccol], writes=[b_sc16[1], b_kb1], tag="w2sc")
                        p.op("vector", lambda v, j=j, sb=sb: v.tensor_mul(
                            h0[j][:, sb * 512:(sb + 1) * 512], sc16[0][:, :], sc16[1][:, :]),
                            reads=[b_sc16[0], b_sc16[1]], writes=[b_h0[j]], tag="h0mul")
            if cfg.get("stop_after_S1"):
                return _stop(b_h0)
            if ft + 1 < NFT:
                emit_ln2(ft + 1)
            for (wt_d, scn, src, bsrc, dst, bdst) in (
                    (h1t_d, "h1s", h0, b_h0, h1v, b_h1),
                    (h2t_d, "h2s", h1v, b_h1, h0, b_h0)):
                for j2 in range(NJ):
                    wi = j2 % 2
                    for qd in range(4):
                        jq0 = qd * NJ // 4
                        p.dma("gpsimd", lambda g, j2=j2, wi=wi, wt_d=wt_d, jq0=jq0: g.dma_start(
                            PB.view(o_ws + wi * 4096 + jq0 * 128, NJ * 32).rearrange(
                                "p (n o) -> p n o", o=128),
                            wt_d.rearrange("(n p) o -> p n o", p=128)[
                                :, jq0:jq0 + NJ // 4, j2 * 128:(j2 + 1) * 128]),
                            sem_w[wi], writes=[b_fsl[2 * wi], b_fsl[2 * wi + 1]],
                            war_free=(qd > 0), tag="hwl")
                    for sb in range(NSB):
                        ps, bps = slot()
                        for j in range(NJ):
                            p.op("tensor", lambda t, j=j, sb=sb, ps=ps, wi=wi, src=src: t.matmul(
                                ps, PB.view(o_ws + wi * 4096 + j * 128, 128),
                                src[j][:, sb * 512:(sb + 1) * 512],
                                start=(j == 0), stop=(j == NJ - 1)),
                                reads=[b_fsl[2 * wi], b_fsl[2 * wi + 1], bsrc[j]],
                                writes=[bps], war_free=(j > 0), tag="hMM")
                        p.op("scalar", lambda s, j2=j2, sb=sb, ps=ps, scn=scn, dst=dst,
                             bdst=bdst: s.activation(
                            dst[j2][:, sb * 512:(sb + 1) * 512], ps, AF.Gelu,
                            bias=0.0, scale=cc(scn, j2)),
                            reads=[bps, bccol], writes=[bdst[j2]], tag="gelu")
            if cfg.get("stop_after_S3"):
                return _stop(b_h0 + b_h1)
            for sb in range(NSB):
                t0 = tb0 + sb * 512
                for o in range(ND):
                    wi = o % 2
                    for qd in range(4):
                        jq0 = qd * NJ // 4
                        p.dma("gpsimd", lambda g, o=o, wi=wi, jq0=jq0: g.dma_start(
                            PB.view(o_ws + wi * 4096 + jq0 * 128, NJ * 32).rearrange(
                                "p (n o2) -> p n o2", o2=128),
                            w3t_d.rearrange("(n p) o2 -> p n o2", p=128)[
                                :, jq0:jq0 + NJ // 4, o * 128:(o + 1) * 128]),
                            sem_w[wi], writes=[b_fsl[2 * wi], b_fsl[2 * wi + 1]],
                            war_free=(qd > 0), tag="w3l")
                    ps, bps = slot()
                    for j in range(NJ):
                        p.op("tensor", lambda t, j=j, sb=sb, ps=ps, wi=wi: t.matmul(
                            ps, PB.view(o_ws + wi * 4096 + j * 128, 128),
                            h0[j][:, sb * 512:(sb + 1) * 512],
                            start=(j == 0), stop=(j == NJ - 1)),
                            reads=[b_fsl[2 * wi], b_fsl[2 * wi + 1], b_h0[j]],
                            writes=[bps], war_free=(j > 0), tag="w3MM")
                    p.op("scalar", lambda s, o=o, ps=ps: s.activation(
                        big1[o], ps, AF.Identity, bias=0.0, scale=cc("w3s", o)),
                        reads=[bps, bccol], writes=[b_big1], tag="w3evac")
                p.dma("sync", lambda s, t0=t0: s.dma_start(
                    PF.view(o_big2, ND * 512).rearrange("p (n t) -> p n t", t=512),
                    fm(x1d, t0, t0 + 512)), sem_big, reads=[b_x1d],
                    writes=[b_big2], tag="x1ld2")
                emit_stats(cast_f32([big1[o] for o in range(ND)], [b_big1]))
                for o in range(ND):
                    a, ab = asc[o % 2], b_asc[o % 2]
                    p.op("vector", lambda v, o=o, a=a: v.tensor_sub(a, big1[o], mean_b),
                         reads=[b_big1, b_stats], writes=[ab])
                    p.op("vector", lambda v, a=a: v.tensor_mul(a, a, rstd_b),
                         reads=[ab, b_stats], writes=[ab])
                    p.op("vector", lambda v, o=o, a=a: v.tensor_add(big2[o], big2[o], a),
                         reads=[ab, b_big2], writes=[b_big2], tag="x2add")
                p.dma("sync", lambda s, t0=t0: s.dma_start(
                    fm(outT, t0, t0 + 512),
                    PF.view(o_big2, ND * 512).rearrange("p (n t) -> p n t", t=512)),
                    sem_st, reads=[b_big2], tag="outst")

        p.barrier("sync", [Tok("dma", sem=sem_st, idx=sem_st.count)])
        p.finalize(block)
    nc._layout = dict(o_xln=o_xln, o_qs=o_qs, o_ao=o_ao, o_k=o_k, o_v=o_v,
                      o_sc=o_sc, o_ws=o_ws,
                      o_h0=o_h0, o_h1=o_h1, o_xln2=o_xln2,
                      o_st=o_st, o_big1=o_big1, o_big2=o_big2)
    return nc


_CACHE = {}


def _get_nc(cfg_key, cfg):
    if cfg_key not in _CACHE:
        _CACHE[cfg_key] = build_program(cfg)
    return _CACHE[cfg_key]


def make_host_inputs(cfg, x, Wq, bq, Wk, bk, Wv, bv, Wo, bo, ln_g, ln_b,
                     w1_w, w1_s, w2_w, w2_s, h1_w, h1_s, h2_w, h2_s, w3_w, w3_s):
    D, To, Tf = cfg["D"], cfg["To"], cfg["Tf"]
    B = x.shape[0]
    f = np.float32
    ac = np.ascontiguousarray

    def col8(vec, n):
        return vec.reshape(n, 128).T.astype(f)

    ccol = np.zeros((128, NCC), f)
    ccol[:, CC["bq"]:CC["bq"] + 8] = col8(bq, 8)
    ccol[:, CC["bk"]:CC["bk"] + 8] = col8(bk, 8)
    ccol[:, CC["bo"]:CC["bo"] + 8] = col8(bo, 8)
    ccol[:, CC["lng"]:CC["lng"] + 8] = col8(ln_g, 8)
    ccol[:, CC["lnb"]:CC["lnb"] + 8] = col8(ln_b, 8)
    ccol[:, CC["w1s"]:CC["w1s"] + 32] = col8(w1_s, 32)
    ccol[:, CC["w2s"]:CC["w2s"] + 32] = col8(w2_s, 32)
    ccol[:, CC["h1s"]:CC["h1s"] + 32] = col8(h1_s, 32)
    ccol[:, CC["h2s"]:CC["h2s"] + 32] = col8(h2_s, 32)
    ccol[:, CC["w3s"]:CC["w3s"] + 8] = col8(w3_s, 8)
    ccol[:, CC["eps"]] = 1e-5
    crow = np.concatenate([bv.astype(f), np.full(128, -1.0, f)])[None, :]
    shared = dict(
        wq=ac(Wq.astype(f)), wk=ac(Wk.astype(f)), wv=ac(Wv.astype(f)),
        wo=ac(Wo.astype(f)),
        w1t=ac(w1_w.T.astype(f)), w2t=ac(w2_w.T.astype(f)),
        h1t=ac(h1_w.T.astype(f)), h2t=ac(h2_w.T.astype(f)),
        w3t=ac(w3_w.T.astype(f)),
        ccol=ccol, crow=ac(crow.astype(f)),
        cones=np.ones((128, 128), f),
    )
    in_maps = []
    for b in range(B):
        xt = ac(x[b].T.astype(f))
        for hh in range(2):
            if hh == 0:
                xcore = xt
            else:
                xcore = ac(np.concatenate([xt[:, To:], xt[:, :To]], axis=1))
            in_maps.append(dict(shared, xT=xcore))
    return in_maps


def kernel(L0_attn_x, L0_attn_Wq, L0_attn_bq, L0_attn_Wk, L0_attn_bk,
           L0_attn_Wv, L0_attn_bv, L0_attn_Wo, L0_attn_bo,
           L0_attn_ln_g, L0_attn_ln_b,
           L0_ffn_w1_w, L0_ffn_w1_s, L0_ffn_w2_w, L0_ffn_w2_s,
           L0_ffn_h1_w, L0_ffn_h1_s, L0_ffn_h2_w, L0_ffn_h2_s,
           L0_ffn_w3_w, L0_ffn_w3_s):
    cfg = default_cfg()
    x = np.asarray(L0_attn_x)
    B, S, D = x.shape
    To = cfg["To"]
    nc = _get_nc("full", cfg)
    in_maps = make_host_inputs(
        cfg, x, L0_attn_Wq, L0_attn_bq, L0_attn_Wk, L0_attn_bk,
        L0_attn_Wv, L0_attn_bv, L0_attn_Wo, L0_attn_bo,
        L0_attn_ln_g, L0_attn_ln_b,
        L0_ffn_w1_w, L0_ffn_w1_s, L0_ffn_w2_w, L0_ffn_w2_s,
        L0_ffn_h1_w, L0_ffn_h1_s, L0_ffn_h2_w, L0_ffn_h2_s,
        L0_ffn_w3_w, L0_ffn_w3_s)
    res = run_bass_kernel_spmd(nc, in_maps, core_ids=list(range(8)))
    out = np.empty((B, S, D), np.float32)
    for b in range(B):
        out[b, :To, :] = res.results[2 * b]["outT"].T
        out[b, To:, :] = res.results[2 * b + 1]["outT"].T
    return out


# version 14
# speedup vs baseline: 1.4573x; 1.0808x over previous
import sys
sys.path.insert(0, '/opt/trn_rl_repo')
from contextlib import ExitStack
import numpy as np
import concourse.bass as bass
import concourse.mybir as mybir
from concourse.bass_utils import run_bass_kernel_spmd

F32 = mybir.dt.float32
BF16 = mybir.dt.bfloat16
AF = mybir.ActivationFunctionType
ALU = mybir.AluOpType
AX = mybir.AxisListType

ENGINES = ("tensor", "vector", "scalar", "gpsimd", "sync")


class Buf:
    __slots__ = ("name", "last_writer", "readers", "psum")

    def __init__(self, name, psum=False):
        self.name = name
        self.last_writer = None
        self.readers = []
        self.psum = psum


class DmaSem:
    __slots__ = ("handle", "count", "name")

    def __init__(self, handle, name):
        self.handle = handle
        self.count = 0
        self.name = name


class Op:
    __slots__ = ("engine", "fn", "seq", "inc", "sem_val", "deps", "dma_sem", "tag")

    def __init__(self, engine, fn, seq, dma_sem, tag):
        self.engine = engine
        self.fn = fn
        self.seq = seq
        self.inc = dma_sem is not None
        self.sem_val = None
        self.deps = []
        self.dma_sem = dma_sem
        self.tag = tag


class Tok:
    __slots__ = ("kind", "engine", "idx", "sem")

    def __init__(self, kind, engine=None, idx=None, sem=None):
        self.kind = kind
        self.engine = engine
        self.idx = idx
        self.sem = sem


class Prog:
    def __init__(self, nc, stack):
        self.nc = nc
        self.stack = stack
        self.ops = {e: [] for e in ENGINES}
        self.sems = {e: stack.enter_context(nc.semaphore(f"pg_{e}")) for e in ENGINES}
        self.n_bufs = 0
        self.n_dsems = 0

    def buf(self, name=None, psum=False):
        self.n_bufs += 1
        return Buf(name or f"b{self.n_bufs}", psum)

    def dma_sem(self, name=None):
        self.n_dsems += 1
        name = name or f"ds{self.n_dsems}"
        return DmaSem(self.stack.enter_context(self.nc.semaphore(name)), name)

    def _collect_deps(self, o, tok, reads, writes, war_free):
        deps = []

        def add(d):
            if d is tok:
                return
            if d.kind == "eng":
                deps.append(("eng", d.engine, d.idx))
            else:
                deps.append(("dma", d.sem, 16 * d.sem.count))

        for b in reads:
            if b.last_writer is not None:
                add(b.last_writer)
            if b.psum:
                for r in b.readers:
                    if r.kind == "eng" and r.engine != tok.engine:
                        add(r)
            b.readers.append(tok)
        for b in writes:
            if not war_free:
                if b.last_writer is not None:
                    add(b.last_writer)
                for r in b.readers:
                    add(r)
            b.readers = []
            b.last_writer = tok
        o.deps = deps

    def op(self, engine, fn, reads=(), writes=(), war_free=False, tag=""):
        seq = len(self.ops[engine])
        tok = Tok("eng", engine=engine, idx=seq)
        o = Op(engine, fn, seq, None, tag)
        self._collect_deps(o, tok, reads, writes, war_free)
        self.ops[engine].append(o)
        return tok

    def dma(self, qengine, fn, sem, reads=(), writes=(), war_free=False, tag=""):
        seq = len(self.ops[qengine])
        tok = Tok("dma", engine=qengine, idx=None, sem=sem)
        o = Op(qengine, fn, seq, sem, tag)
        self._collect_deps(o, tok, reads, writes, war_free)
        sem.count += 1
        tok.idx = sem.count
        self.ops[qengine].append(o)
        return tok

    def barrier(self, engine, toks, tag="barrier"):
        seq = len(self.ops[engine])
        o = Op(engine, lambda eng: eng.nop(), seq, None, tag)
        deps = []
        for d in toks:
            if d.kind == "eng":
                deps.append(("eng", d.engine, d.idx))
            else:
                deps.append(("dma", d.sem, 16 * d.sem.count))
        o.deps = deps
        self.ops[engine].append(o)
        return Tok("eng", engine=engine, idx=seq)

    def _resolve(self):
        for e in ENGINES:
            for o in self.ops[e]:
                kept = []
                for d in o.deps:
                    if d[0] == "eng":
                        if d[1] == "tensor" and e == "tensor":
                            continue
                        self.ops[d[1]][d[2]].inc = True
                    kept.append(d)
                o.deps = kept
        for e in ENGINES:
            v = 0
            for o in self.ops[e]:
                if o.dma_sem is not None:
                    continue
                if o.inc:
                    v += 1
                    o.sem_val = v

    def finalize(self, block):
        self._resolve()
        prog = self

        def section(e):
            def body(eng):
                watermark = {}
                for o in prog.ops[e]:
                    for d in o.deps:
                        if d[0] == "eng":
                            sem = prog.sems[d[1]]
                            val = prog.ops[d[1]][d[2]].sem_val
                            assert val is not None, (d, o.tag)
                        else:
                            sem = d[1].handle
                            val = d[2]
                        key = id(sem)
                        if watermark.get(key, -1) >= val:
                            continue
                        watermark[key] = val
                        eng.wait_ge(sem, val)
                    inst = o.fn(eng)
                    if o.dma_sem is not None:
                        inst.then_inc(o.dma_sem.handle, 16)
                    elif o.inc:
                        inst.then_inc(prog.sems[e], 1)
            return body

        for e in ENGINES:
            if not self.ops[e]:
                continue
            getattr(block, e)(section(e))


class Pool:

    def __init__(self, nc, stack, name, dtype, elems):
        self.t = stack.enter_context(nc.sbuf_tensor(name, [128, elems], dtype))
        self.elems = elems
        self.off = 0

    def alloc(self, n, align=64):
        o = -(-self.off // align) * align
        assert o + n <= self.elems, f"pool overflow: {o}+{n} > {self.elems}"
        self.off = o + n
        return o

    def view(self, off, n, p0=0, p1=128):
        return self.t[p0:p1, off:off + n]


CC = {}
_o = 0
for _nm, _n in [("bq", 8), ("bk", 8), ("bo", 8), ("lng", 8), ("lnb", 8),
                ("w1s", 32), ("w2s", 32), ("h1s", 32), ("h2s", 32), ("w3s", 8),
                ("eps", 1)]:
    CC[_nm] = _o
    _o += _n
NCC = _o


def default_cfg():
    return dict(D=1024, H=16, DH=4096, To=2048, Tf=4096)


def build_program(cfg):
    D, H, DH, To, Tf = cfg["D"], cfg["H"], cfg["DH"], cfg["To"], cfg["Tf"]
    ND = D // 128
    NP = H // 2
    NJ = DH // 128
    NKC = Tf // 128
    NSEG = Tf // 512
    NQT = To // 512
    NTC_F = Tf // 512
    NTC_O = To // 512
    TB = min(1024, To)
    NFT = To // TB
    NSB = TB // 512
    assert NP == ND and D == 1024

    nc = bass.Bass("TRN2", target_bir_lowering=False, debug=False,
                   dynamic_dma_scratch_size=16384)
    di = lambda n, s: nc.dram_tensor(n, s, F32, kind="ExternalInput").ap()
    xT = di("xT", [D, Tf])
    wq_d, wk_d, wv_d, wo_d = di("wq", [D, D]), di("wk", [D, D]), di("wv", [D, D]), di("wo", [D, D])
    w1t_d, w2t_d = di("w1t", [D, DH]), di("w2t", [D, DH])
    h1t_d, h2t_d = di("h1t", [DH, DH]), di("h2t", [DH, DH])
    w3t_d = di("w3t", [DH, D])
    ccol_d = di("ccol", [128, NCC])
    crow_d = di("crow", [1, D + 128])
    cones_d = di("cones", [128, 128])
    x1d = nc.dram_tensor("x1d", [D, To], F32, kind="Internal").ap()
    outT = nc.dram_tensor("outT", [D, To], F32, kind="ExternalOutput").ap()

    def fm(dram_ap, t0, t1, n_last=None):
        r = dram_ap.rearrange("(n p) t -> p n t", p=128)
        return r[:, :, t0:t1]

    stack = ExitStack()
    with stack:
        E = stack.enter_context
        p = Prog(nc, stack)

        _o_sc = 0
        _o_ws = _o_sc + 4 * 512
        _main = _o_ws + 2 * ND * 512
        _a_end = _main + ND * Tf + 2 * ND * To + Tf + NKC * 130
        _b_end = _main + 2 * NJ * TB + ND * TB
        pb_elems = max(_a_end, _b_end) + 64
        pf_elems = (3 * 512 + 2 * ND * 512)
        PB = Pool(nc, stack, "PB", BF16, pb_elems)
        PF = Pool(nc, stack, "PF", F32, pf_elems)

        ccol_t = E(nc.sbuf_tensor("ccol_t", [128, NCC], F32))
        onesf_t = E(nc.sbuf_tensor("onesf_t", [128, 128], F32))
        ones_b = E(nc.sbuf_tensor("ones_b", [128, 128], BF16))
        neg_b = E(nc.sbuf_tensor("neg_b", [128, 128], BF16))
        bks_t = E(nc.sbuf_tensor("bks_t", [128, 8], F32))

        psP1a = E(nc.psum_tensor("psP1a", [128, 512], F32))
        psP1b = E(nc.psum_tensor("psP1b", [128, 512], F32))
        psST0 = E(nc.psum_tensor("psST0", [128, 1024], F32))
        psST1 = E(nc.psum_tensor("psST1", [128, 1024], F32))
        psPVa = E(nc.psum_tensor("psPVa", [65, 512], F32))
        psPVb = E(nc.psum_tensor("psPVb", [65, 512], F32))
        block = E(nc.Block())

        bP1a, bP1b = p.buf("psP1a", True), p.buf("psP1b", True)
        bST = [p.buf(f"psST{i}", True) for i in range(4)]
        bPVa, bPVb = p.buf("psPVa", True), p.buf("psPVb", True)
        _slots = [(psP1a, 0, bP1a), (psP1b, 0, bP1b),
                  (psST0, 0, bST[0]), (psST0, 512, bST[1]),
                  (psST1, 0, bST[2]), (psST1, 512, bST[3])]
        _si = [0]

        _slot_pool = [list(range(6))]

        def slot():
            pool = _slot_pool[0]
            h, lo, b = _slots[pool[_si[0] % len(pool)]]
            _si[0] += 1
            return h[:, lo:lo + 512], b

        bccol, bonesf, bonesb, bnegb, bbks = (p.buf(x) for x in
            ["ccol", "onesf", "onesb", "negb", "bks"])

        sem_c = p.dma_sem("c")
        sem_cg = p.dma_sem("cg")
        sem_x = [p.dma_sem("x0"), p.dma_sem("x1")]
        sem_w = [p.dma_sem(f"w{i}") for i in range(4)]
        sem_bv = p.dma_sem("bv")
        sem_qg = [p.dma_sem("qg0"), p.dma_sem("qg1")]
        sem_gg = [p.dma_sem("gg0"), p.dma_sem("gg1")]
        sem_big = p.dma_sem("big")
        sem_st = p.dma_sem("st")

        p.dma("sync", lambda s: s.dma_start(ccol_t[:, :], ccol_d), sem_c, writes=[bccol])
        p.dma("sync", lambda s: s.dma_start(onesf_t[:, :], cones_d), sem_c, writes=[bonesf])
        p.dma("gpsimd", lambda g: g.dma_start(ones_b[:, :], cones_d), sem_cg, writes=[bonesb])
        p.dma("gpsimd", lambda g: g.dma_start(neg_b[0:1, :], crow_d[0:1, D:D + 128]),
              sem_cg, writes=[bnegb])
        p.dma("gpsimd", lambda g: g.dma_start(neg_b[64:65, :], crow_d[0:1, D:D + 128]),
              sem_cg, writes=[bnegb], war_free=True)
        p.op("vector", lambda v: v.tensor_scalar_mul(
            bks_t[:, :], ccol_t[:, CC["bk"]:CC["bk"] + 8], 0.125),
            reads=[bccol], writes=[bbks])

        def cc(nm, i=0):
            return ccol_t[:, CC[nm] + i:CC[nm] + i + 1]

        eps_ap = ccol_t[:, CC["eps"]:CC["eps"] + 1]

        o_sc = _o_sc
        o_ws = _o_ws
        o_xln = _main
        o_qs = o_xln + ND * Tf
        o_ao = o_qs + ND * To
        o_k = o_ao + ND * To
        o_v = o_k + Tf
        o_pt_m = _o_ws + 5120
        o_scpy = _o_ws + 2048
        o_pt0 = _o_ws + 3072
        o_grow = _o_ws + 4096
        assert Tf <= 4096
        o_h0 = _main
        o_h1 = o_h0 + NJ * TB
        o_xln2 = o_h1 + NJ * TB

        xln = [PB.view(o_xln + i * Tf, Tf) for i in range(ND)]
        q_s = [PB.view(o_qs + i * To, To) for i in range(ND)]
        ao = [PB.view(o_ao + i * To, To) for i in range(ND)]
        K_hp = PB.view(o_k, Tf)
        V_hp = [PB.view(o_v + k * 130, 130) for k in range(NKC)]
        sc16 = [PB.view(o_sc + i * 512, 512) for i in range(4)]
        b_xln = [p.buf(f"xln{i}") for i in range(ND)]
        b_qs = [p.buf(f"qs{i}") for i in range(ND)]
        b_ao = [p.buf(f"ao{i}") for i in range(ND)]
        b_k = p.buf("K_hp")
        b_v = [p.buf(f"V{k}") for k in range(NKC)]
        b_pt = [p.buf("pt0"), p.buf("pt1"), p.buf("pt2")]

        def PTof(par):
            return o_pt0 if par == 0 else o_pt_m + (par - 1) * 1024
        scpy = [PB.view(o_scpy + i * 512, 512) for i in range(2)]
        b_scpy = [p.buf("scpy0"), p.buf("scpy1")]
        b_grow = [p.buf("grow0"), p.buf("grow1")]
        b_kb1 = p.buf("kb1")
        b_sc16 = [p.buf(f"sc16_{i}") for i in range(4)]
        b_ws = p.buf("wslab")
        b_wsV = p.buf("wslabV")
        b_xbs = [p.buf("xb0"), p.buf("xb1")]

        o_st = PF.alloc(3 * 512)
        mean_b = PF.view(o_st, 512)
        rstd_b = PF.view(o_st + 512, 512)
        s1f = PF.view(o_st + 1024, 512)
        b_stats = p.buf("stats")
        asc = [s1f, s1f]
        b_asc = [b_stats, b_stats]

        o_big1 = PF.alloc(ND * 512)
        o_rd = o_big1
        o_rb = o_big1 + 1024
        o_bv = o_big1 + 1536
        o_brow = o_big1 + 1664
        o_cm = o_big1 + 1792
        o_gc = o_big1 + 1920
        o_big2 = PF.alloc(ND * 512)
        big1 = [PF.view(o_big1 + i * 512, 512) for i in range(ND)]
        big2 = [PF.view(o_big2 + i * 512, 512) for i in range(ND)]
        b_stats = p.buf("stats")
        b_cm = [p.buf("cm0"), p.buf("cm1")]
        b_gc = [p.buf("gc0"), p.buf("gc1")]
        b_rd, b_rb, b_bv, b_brow = p.buf("rd"), p.buf("rb"), p.buf("bv"), p.buf("brow")
        b_big1, b_big2 = p.buf("big1"), p.buf("big2")
        b_x1d = p.buf("x1d_dram")

        def emit_stats(bf_chunks):
            n = len(bf_chunks)
            for i, (kind, xap, xbufs) in enumerate(bf_chunks):
                if kind == "f32":
                    dst, db = sc16[i % 2], b_sc16[i % 2]
                    p.op("vector", lambda v, xap=xap, dst=dst: v.tensor_copy(
                        dst[:, :], xap), reads=list(xbufs), writes=[db, b_kb1], tag="cast")
                    xap, xbufs = dst[:, :], [db]
                sq, bsq = sc16[2 + (i % 2)], b_sc16[2 + (i % 2)]
                p.op("vector", lambda v, xap=xap, sq=sq: v.tensor_mul(sq[:, :], xap, xap),
                     reads=list(xbufs), writes=[bsq, b_kb1], tag="sq")
                p.op("tensor", lambda t, xap=xap, i=i, n=n: t.matmul(
                    psP1a[:, :], ones_b[:, :], xap, start=(i == 0), stop=(i == n - 1)),
                    reads=[bonesb] + list(xbufs), writes=[bP1a], war_free=(i > 0), tag="sMM")
                p.op("tensor", lambda t, sq=sq, i=i, n=n: t.matmul(
                    psP1b[:, :], ones_b[:, :], sq[:, :], start=(i == 0), stop=(i == n - 1)),
                    reads=[bonesb, bsq], writes=[bP1b], war_free=(i > 0), tag="qMM")
            inv = 1.0 / (n * 128)
            p.op("vector", lambda v: v.tensor_scalar_mul(mean_b, psP1a[:, :], inv),
                 reads=[bP1a], writes=[b_stats])
            p.op("vector", lambda v: v.tensor_scalar_mul(s1f, psP1b[:, :], inv),
                 reads=[bP1b], writes=[b_stats])
            p.op("vector", lambda v: v.scalar_tensor_tensor(
                rstd_b, mean_b, 1.0, mean_b, op0=ALU.mult, op1=ALU.mult),
                reads=[b_stats], writes=[b_stats])
            p.op("vector", lambda v: v.tensor_sub(s1f, s1f, rstd_b),
                 reads=[b_stats], writes=[b_stats])
            p.op("scalar", lambda s: s.activation(s1f, s1f, AF.Sqrt, bias=eps_ap, scale=1.0),
                 reads=[b_stats, bccol], writes=[b_stats])
            p.op("vector", lambda v: v.reciprocal(rstd_b, s1f),
                 reads=[b_stats], writes=[b_stats])

        def cast_f32(srcs, bufs_read):
            return [("f32", ap, list(bufs_read)) for ap in srcs]

        xb = [[PB.view(o_ws + sl * ND * 512 + i * 512, 512) for i in range(ND)]
              for sl in range(2)]
        for tc in range(NTC_F if not cfg.get("skip_a1") else 0):
            sl = tc % 2
            dst = PB.view(o_ws + sl * ND * 512, ND * 512).rearrange(
                "p (n t) -> p n t", t=512)
            p.dma("gpsimd", lambda g, tc=tc, dst=dst: g.dma_start(
                dst, fm(xT, tc * 512, (tc + 1) * 512)),
                sem_x[sl], writes=[b_xbs[sl]], tag="xload")
            emit_stats([("bf", xb[sl][i][:, :], [b_xbs[sl]]) for i in range(ND)])
            for i in range(ND):
                a, ab = asc[i % 2], b_asc[i % 2]
                p.op("vector", lambda v, i=i, sl=sl, a=a: v.tensor_sub(
                    a, xb[sl][i][:, :], mean_b), reads=[b_xbs[sl], b_stats], writes=[ab])
                p.op("vector", lambda v, i=i, tc=tc, a=a: v.tensor_mul(
                    xln[i][:, tc * 512:(tc + 1) * 512], a, rstd_b),
                    reads=[ab, b_stats], writes=[b_xln[i]], tag="ln1ap")

        wsl_v = PB.view(o_ws, ND * 1024).rearrange("p (n o) -> p n o", o=1024)
        wq_sl = [PB.view(o_ws + i * 1024, 1024) for i in range(ND)]
        p.dma("gpsimd", lambda g: g.dma_start(
            wsl_v, wq_d.rearrange("(n p) o -> p n o", p=128)),
            sem_w[0], writes=[b_ws, b_wsV, b_xbs[0], b_xbs[1]] + b_pt + b_grow, tag="wq")
        for tt in range(NQT if not cfg.get("skip_a2") else 0):
            t0 = tt * 512
            for o in range(ND):
                ps, bps = slot()
                for i in range(ND):
                    p.op("tensor", lambda t, i=i, o=o, t0=t0, ps=ps: t.matmul(
                        ps, wq_sl[i][:, o * 128:(o + 1) * 128],
                        xln[i][:, t0:t0 + 512], start=(i == 0), stop=(i == ND - 1)),
                        reads=[b_ws, b_xln[i]], writes=[bps], war_free=(i > 0), tag="qpMM")
                p.op("scalar", lambda s, o=o, ps=ps: s.activation(
                    big1[o], ps, AF.Identity, bias=cc("bq", o), scale=1.0),
                    reads=[bps, bccol], writes=[b_big1], tag="qevac")
            emit_stats(cast_f32([big1[o] for o in range(ND)], [b_big1]))
            for o in range(ND):
                a, ab = asc[o % 2], b_asc[o % 2]
                p.op("vector", lambda v, o=o, a=a: v.tensor_sub(a, big1[o], mean_b),
                     reads=[b_big1, b_stats], writes=[ab])
                p.op("vector", lambda v, a=a: v.tensor_mul(a, a, rstd_b),
                     reads=[ab, b_stats], writes=[ab])
                p.op("scalar", lambda s, o=o, t0=t0, a=a: s.activation(
                    q_s[o][:, t0:t0 + 512], a, AF.Identity,
                    bias=cc("lnb", o), scale=cc("lng", o)),
                    reads=[ab, bccol], writes=[b_qs[o]], tag="qln")

        if cfg.get("stop_after_A2"):
            p.barrier("sync", [Tok("dma", sem=sem_c, idx=sem_c.count)])
            p.finalize(block)
            nc._layout = dict(o_xln=o_xln, o_qs=o_qs, o_ao=o_ao, o_k=o_k, o_v=o_v,
                              o_sc=o_sc, o_ws=o_ws,
                              o_st=o_st, o_big1=o_big1, o_big2=o_big2)
            return nc

        vvíew = PB.t[:, o_v:o_v + NKC * 130].rearrange("p (k c) -> p k c", c=130)
        p.op("vector", lambda v: v.memset(vvíew[:, :, 64:65], 1.0),
             writes=list(b_v), tag="vones")
        p.op("vector", lambda v: v.memset(vvíew[:, :, 129:130], 1.0),
             writes=list(b_v), war_free=True, tag="vones")

        p.op("vector", lambda v: v.memset(PF.view(o_gc, 256), 0.0),
             writes=[b_gc[0], b_gc[1], b_big1], tag="gcinit")
        kvK = [PB.view(o_ws + i * 128, 128) for i in range(ND)]
        kvV = [PB.view(o_ws + 1024 + i * 128, 128) for i in range(ND)]
        bvrow = PF.view(o_brow, 128, 0, 1)

        def emit_kv(hp):
            c0 = hp * 128
            p.dma("gpsimd", lambda g, c0=c0: g.dma_start(
                PB.view(o_ws, ND * 128).rearrange("p (n o) -> p n o", o=128),
                wk_d.rearrange("(n p) o -> p n o", p=128)[:, :, c0:c0 + 128]),
                sem_w[0], writes=[b_ws], tag="wk")
            p.dma("gpsimd", lambda g, c0=c0: g.dma_start(
                PB.view(o_ws + 1024, ND * 128).rearrange("p (n o) -> p n o", o=128),
                wv_d.rearrange("(n p) o -> p n o", p=128)[:, :, c0:c0 + 128]),
                sem_w[1], writes=[b_wsV], tag="wv")
            p.dma("sync", lambda s, c0=c0: s.dma_start(
                bvrow, crow_d[0:1, c0:c0 + 128]), sem_bv, writes=[b_brow, b_big1], tag="bvrow")
            p.op("tensor", lambda t: t.matmul(
                psP1a[:, 0:128], onesf_t[0:1, :], bvrow,
                start=True, stop=True), reads=[bonesf, b_brow], writes=[bP1a], tag="bvMM")
            p.op("vector", lambda v: v.tensor_copy(PF.view(o_bv, 128), psP1a[:, 0:128]),
                 reads=[bP1a], writes=[b_bv, b_big1])
            for tc in range(NTC_F):
                ps, bps = slot()
                for i in range(ND):
                    p.op("tensor", lambda t, i=i, tc=tc, ps=ps: t.matmul(
                        ps, kvK[i][:, :], xln[i][:, tc * 512:(tc + 1) * 512],
                        start=(i == 0), stop=(i == ND - 1)),
                        reads=[b_ws, b_xln[i]], writes=[bps], war_free=(i > 0),
                        tag="kMM")
                p.op("scalar", lambda s, hp=hp, tc=tc, ps=ps: s.activation(
                    K_hp[:, tc * 512:(tc + 1) * 512], ps, AF.Identity,
                    bias=bks_t[:, hp:hp + 1], scale=0.125),
                    reads=[bps, bbks], writes=[b_k], tag="kevac")
            for kc in range(NKC):
                psh, lo, bps = _slots[_si[0] % len(_slots)]
                _si[0] += 1
                for i in range(ND):
                    p.op("tensor", lambda t, i=i, kc=kc, psh=psh, lo=lo: t.matmul(
                        psh[:, lo:lo + 128], xln[i][:, kc * 128:(kc + 1) * 128],
                        kvV[i][:, :], start=(i == 0), stop=(i == ND - 1)),
                        reads=[b_wsV, b_xln[i]], writes=[bps], war_free=(i > 0), tag="vMM")
                for h in range(2):
                    p.op("vector", lambda v, kc=kc, h=h, psh=psh, lo=lo: v.scalar_tensor_tensor(
                        V_hp[kc][:, h * 65:h * 65 + 64], psh[:, lo + h * 64:lo + h * 64 + 64],
                        1.0, PF.view(o_bv, 128)[:, h * 64:(h + 1) * 64],
                        op0=ALU.mult, op1=ALU.add),
                        reads=[bps, b_bv], writes=[b_v[kc]], tag="vevac")

        def pass1_steps(hp, qt):
            c = hp
            t0 = qt * 512
            gsl = qt % 2
            steps = []

            for qs in range(4):
                for seg in range(NSEG):
                    for h in range(2):
                        def th(qs=qs, h=h, seg=seg):
                            k = (qs * NSEG * 2 + seg * 2 + h) % 2
                            pa, pb = (psP1a, bP1a) if k == 0 else (psP1b, bP1b)
                            sc_, bsc_ = scpy[k], b_scpy[k]
                            p.op("tensor", lambda t, qs=qs, h=h, seg=seg: t.matmul(
                                pa[:, :],
                                q_s[c][h * 64:h * 64 + 64,
                                       t0 + qs * 128:t0 + (qs + 1) * 128],
                                K_hp[h * 64:h * 64 + 64, seg * 512:(seg + 1) * 512],
                                start=True, stop=True),
                                reads=[b_qs[c], b_k], writes=[pb], tag="s1MM")
                            col = o_cm + gsl * 64 + qs * 16 + h * 8 + seg
                            p.op("vector", lambda v, col=col: v.reduce_max(
                                PF.view(col, 1), pa[:, :], axis=AX.X, negate=True),
                                reads=[pb], writes=[b_cm[gsl], b_big1], tag="s1max")
                        steps.append(th)

                def th2(qs=qs):
                    gc0 = o_gc + gsl * 128
                    base = o_cm + gsl * 64 + qs * 16
                    for h in range(2):
                        col = gc0 + qs * 2 + h
                        p.op("vector", lambda v, h=h, col=col: v.tensor_reduce(
                            PF.view(col, 1), PF.view(base + h * 8, NSEG),
                            axis=AX.X, op=ALU.min),
                            reads=[b_cm[gsl]], writes=[b_gc[gsl], b_big1], tag="gmin")
                        p.dma("gpsimd", lambda g, h=h, col=col: g.dma_start(
                            PB.t[h * 64:h * 64 + 1,
                                 o_grow + gsl * 512 + qs * 128:
                                 o_grow + gsl * 512 + qs * 128 + 128],
                            PF.t[0:128, col:col + 1]),
                            sem_gg[gsl], reads=[b_gc[gsl]], writes=[b_grow[gsl]],
                            tag="grow")
                steps.append(th2)
            return steps

        def emit_pass2(hp, qt, nxt):
            c = hp
            t0 = qt * 512
            gsl = qt % 2

            def emit_pv(kc):
                par = kc % 3
                for h in range(2):
                    psv, bpv = (psPVa, bPVa) if h == 0 else (psPVb, bPVb)
                    p.op("tensor", lambda t, kc=kc, h=h, par=par, psv=psv: t.matmul(
                        psv[:, :], V_hp[kc][:, h * 65:h * 65 + 65],
                        PB.t[:, PTof(par) + h * 512:
                             PTof(par) + (h + 1) * 512],
                        start=(kc == 0), stop=(kc == NKC - 1)),
                        reads=[b_v[kc], b_pt[par]], writes=[bpv],
                        war_free=(kc > 0), tag="pvMM")

            g0 = o_grow + gsl * 512
            for kc in range(NKC):
                pst = psST0 if kc % 2 == 0 else psST1
                bsth = [bST[2 * (kc % 2)], bST[2 * (kc % 2) + 1]]
                for h in range(2):
                    p.op("tensor", lambda t, kc=kc, h=h, pst=pst: t.matmul(
                        pst[:, h * 512:(h + 1) * 512],
                        K_hp[h * 64:h * 64 + 64, kc * 128:(kc + 1) * 128],
                        q_s[c][h * 64:h * 64 + 64, t0:t0 + 512],
                        start=True, stop=False, tile_position=(h * 64, 0)),
                        reads=[b_k, b_qs[c]], writes=[bsth[h]], tag="stMM")
                for h in range(2):
                    p.op("tensor", lambda t, h=h, pst=pst: t.matmul(
                        pst[:, h * 512:(h + 1) * 512], neg_b[h * 64:h * 64 + 1, :],
                        PB.t[h * 64:h * 64 + 1, g0:g0 + 512],
                        start=False, stop=True, tile_position=(h * 64, 0)),
                        reads=[bnegb, b_grow[gsl]], writes=[bsth[h]], war_free=True,
                        tag="subMM")
                for _ in range(2):
                    if nxt:
                        nxt.pop(0)()
                if kc > 1:
                    emit_pv(kc - 2)
                par3 = kc % 3
                p.op("scalar", lambda s, par3=par3, pst=pst: s.activation(
                    PB.t[:, PTof(par3):PTof(par3) + 1024],
                    pst[:, :], AF.Exp), reads=bsth, writes=[b_pt[par3]], tag="exp")
            emit_pv(NKC - 2)
            emit_pv(NKC - 1)
            rd = PF.view(o_rd, 1024, 64, 65)
            p.op("vector", lambda v: v.reciprocal(rd[:, 0:512], psPVa[64:65, :]),
                 reads=[bPVa], writes=[b_rd, b_big1], tag="recipA")
            p.op("vector", lambda v: v.reciprocal(rd[:, 512:1024], psPVb[64:65, :]),
                 reads=[bPVb], writes=[b_rd], war_free=True, tag="recipB")
            p.op("tensor", lambda t: t.matmul(
                psP1a[0:64, :], onesf_t[64:65, 0:64], rd[:, 0:512],
                start=True, stop=True), reads=[bonesf, b_rd], writes=[bP1a], tag="rbA")
            p.op("tensor", lambda t: t.matmul(
                psP1b[64:128, :], onesf_t[64:65, 0:64], rd[:, 512:1024],
                start=True, stop=True, tile_position=(64, 64)),
                reads=[bonesf, b_rd], writes=[bP1b], tag="rbB")
            p.op("vector", lambda v: v.tensor_copy(PF.view(o_rb, 512, 0, 64),
                 psP1a[0:64, :]), reads=[bP1a], writes=[b_rb, b_big1])
            p.op("vector", lambda v: v.tensor_copy(PF.view(o_rb, 512, 64, 128),
                 psP1b[64:128, :]), reads=[bP1b], writes=[b_rb], war_free=True)
            for h in range(2):
                psv, bpv = (psPVa, bPVa) if h == 0 else (psPVb, bPVb)
                p.op("vector", lambda v, h=h, psv=psv: v.scalar_tensor_tensor(
                    PF.view(o_st + 1024, 512, h * 64, (h + 1) * 64), psv[0:64, :], 1.0,
                    PF.view(o_rb, 512, h * 64, (h + 1) * 64),
                    op0=ALU.mult, op1=ALU.mult),
                    reads=[bpv, b_rb], writes=[b_stats],
                    war_free=(h == 1), tag="numdiv")
                p.op("vector", lambda v, h=h: v.tensor_add(
                    ao[c][h * 64:(h + 1) * 64, t0:t0 + 512],
                    PF.view(o_st + 1024, 512, h * 64, (h + 1) * 64),
                    q_s[c][h * 64:(h + 1) * 64, t0:t0 + 512]),
                    reads=[b_stats, b_qs[c]], writes=[b_ao[c]],
                    war_free=(h == 1), tag="aoadd")

        for hp in range(NP if not cfg.get("skip_attn") else 0):
            emit_kv(hp)
            for th in pass1_steps(hp, 0):
                th()
            for qt in range(NQT):
                nxt = pass1_steps(hp, qt + 1) if qt + 1 < NQT else []
                emit_pass2(hp, qt, nxt)
                for th in nxt:
                    th()

        p.dma("gpsimd", lambda g: g.dma_start(
            wsl_v, wo_d.rearrange("(n p) o -> p n o", p=128)),
            sem_w[0], writes=[b_ws, b_wsV] + b_pt + b_grow, tag="wo")
        for tt in range(NTC_O if not cfg.get("skip_attn") else 0):
            t0 = tt * 512
            p.dma("sync", lambda s, t0=t0: s.dma_start(
                PF.view(o_big2, ND * 512).rearrange("p (n t) -> p n t", t=512),
                fm(xT, t0, t0 + 512)), sem_big, writes=[b_big2], tag="xown")
            for o in range(ND):
                ps, bps = slot()
                for i in range(ND):
                    p.op("tensor", lambda t, i=i, o=o, t0=t0, ps=ps: t.matmul(
                        ps, wq_sl[i][:, o * 128:(o + 1) * 128], ao[i][:, t0:t0 + 512],
                        start=(i == 0), stop=(i == ND - 1)),
                        reads=[b_ws, b_ao[i]], writes=[bps], war_free=(i > 0), tag="woMM")
                p.op("scalar", lambda s, o=o, ps=ps: s.activation(
                    big1[o], ps, AF.Identity, bias=cc("bo", o), scale=1.0),
                    reads=[bps, bccol], writes=[b_big1], tag="woevac")
            emit_stats(cast_f32([big1[o] for o in range(ND)], [b_big1]))
            for o in range(ND):
                a, ab = asc[o % 2], b_asc[o % 2]
                p.op("vector", lambda v, o=o, a=a: v.tensor_sub(a, big1[o], mean_b),
                     reads=[b_big1, b_stats], writes=[ab])
                p.op("vector", lambda v, a=a: v.tensor_mul(a, a, rstd_b),
                     reads=[ab, b_stats], writes=[ab])
                p.op("vector", lambda v, o=o, a=a: v.tensor_add(big2[o], big2[o], a),
                     reads=[ab, b_big2], writes=[b_big2], tag="x1add")
            p.dma("sync", lambda s, t0=t0: s.dma_start(
                fm(x1d, t0, t0 + 512),
                PF.view(o_big2, ND * 512).rearrange("p (n t) -> p n t", t=512)),
                sem_big, reads=[b_big2], writes=[b_x1d], tag="x1st")

        if cfg.get("stop_after_A"):
            p.barrier("sync", [Tok("dma", sem=sem_big, idx=sem_big.count)])
            p.finalize(block)
            nc._layout = dict(o_xln=o_xln, o_qs=o_qs, o_ao=o_ao, o_k=o_k, o_v=o_v,
                              o_sc=o_sc, o_ws=o_ws,
                              o_st=o_st, o_big1=o_big1, o_big2=o_big2)
            return nc

        a_bufs = (b_xln + b_qs + b_ao + [b_k, b_kb1] + b_v + b_pt + b_grow +
                  [b_ws, b_wsV, b_xbs[0], b_xbs[1]])
        fence = p.op("vector", lambda v: v.engine_nop(), writes=a_bufs, tag="fence")

        h0 = [PB.view(o_h0 + j * TB, TB) for j in range(NJ)]
        h1v = [PB.view(o_h1 + j * TB, TB) for j in range(NJ)]
        xln2 = [PB.view(o_xln2 + i * TB, TB) for i in range(ND)]
        b_h0 = [p.buf(f"h0_{j}") for j in range(NJ)]
        b_h1 = [p.buf(f"h1_{j}") for j in range(NJ)]
        b_xln2 = [p.buf(f"xln2_{i}") for i in range(ND)]
        fsl = [PB.view(o_ws + s * 2048, 2048) for s in range(4)]
        b_fsl = [p.buf(f"fsl{s}") for s in range(4)]
        for b in b_h0 + b_h1 + b_xln2 + b_fsl:
            b.last_writer = fence
        _slot_pool[0] = [2, 3, 4, 5]

        def _stop(bufs):
            t = p.op("vector", lambda v: v.engine_nop(), reads=bufs, tag="stopper")
            p.barrier("sync", [t, Tok("dma", sem=sem_big, idx=sem_big.count)])
            p.finalize(block)
            nc._layout = dict(o_xln=o_xln, o_qs=o_qs, o_ao=o_ao, o_k=o_k, o_v=o_v,
                              o_sc=o_sc, o_ws=o_ws,
                              o_h0=o_h0, o_h1=o_h1, o_xln2=o_xln2,
                              o_st=o_st, o_big1=o_big1, o_big2=o_big2)
            return nc

        def emit_ln2(ft):
            tb0 = ft * TB
            for sb in range(NSB if not cfg.get("skip_ln2") else 0):
                t0 = tb0 + sb * 512
                p.dma("sync", lambda s, t0=t0: s.dma_start(
                    PF.view(o_big2, ND * 512).rearrange("p (n t) -> p n t", t=512),
                    fm(x1d, t0, t0 + 512)), sem_big, reads=[b_x1d],
                    writes=[b_big2], tag="x1ld")
                emit_stats(cast_f32([big2[i] for i in range(ND)], [b_big2]))
                for i in range(ND):
                    a, ab = asc[i % 2], b_asc[i % 2]
                    p.op("vector", lambda v, i=i, a=a: v.tensor_sub(a, big2[i], mean_b),
                         reads=[b_big2, b_stats], writes=[ab])
                    p.op("vector", lambda v, i=i, sb=sb, a=a: v.tensor_mul(
                        xln2[i][:, sb * 512:(sb + 1) * 512], a, rstd_b),
                        reads=[ab, b_stats], writes=[b_xln2[i]], tag="ln2ap")

        emit_ln2(0)
        for ft in range(NFT):
            tb0 = ft * TB
            if cfg.get("stop_after_LN2"):
                return _stop(b_xln2)
            for jg in range(DH // 256 if not cfg.get("s1_one_jg") else 1):
                wi = jg % 2
                p.dma("gpsimd", lambda g, jg=jg, wi=wi: g.dma_start(
                    fsl[wi].rearrange("p (n o) -> p n o", o=256),
                    w1t_d.rearrange("(n p) o -> p n o", p=128)[:, :, jg * 256:(jg + 1) * 256]),
                    sem_w[wi], writes=[b_fsl[wi]], tag="w1l")
                p.dma("gpsimd", lambda g, jg=jg, wi=wi: g.dma_start(
                    fsl[2 + wi].rearrange("p (n o) -> p n o", o=256),
                    w2t_d.rearrange("(n p) o -> p n o", p=128)[:, :, jg * 256:(jg + 1) * 256]),
                    sem_w[2 + wi], writes=[b_fsl[2 + wi]], tag="w2l")
                for jj in range(2):
                    j = jg * 2 + jj
                    for sb in range(NSB):
                        psa, bpsa = slot()
                        psb2, bpsb2 = slot()
                        for i in range(ND):
                            p.op("tensor", lambda t, i=i, jj=jj, sb=sb, psa=psa, wi=wi: t.matmul(
                                psa, fsl[wi][:, i * 256 + jj * 128:i * 256 + (jj + 1) * 128],
                                xln2[i][:, sb * 512:(sb + 1) * 512],
                                start=(i == 0), stop=(i == ND - 1)),
                                reads=[b_fsl[wi], b_xln2[i]], writes=[bpsa],
                                war_free=(i > 0), tag="w1MM")
                        for i in range(ND):
                            p.op("tensor", lambda t, i=i, jj=jj, sb=sb, psb2=psb2, wi=wi: t.matmul(
                                psb2, fsl[2 + wi][:, i * 256 + jj * 128:i * 256 + (jj + 1) * 128],
                                xln2[i][:, sb * 512:(sb + 1) * 512],
                                start=(i == 0), stop=(i == ND - 1)),
                                reads=[b_fsl[2 + wi], b_xln2[i]], writes=[bpsb2],
                                war_free=(i > 0), tag="w2MM")
                        _af1 = AF.Identity if cfg.get("no_silu") else (AF.Gelu if cfg.get("silu_as_gelu") else (AF.Sigmoid if cfg.get("silu_as_sigmoid") else AF.Silu))
                        if cfg.get("one_silu") and (jg, jj, sb) != (0, 0, 0):
                            _af1 = AF.Identity
                        _sc1 = 0.5 if cfg.get("silu_const_scale") else cc("w1s", j)
                        if cfg.get("silu_from_sbuf"):
                            p.op("scalar", lambda s, psa=psa: s.activation(
                                sc16[2][:, :], psa, AF.Identity, bias=0.0, scale=1.0),
                                reads=[bpsa], writes=[b_sc16[2], b_kb1], tag="pre")
                            p.op("scalar", lambda s, j=j, _af1=_af1, _sc1=_sc1: s.activation(
                                sc16[0][:, :], sc16[2][:, :], _af1, bias=0.0, scale=_sc1),
                                reads=[b_sc16[2], bccol], writes=[b_sc16[0], b_kb1], tag="silu")
                        else:
                            p.op("scalar", lambda s, j=j, psa=psa, _af1=_af1, _sc1=_sc1: s.activation(
                                sc16[0][:, :], psa, _af1, bias=0.0, scale=_sc1),
                                reads=[bpsa, bccol], writes=[b_sc16[0], b_kb1], tag="silu")
                        _af2 = AF.Silu if cfg.get("w2_as_silu") else AF.Identity
                        p.op("scalar", lambda s, j=j, psb2=psb2, _af2=_af2: s.activation(
                            sc16[1][:, :], psb2, _af2, bias=0.0, scale=cc("w2s", j)),
                            reads=[bpsb2, bccol], writes=[b_sc16[1], b_kb1], tag="w2sc")
                        p.op("vector", lambda v, j=j, sb=sb: v.tensor_mul(
                            h0[j][:, sb * 512:(sb + 1) * 512], sc16[0][:, :], sc16[1][:, :]),
                            reads=[b_sc16[0], b_sc16[1]], writes=[b_h0[j]], tag="h0mul")
            if cfg.get("stop_after_S1"):
                return _stop(b_h0)
            if ft + 1 < NFT:
                emit_ln2(ft + 1)
            for (wt_d, scn, src, bsrc, dst, bdst) in (
                    (h1t_d, "h1s", h0, b_h0, h1v, b_h1),
                    (h2t_d, "h2s", h1v, b_h1, h0, b_h0)):
                for j2 in range(NJ):
                    wi = j2 % 2
                    for qd in range(4):
                        jq0 = qd * NJ // 4
                        p.dma("gpsimd", lambda g, j2=j2, wi=wi, wt_d=wt_d, jq0=jq0: g.dma_start(
                            PB.view(o_ws + wi * 4096 + jq0 * 128, NJ * 32).rearrange(
                                "p (n o) -> p n o", o=128),
                            wt_d.rearrange("(n p) o -> p n o", p=128)[
                                :, jq0:jq0 + NJ // 4, j2 * 128:(j2 + 1) * 128]),
                            sem_w[wi], writes=[b_fsl[2 * wi], b_fsl[2 * wi + 1]],
                            war_free=(qd > 0), tag="hwl")
                    for sb in range(NSB):
                        ps, bps = slot()
                        for j in range(NJ):
                            p.op("tensor", lambda t, j=j, sb=sb, ps=ps, wi=wi, src=src: t.matmul(
                                ps, PB.view(o_ws + wi * 4096 + j * 128, 128),
                                src[j][:, sb * 512:(sb + 1) * 512],
                                start=(j == 0), stop=(j == NJ - 1)),
                                reads=[b_fsl[2 * wi], b_fsl[2 * wi + 1], bsrc[j]],
                                writes=[bps], war_free=(j > 0), tag="hMM")
                        p.op("scalar", lambda s, j2=j2, sb=sb, ps=ps, scn=scn, dst=dst,
                             bdst=bdst: s.activation(
                            dst[j2][:, sb * 512:(sb + 1) * 512], ps, AF.Gelu,
                            bias=0.0, scale=cc(scn, j2)),
                            reads=[bps, bccol], writes=[bdst[j2]], tag="gelu")
            if cfg.get("stop_after_S3"):
                return _stop(b_h0 + b_h1)
            for sb in range(NSB):
                t0 = tb0 + sb * 512
                for o in range(ND):
                    wi = o % 2
                    for qd in range(4):
                        jq0 = qd * NJ // 4
                        p.dma("gpsimd", lambda g, o=o, wi=wi, jq0=jq0: g.dma_start(
                            PB.view(o_ws + wi * 4096 + jq0 * 128, NJ * 32).rearrange(
                                "p (n o2) -> p n o2", o2=128),
                            w3t_d.rearrange("(n p) o2 -> p n o2", p=128)[
                                :, jq0:jq0 + NJ // 4, o * 128:(o + 1) * 128]),
                            sem_w[wi], writes=[b_fsl[2 * wi], b_fsl[2 * wi + 1]],
                            war_free=(qd > 0), tag="w3l")
                    ps, bps = slot()
                    for j in range(NJ):
                        p.op("tensor", lambda t, j=j, sb=sb, ps=ps, wi=wi: t.matmul(
                            ps, PB.view(o_ws + wi * 4096 + j * 128, 128),
                            h0[j][:, sb * 512:(sb + 1) * 512],
                            start=(j == 0), stop=(j == NJ - 1)),
                            reads=[b_fsl[2 * wi], b_fsl[2 * wi + 1], b_h0[j]],
                            writes=[bps], war_free=(j > 0), tag="w3MM")
                    p.op("scalar", lambda s, o=o, ps=ps: s.activation(
                        big1[o], ps, AF.Identity, bias=0.0, scale=cc("w3s", o)),
                        reads=[bps, bccol], writes=[b_big1], tag="w3evac")
                p.dma("sync", lambda s, t0=t0: s.dma_start(
                    PF.view(o_big2, ND * 512).rearrange("p (n t) -> p n t", t=512),
                    fm(x1d, t0, t0 + 512)), sem_big, reads=[b_x1d],
                    writes=[b_big2], tag="x1ld2")
                emit_stats(cast_f32([big1[o] for o in range(ND)], [b_big1]))
                for o in range(ND):
                    a, ab = asc[o % 2], b_asc[o % 2]
                    p.op("vector", lambda v, o=o, a=a: v.tensor_sub(a, big1[o], mean_b),
                         reads=[b_big1, b_stats], writes=[ab])
                    p.op("vector", lambda v, a=a: v.tensor_mul(a, a, rstd_b),
                         reads=[ab, b_stats], writes=[ab])
                    p.op("vector", lambda v, o=o, a=a: v.tensor_add(big2[o], big2[o], a),
                         reads=[ab, b_big2], writes=[b_big2], tag="x2add")
                p.dma("sync", lambda s, t0=t0: s.dma_start(
                    fm(outT, t0, t0 + 512),
                    PF.view(o_big2, ND * 512).rearrange("p (n t) -> p n t", t=512)),
                    sem_st, reads=[b_big2], tag="outst")

        p.barrier("sync", [Tok("dma", sem=sem_st, idx=sem_st.count)])
        p.finalize(block)
    nc._layout = dict(o_xln=o_xln, o_qs=o_qs, o_ao=o_ao, o_k=o_k, o_v=o_v,
                      o_sc=o_sc, o_ws=o_ws,
                      o_h0=o_h0, o_h1=o_h1, o_xln2=o_xln2,
                      o_st=o_st, o_big1=o_big1, o_big2=o_big2)
    return nc


_CACHE = {}


def _get_nc(cfg_key, cfg):
    if cfg_key not in _CACHE:
        _CACHE[cfg_key] = build_program(cfg)
    return _CACHE[cfg_key]


def make_host_inputs(cfg, x, Wq, bq, Wk, bk, Wv, bv, Wo, bo, ln_g, ln_b,
                     w1_w, w1_s, w2_w, w2_s, h1_w, h1_s, h2_w, h2_s, w3_w, w3_s):
    D, To, Tf = cfg["D"], cfg["To"], cfg["Tf"]
    B = x.shape[0]
    f = np.float32
    ac = np.ascontiguousarray

    def col8(vec, n):
        return vec.reshape(n, 128).T.astype(f)

    ccol = np.zeros((128, NCC), f)
    ccol[:, CC["bq"]:CC["bq"] + 8] = col8(bq, 8)
    ccol[:, CC["bk"]:CC["bk"] + 8] = col8(bk, 8)
    ccol[:, CC["bo"]:CC["bo"] + 8] = col8(bo, 8)
    ccol[:, CC["lng"]:CC["lng"] + 8] = col8(ln_g, 8)
    ccol[:, CC["lnb"]:CC["lnb"] + 8] = col8(ln_b, 8)
    ccol[:, CC["w1s"]:CC["w1s"] + 32] = col8(w1_s, 32)
    ccol[:, CC["w2s"]:CC["w2s"] + 32] = col8(w2_s, 32)
    ccol[:, CC["h1s"]:CC["h1s"] + 32] = col8(h1_s, 32)
    ccol[:, CC["h2s"]:CC["h2s"] + 32] = col8(h2_s, 32)
    ccol[:, CC["w3s"]:CC["w3s"] + 8] = col8(w3_s, 8)
    ccol[:, CC["eps"]] = 1e-5
    crow = np.concatenate([bv.astype(f), np.full(128, -1.0, f)])[None, :]
    shared = dict(
        wq=ac(Wq.astype(f)), wk=ac(Wk.astype(f)), wv=ac(Wv.astype(f)),
        wo=ac(Wo.astype(f)),
        w1t=ac(w1_w.T.astype(f)), w2t=ac(w2_w.T.astype(f)),
        h1t=ac(h1_w.T.astype(f)), h2t=ac(h2_w.T.astype(f)),
        w3t=ac(w3_w.T.astype(f)),
        ccol=ccol, crow=ac(crow.astype(f)),
        cones=np.ones((128, 128), f),
    )
    in_maps = []
    for b in range(B):
        xt = ac(x[b].T.astype(f))
        for hh in range(2):
            if hh == 0:
                xcore = xt
            else:
                xcore = ac(np.concatenate([xt[:, To:], xt[:, :To]], axis=1))
            in_maps.append(dict(shared, xT=xcore))
    return in_maps


def kernel(L0_attn_x, L0_attn_Wq, L0_attn_bq, L0_attn_Wk, L0_attn_bk,
           L0_attn_Wv, L0_attn_bv, L0_attn_Wo, L0_attn_bo,
           L0_attn_ln_g, L0_attn_ln_b,
           L0_ffn_w1_w, L0_ffn_w1_s, L0_ffn_w2_w, L0_ffn_w2_s,
           L0_ffn_h1_w, L0_ffn_h1_s, L0_ffn_h2_w, L0_ffn_h2_s,
           L0_ffn_w3_w, L0_ffn_w3_s):
    cfg = default_cfg()
    x = np.asarray(L0_attn_x)
    B, S, D = x.shape
    To = cfg["To"]
    nc = _get_nc("full", cfg)
    in_maps = make_host_inputs(
        cfg, x, L0_attn_Wq, L0_attn_bq, L0_attn_Wk, L0_attn_bk,
        L0_attn_Wv, L0_attn_bv, L0_attn_Wo, L0_attn_bo,
        L0_attn_ln_g, L0_attn_ln_b,
        L0_ffn_w1_w, L0_ffn_w1_s, L0_ffn_w2_w, L0_ffn_w2_s,
        L0_ffn_h1_w, L0_ffn_h1_s, L0_ffn_h2_w, L0_ffn_h2_s,
        L0_ffn_w3_w, L0_ffn_w3_s)
    res = run_bass_kernel_spmd(nc, in_maps, core_ids=list(range(8)))
    out = np.empty((B, S, D), np.float32)
    for b in range(B):
        out[b, :To, :] = res.results[2 * b]["outT"].T
        out[b, To:, :] = res.results[2 * b + 1]["outT"].T
    return out


# version 15
# speedup vs baseline: 1.4584x; 1.0007x over previous
import sys
sys.path.insert(0, '/opt/trn_rl_repo')
from contextlib import ExitStack
import numpy as np
import concourse.bass as bass
import concourse.mybir as mybir
from concourse.bass_utils import run_bass_kernel_spmd

F32 = mybir.dt.float32
BF16 = mybir.dt.bfloat16
AF = mybir.ActivationFunctionType
ALU = mybir.AluOpType
AX = mybir.AxisListType

ENGINES = ("tensor", "vector", "scalar", "gpsimd", "sync")


class Buf:
    __slots__ = ("name", "last_writer", "readers", "psum")

    def __init__(self, name, psum=False):
        self.name = name
        self.last_writer = None
        self.readers = []
        self.psum = psum


class DmaSem:
    __slots__ = ("handle", "count", "name")

    def __init__(self, handle, name):
        self.handle = handle
        self.count = 0
        self.name = name


class Op:
    __slots__ = ("engine", "fn", "seq", "inc", "sem_val", "deps", "dma_sem", "tag")

    def __init__(self, engine, fn, seq, dma_sem, tag):
        self.engine = engine
        self.fn = fn
        self.seq = seq
        self.inc = dma_sem is not None
        self.sem_val = None
        self.deps = []
        self.dma_sem = dma_sem
        self.tag = tag


class Tok:
    __slots__ = ("kind", "engine", "idx", "sem")

    def __init__(self, kind, engine=None, idx=None, sem=None):
        self.kind = kind
        self.engine = engine
        self.idx = idx
        self.sem = sem


class Prog:
    def __init__(self, nc, stack):
        self.nc = nc
        self.stack = stack
        self.ops = {e: [] for e in ENGINES}
        self.sems = {e: stack.enter_context(nc.semaphore(f"pg_{e}")) for e in ENGINES}
        self.n_bufs = 0
        self.n_dsems = 0

    def buf(self, name=None, psum=False):
        self.n_bufs += 1
        return Buf(name or f"b{self.n_bufs}", psum)

    def dma_sem(self, name=None):
        self.n_dsems += 1
        name = name or f"ds{self.n_dsems}"
        return DmaSem(self.stack.enter_context(self.nc.semaphore(name)), name)

    def _collect_deps(self, o, tok, reads, writes, war_free):
        deps = []

        def add(d):
            if d is tok:
                return
            if d.kind == "eng":
                deps.append(("eng", d.engine, d.idx))
            else:
                deps.append(("dma", d.sem, 16 * d.sem.count))

        for b in reads:
            if b.last_writer is not None:
                add(b.last_writer)
            if b.psum:
                for r in b.readers:
                    if r.kind == "eng" and r.engine != tok.engine:
                        add(r)
            b.readers.append(tok)
        for b in writes:
            if not war_free:
                if b.last_writer is not None:
                    add(b.last_writer)
                for r in b.readers:
                    add(r)
            b.readers = []
            b.last_writer = tok
        o.deps = deps

    def op(self, engine, fn, reads=(), writes=(), war_free=False, tag=""):
        seq = len(self.ops[engine])
        tok = Tok("eng", engine=engine, idx=seq)
        o = Op(engine, fn, seq, None, tag)
        self._collect_deps(o, tok, reads, writes, war_free)
        self.ops[engine].append(o)
        return tok

    def dma(self, qengine, fn, sem, reads=(), writes=(), war_free=False, tag=""):
        seq = len(self.ops[qengine])
        tok = Tok("dma", engine=qengine, idx=None, sem=sem)
        o = Op(qengine, fn, seq, sem, tag)
        self._collect_deps(o, tok, reads, writes, war_free)
        sem.count += 1
        tok.idx = sem.count
        self.ops[qengine].append(o)
        return tok

    def barrier(self, engine, toks, tag="barrier"):
        seq = len(self.ops[engine])
        o = Op(engine, lambda eng: eng.nop(), seq, None, tag)
        deps = []
        for d in toks:
            if d.kind == "eng":
                deps.append(("eng", d.engine, d.idx))
            else:
                deps.append(("dma", d.sem, 16 * d.sem.count))
        o.deps = deps
        self.ops[engine].append(o)
        return Tok("eng", engine=engine, idx=seq)

    def _resolve(self):
        for e in ENGINES:
            for o in self.ops[e]:
                kept = []
                for d in o.deps:
                    if d[0] == "eng":
                        if d[1] == "tensor" and e == "tensor":
                            continue
                        self.ops[d[1]][d[2]].inc = True
                    kept.append(d)
                o.deps = kept
        for e in ENGINES:
            v = 0
            for o in self.ops[e]:
                if o.dma_sem is not None:
                    continue
                if o.inc:
                    v += 1
                    o.sem_val = v

    def finalize(self, block):
        self._resolve()
        prog = self

        def section(e):
            def body(eng):
                watermark = {}
                for o in prog.ops[e]:
                    for d in o.deps:
                        if d[0] == "eng":
                            sem = prog.sems[d[1]]
                            val = prog.ops[d[1]][d[2]].sem_val
                            assert val is not None, (d, o.tag)
                        else:
                            sem = d[1].handle
                            val = d[2]
                        key = id(sem)
                        if watermark.get(key, -1) >= val:
                            continue
                        watermark[key] = val
                        eng.wait_ge(sem, val)
                    inst = o.fn(eng)
                    if o.dma_sem is not None:
                        inst.then_inc(o.dma_sem.handle, 16)
                    elif o.inc:
                        inst.then_inc(prog.sems[e], 1)
            return body

        for e in ENGINES:
            if not self.ops[e]:
                continue
            getattr(block, e)(section(e))


class Pool:

    def __init__(self, nc, stack, name, dtype, elems):
        self.t = stack.enter_context(nc.sbuf_tensor(name, [128, elems], dtype))
        self.elems = elems
        self.off = 0

    def alloc(self, n, align=64):
        o = -(-self.off // align) * align
        assert o + n <= self.elems, f"pool overflow: {o}+{n} > {self.elems}"
        self.off = o + n
        return o

    def view(self, off, n, p0=0, p1=128):
        return self.t[p0:p1, off:off + n]


CC = {}
_o = 0
for _nm, _n in [("bq", 8), ("bk", 8), ("bo", 8), ("lng", 8), ("lnb", 8),
                ("w1s", 32), ("w2s", 32), ("h1s", 32), ("h2s", 32), ("w3s", 8),
                ("eps", 1)]:
    CC[_nm] = _o
    _o += _n
NCC = _o


def default_cfg():
    return dict(D=1024, H=16, DH=4096, To=2048, Tf=4096)


def build_program(cfg):
    D, H, DH, To, Tf = cfg["D"], cfg["H"], cfg["DH"], cfg["To"], cfg["Tf"]
    ND = D // 128
    NP = H // 2
    NJ = DH // 128
    NKC = Tf // 128
    NSEG = Tf // 512
    NQT = To // 512
    NTC_F = Tf // 512
    NTC_O = To // 512
    TB = min(1024, To)
    NFT = To // TB
    NSB = TB // 512
    assert NP == ND and D == 1024

    nc = bass.Bass("TRN2", target_bir_lowering=False, debug=False,
                   dynamic_dma_scratch_size=16384)
    di = lambda n, s: nc.dram_tensor(n, s, F32, kind="ExternalInput").ap()
    xT = di("xT", [D, Tf])
    wq_d, wk_d, wv_d, wo_d = di("wq", [D, D]), di("wk", [D, D]), di("wv", [D, D]), di("wo", [D, D])
    w1t_d, w2t_d = di("w1t", [D, DH]), di("w2t", [D, DH])
    h1t_d, h2t_d = di("h1t", [DH, DH]), di("h2t", [DH, DH])
    w3t_d = di("w3t", [DH, D])
    ccol_d = di("ccol", [128, NCC])
    crow_d = di("crow", [1, D + 128])
    cones_d = di("cones", [128, 128])
    x1d = nc.dram_tensor("x1d", [D, To], F32, kind="Internal").ap()
    outT = nc.dram_tensor("outT", [D, To], F32, kind="ExternalOutput").ap()

    def fm(dram_ap, t0, t1, n_last=None):
        r = dram_ap.rearrange("(n p) t -> p n t", p=128)
        return r[:, :, t0:t1]

    stack = ExitStack()
    with stack:
        E = stack.enter_context
        p = Prog(nc, stack)

        _o_sc = 0
        _o_ws = _o_sc + 4 * 512
        _main = _o_ws + 2 * ND * 512
        _a_end = _main + ND * Tf + 2 * ND * To + Tf + NKC * 130
        _b_end = _main + 2 * NJ * TB + ND * TB
        pb_elems = max(_a_end, _b_end) + 64
        pf_elems = (3 * 512 + 2 * ND * 512)
        PB = Pool(nc, stack, "PB", BF16, pb_elems)
        PF = Pool(nc, stack, "PF", F32, pf_elems)

        ccol_t = E(nc.sbuf_tensor("ccol_t", [128, NCC], F32))
        onesf_t = E(nc.sbuf_tensor("onesf_t", [128, 128], F32))
        ones_b = E(nc.sbuf_tensor("ones_b", [128, 128], BF16))
        neg_b = E(nc.sbuf_tensor("neg_b", [128, 128], BF16))
        bks_t = E(nc.sbuf_tensor("bks_t", [128, 8], F32))

        psP1a = E(nc.psum_tensor("psP1a", [128, 512], F32))
        psP1b = E(nc.psum_tensor("psP1b", [128, 512], F32))
        psST0 = E(nc.psum_tensor("psST0", [128, 1024], F32))
        psST1 = E(nc.psum_tensor("psST1", [128, 1024], F32))
        psPVa = E(nc.psum_tensor("psPVa", [65, 512], F32))
        psPVb = E(nc.psum_tensor("psPVb", [65, 512], F32))
        block = E(nc.Block())

        bP1a, bP1b = p.buf("psP1a", True), p.buf("psP1b", True)
        bST = [p.buf(f"psST{i}", True) for i in range(4)]
        bPVa, bPVb = p.buf("psPVa", True), p.buf("psPVb", True)
        _slots = [(psP1a, 0, bP1a), (psP1b, 0, bP1b),
                  (psST0, 0, bST[0]), (psST0, 512, bST[1]),
                  (psST1, 0, bST[2]), (psST1, 512, bST[3])]
        _si = [0]

        _slot_pool = [list(range(6))]

        def slot():
            pool = _slot_pool[0]
            h, lo, b = _slots[pool[_si[0] % len(pool)]]
            _si[0] += 1
            return h[:, lo:lo + 512], b

        bccol, bonesf, bonesb, bnegb, bbks = (p.buf(x) for x in
            ["ccol", "onesf", "onesb", "negb", "bks"])

        sem_c = p.dma_sem("c")
        sem_cg = p.dma_sem("cg")
        sem_x = [p.dma_sem("x0"), p.dma_sem("x1")]
        sem_w = [p.dma_sem(f"w{i}") for i in range(4)]
        sem_bv = p.dma_sem("bv")
        sem_qg = [p.dma_sem("qg0"), p.dma_sem("qg1")]
        sem_gg = [p.dma_sem("gg0"), p.dma_sem("gg1")]
        sem_big = p.dma_sem("big")
        sem_st = p.dma_sem("st")

        p.dma("sync", lambda s: s.dma_start(ccol_t[:, :], ccol_d), sem_c, writes=[bccol])
        p.dma("sync", lambda s: s.dma_start(onesf_t[:, :], cones_d), sem_c, writes=[bonesf])
        p.dma("gpsimd", lambda g: g.dma_start(ones_b[:, :], cones_d), sem_cg, writes=[bonesb])
        p.dma("gpsimd", lambda g: g.dma_start(neg_b[0:1, :], crow_d[0:1, D:D + 128]),
              sem_cg, writes=[bnegb])
        p.dma("gpsimd", lambda g: g.dma_start(neg_b[64:65, :], crow_d[0:1, D:D + 128]),
              sem_cg, writes=[bnegb], war_free=True)
        p.op("vector", lambda v: v.tensor_scalar_mul(
            bks_t[:, :], ccol_t[:, CC["bk"]:CC["bk"] + 8], 0.125),
            reads=[bccol], writes=[bbks])

        def cc(nm, i=0):
            return ccol_t[:, CC[nm] + i:CC[nm] + i + 1]

        eps_ap = ccol_t[:, CC["eps"]:CC["eps"] + 1]

        o_sc = _o_sc
        o_ws = _o_ws
        o_xln = _main
        o_qs = o_xln + ND * Tf
        o_ao = o_qs + ND * To
        o_k = o_ao + ND * To
        o_v = o_k + Tf
        o_pt_m = _o_ws + 5120
        o_scpy = _o_ws + 2048
        o_pt0 = _o_ws + 3072
        o_grow = _o_ws + 4096
        assert Tf <= 4096
        o_h0 = _main
        o_h1 = o_h0 + NJ * TB
        o_xln2 = o_h1 + NJ * TB

        xln = [PB.view(o_xln + i * Tf, Tf) for i in range(ND)]
        q_s = [PB.view(o_qs + i * To, To) for i in range(ND)]
        ao = [PB.view(o_ao + i * To, To) for i in range(ND)]
        K_hp = PB.view(o_k, Tf)
        V_hp = [PB.view(o_v + k * 130, 130) for k in range(NKC)]
        sc16 = [PB.view(o_sc + i * 512, 512) for i in range(4)]
        b_xln = [p.buf(f"xln{i}") for i in range(ND)]
        b_qs = [p.buf(f"qs{i}") for i in range(ND)]
        b_ao = [p.buf(f"ao{i}") for i in range(ND)]
        b_k = p.buf("K_hp")
        b_v = [p.buf(f"V{k}") for k in range(NKC)]
        b_pt = [p.buf("pt0"), p.buf("pt1"), p.buf("pt2")]

        def PTof(par):
            return o_pt0 if par == 0 else o_pt_m + (par - 1) * 1024
        scpy = [PB.view(o_scpy + i * 512, 512) for i in range(2)]
        b_scpy = [p.buf("scpy0"), p.buf("scpy1")]
        b_grow = [p.buf("grow0"), p.buf("grow1")]
        b_kb1 = p.buf("kb1")
        b_sc16 = [p.buf(f"sc16_{i}") for i in range(4)]
        b_ws = p.buf("wslab")
        b_wsV = p.buf("wslabV")
        b_xbs = [p.buf("xb0"), p.buf("xb1")]

        o_st = PF.alloc(3 * 512)
        mean_b = PF.view(o_st, 512)
        rstd_b = PF.view(o_st + 512, 512)
        s1f = PF.view(o_st + 1024, 512)
        b_stats = p.buf("stats")
        asc = [s1f, s1f]
        b_asc = [b_stats, b_stats]

        o_big1 = PF.alloc(ND * 512)
        o_rd = o_big1
        o_rb = o_big1 + 1024
        o_bv = o_big1 + 1536
        o_brow = o_big1 + 1664
        o_cm = o_big1 + 1792
        o_gc = o_big1 + 1920
        o_big2 = PF.alloc(ND * 512)
        big1 = [PF.view(o_big1 + i * 512, 512) for i in range(ND)]
        big2 = [PF.view(o_big2 + i * 512, 512) for i in range(ND)]
        b_stats = p.buf("stats")
        b_cm = [p.buf("cm0"), p.buf("cm1")]
        b_gc = [p.buf("gc0"), p.buf("gc1")]
        b_rd, b_rb, b_bv, b_brow = p.buf("rd"), p.buf("rb"), p.buf("bv"), p.buf("brow")
        b_big1, b_big2 = p.buf("big1"), p.buf("big2")
        b_x1d = p.buf("x1d_dram")

        def emit_stats(bf_chunks):
            n = len(bf_chunks)
            for i, (kind, xap, xbufs) in enumerate(bf_chunks):
                if kind == "f32":
                    dst, db = sc16[i % 2], b_sc16[i % 2]
                    p.op("vector", lambda v, xap=xap, dst=dst: v.tensor_copy(
                        dst[:, :], xap), reads=list(xbufs), writes=[db, b_kb1], tag="cast")
                    xap, xbufs = dst[:, :], [db]
                sq, bsq = sc16[2 + (i % 2)], b_sc16[2 + (i % 2)]
                p.op("vector", lambda v, xap=xap, sq=sq: v.tensor_mul(sq[:, :], xap, xap),
                     reads=list(xbufs), writes=[bsq, b_kb1], tag="sq")
                p.op("tensor", lambda t, xap=xap, i=i, n=n: t.matmul(
                    psP1a[:, :], ones_b[:, :], xap, start=(i == 0), stop=(i == n - 1)),
                    reads=[bonesb] + list(xbufs), writes=[bP1a], war_free=(i > 0), tag="sMM")
                p.op("tensor", lambda t, sq=sq, i=i, n=n: t.matmul(
                    psP1b[:, :], ones_b[:, :], sq[:, :], start=(i == 0), stop=(i == n - 1)),
                    reads=[bonesb, bsq], writes=[bP1b], war_free=(i > 0), tag="qMM")
            inv = 1.0 / (n * 128)
            p.op("vector", lambda v: v.tensor_scalar_mul(mean_b, psP1a[:, :], inv),
                 reads=[bP1a], writes=[b_stats])
            p.op("vector", lambda v: v.tensor_scalar_mul(s1f, psP1b[:, :], inv),
                 reads=[bP1b], writes=[b_stats])
            p.op("vector", lambda v: v.scalar_tensor_tensor(
                rstd_b, mean_b, 1.0, mean_b, op0=ALU.mult, op1=ALU.mult),
                reads=[b_stats], writes=[b_stats])
            p.op("vector", lambda v: v.tensor_sub(s1f, s1f, rstd_b),
                 reads=[b_stats], writes=[b_stats])
            p.op("scalar", lambda s: s.activation(s1f, s1f, AF.Sqrt, bias=eps_ap, scale=1.0),
                 reads=[b_stats, bccol], writes=[b_stats])
            p.op("vector", lambda v: v.reciprocal(rstd_b, s1f),
                 reads=[b_stats], writes=[b_stats])

        def cast_f32(srcs, bufs_read):
            return [("f32", ap, list(bufs_read)) for ap in srcs]

        xb = [[PB.view(o_ws + sl * ND * 512 + i * 512, 512) for i in range(ND)]
              for sl in range(2)]
        for tc in range(NTC_F if not cfg.get("skip_a1") else 0):
            sl = tc % 2
            dst = PB.view(o_ws + sl * ND * 512, ND * 512).rearrange(
                "p (n t) -> p n t", t=512)
            p.dma("gpsimd", lambda g, tc=tc, dst=dst: g.dma_start(
                dst, fm(xT, tc * 512, (tc + 1) * 512)),
                sem_x[sl], writes=[b_xbs[sl]], tag="xload")
            emit_stats([("bf", xb[sl][i][:, :], [b_xbs[sl]]) for i in range(ND)])
            for i in range(ND):
                a, ab = asc[i % 2], b_asc[i % 2]
                p.op("vector", lambda v, i=i, sl=sl, a=a: v.tensor_sub(
                    a, xb[sl][i][:, :], mean_b), reads=[b_xbs[sl], b_stats], writes=[ab])
                p.op("vector", lambda v, i=i, tc=tc, a=a: v.tensor_mul(
                    xln[i][:, tc * 512:(tc + 1) * 512], a, rstd_b),
                    reads=[ab, b_stats], writes=[b_xln[i]], tag="ln1ap")

        wsl_v = PB.view(o_ws, ND * 1024).rearrange("p (n o) -> p n o", o=1024)
        wq_sl = [PB.view(o_ws + i * 1024, 1024) for i in range(ND)]
        p.dma("gpsimd", lambda g: g.dma_start(
            wsl_v, wq_d.rearrange("(n p) o -> p n o", p=128)),
            sem_w[0], writes=[b_ws, b_wsV, b_xbs[0], b_xbs[1]] + b_pt + b_grow, tag="wq")
        for tt in range(NQT if not cfg.get("skip_a2") else 0):
            t0 = tt * 512
            for o in range(ND):
                ps, bps = slot()
                for i in range(ND):
                    p.op("tensor", lambda t, i=i, o=o, t0=t0, ps=ps: t.matmul(
                        ps, wq_sl[i][:, o * 128:(o + 1) * 128],
                        xln[i][:, t0:t0 + 512], start=(i == 0), stop=(i == ND - 1)),
                        reads=[b_ws, b_xln[i]], writes=[bps], war_free=(i > 0), tag="qpMM")
                p.op("scalar", lambda s, o=o, ps=ps: s.activation(
                    big1[o], ps, AF.Identity, bias=cc("bq", o), scale=1.0),
                    reads=[bps, bccol], writes=[b_big1], tag="qevac")
            emit_stats(cast_f32([big1[o] for o in range(ND)], [b_big1]))
            for o in range(ND):
                a, ab = asc[o % 2], b_asc[o % 2]
                p.op("vector", lambda v, o=o, a=a: v.tensor_sub(a, big1[o], mean_b),
                     reads=[b_big1, b_stats], writes=[ab])
                p.op("vector", lambda v, a=a: v.tensor_mul(a, a, rstd_b),
                     reads=[ab, b_stats], writes=[ab])
                p.op("scalar", lambda s, o=o, t0=t0, a=a: s.activation(
                    q_s[o][:, t0:t0 + 512], a, AF.Identity,
                    bias=cc("lnb", o), scale=cc("lng", o)),
                    reads=[ab, bccol], writes=[b_qs[o]], tag="qln")

        if cfg.get("stop_after_A2"):
            p.barrier("sync", [Tok("dma", sem=sem_c, idx=sem_c.count)])
            p.finalize(block)
            nc._layout = dict(o_xln=o_xln, o_qs=o_qs, o_ao=o_ao, o_k=o_k, o_v=o_v,
                              o_sc=o_sc, o_ws=o_ws,
                              o_st=o_st, o_big1=o_big1, o_big2=o_big2)
            return nc

        vvíew = PB.t[:, o_v:o_v + NKC * 130].rearrange("p (k c) -> p k c", c=130)
        p.op("vector", lambda v: v.memset(vvíew[:, :, 64:65], 1.0),
             writes=list(b_v), tag="vones")
        p.op("vector", lambda v: v.memset(vvíew[:, :, 129:130], 1.0),
             writes=list(b_v), war_free=True, tag="vones")

        p.op("vector", lambda v: v.memset(PF.view(o_gc, 256), 0.0),
             writes=[b_gc[0], b_gc[1], b_big1], tag="gcinit")
        kvK = [PB.view(o_ws + i * 128, 128) for i in range(ND)]
        kvV = [PB.view(o_ws + 1024 + i * 128, 128) for i in range(ND)]
        bvrow = PF.view(o_brow, 128, 0, 1)

        def emit_kv(hp):
            c0 = hp * 128
            p.dma("gpsimd", lambda g, c0=c0: g.dma_start(
                PB.view(o_ws, ND * 128).rearrange("p (n o) -> p n o", o=128),
                wk_d.rearrange("(n p) o -> p n o", p=128)[:, :, c0:c0 + 128]),
                sem_w[0], writes=[b_ws], tag="wk")
            p.dma("gpsimd", lambda g, c0=c0: g.dma_start(
                PB.view(o_ws + 1024, ND * 128).rearrange("p (n o) -> p n o", o=128),
                wv_d.rearrange("(n p) o -> p n o", p=128)[:, :, c0:c0 + 128]),
                sem_w[1], writes=[b_wsV], tag="wv")
            p.dma("sync", lambda s, c0=c0: s.dma_start(
                bvrow, crow_d[0:1, c0:c0 + 128]), sem_bv, writes=[b_brow, b_big1], tag="bvrow")
            p.op("tensor", lambda t: t.matmul(
                psP1a[:, 0:128], onesf_t[0:1, :], bvrow,
                start=True, stop=True), reads=[bonesf, b_brow], writes=[bP1a], tag="bvMM")
            p.op("vector", lambda v: v.tensor_copy(PF.view(o_bv, 128), psP1a[:, 0:128]),
                 reads=[bP1a], writes=[b_bv, b_big1])
            for tc in range(NTC_F):
                ps, bps = slot()
                for i in range(ND):
                    p.op("tensor", lambda t, i=i, tc=tc, ps=ps: t.matmul(
                        ps, kvK[i][:, :], xln[i][:, tc * 512:(tc + 1) * 512],
                        start=(i == 0), stop=(i == ND - 1)),
                        reads=[b_ws, b_xln[i]], writes=[bps], war_free=(i > 0),
                        tag="kMM")
                p.op("scalar", lambda s, hp=hp, tc=tc, ps=ps: s.activation(
                    K_hp[:, tc * 512:(tc + 1) * 512], ps, AF.Identity,
                    bias=bks_t[:, hp:hp + 1], scale=0.125),
                    reads=[bps, bbks], writes=[b_k], tag="kevac")
            for kc in range(NKC):
                psh, lo, bps = _slots[_si[0] % len(_slots)]
                _si[0] += 1
                for i in range(ND):
                    p.op("tensor", lambda t, i=i, kc=kc, psh=psh, lo=lo: t.matmul(
                        psh[:, lo:lo + 128], xln[i][:, kc * 128:(kc + 1) * 128],
                        kvV[i][:, :], start=(i == 0), stop=(i == ND - 1)),
                        reads=[b_wsV, b_xln[i]], writes=[bps], war_free=(i > 0), tag="vMM")
                for h in range(2):
                    p.op("vector", lambda v, kc=kc, h=h, psh=psh, lo=lo: v.scalar_tensor_tensor(
                        V_hp[kc][:, h * 65:h * 65 + 64], psh[:, lo + h * 64:lo + h * 64 + 64],
                        1.0, PF.view(o_bv, 128)[:, h * 64:(h + 1) * 64],
                        op0=ALU.mult, op1=ALU.add),
                        reads=[bps, b_bv], writes=[b_v[kc]], tag="vevac")

        def pass1_steps(hp, qt):
            c = hp
            t0 = qt * 512
            gsl = qt % 2
            steps = []

            for qs in range(4):
                for seg in range(NSEG):
                    for h in range(2):
                        def th(qs=qs, h=h, seg=seg):
                            k = (qs * NSEG * 2 + seg * 2 + h) % 2
                            pa, pb = (psP1a, bP1a) if k == 0 else (psP1b, bP1b)
                            sc_, bsc_ = scpy[k], b_scpy[k]
                            p.op("tensor", lambda t, qs=qs, h=h, seg=seg: t.matmul(
                                pa[:, :],
                                q_s[c][h * 64:h * 64 + 64,
                                       t0 + qs * 128:t0 + (qs + 1) * 128],
                                K_hp[h * 64:h * 64 + 64, seg * 512:(seg + 1) * 512],
                                start=True, stop=True),
                                reads=[b_qs[c], b_k], writes=[pb], tag="s1MM")
                            col = o_cm + gsl * 64 + qs * 16 + h * 8 + seg
                            p.op("vector", lambda v, col=col: v.reduce_max(
                                PF.view(col, 1), pa[:, :], axis=AX.X, negate=True),
                                reads=[pb], writes=[b_cm[gsl], b_big1], tag="s1max")
                        steps.append(th)

                def th2(qs=qs):
                    gc0 = o_gc + gsl * 128
                    base = o_cm + gsl * 64 + qs * 16
                    for h in range(2):
                        col = gc0 + qs * 2 + h
                        p.op("vector", lambda v, h=h, col=col: v.tensor_reduce(
                            PF.view(col, 1), PF.view(base + h * 8, NSEG),
                            axis=AX.X, op=ALU.min),
                            reads=[b_cm[gsl]], writes=[b_gc[gsl], b_big1], tag="gmin")
                        p.dma("gpsimd", lambda g, h=h, col=col: g.dma_start(
                            PB.t[h * 64:h * 64 + 1,
                                 o_grow + gsl * 512 + qs * 128:
                                 o_grow + gsl * 512 + qs * 128 + 128],
                            PF.t[0:128, col:col + 1]),
                            sem_gg[gsl], reads=[b_gc[gsl]], writes=[b_grow[gsl]],
                            war_free=(qs != 0 or h != 0), tag="grow")
                steps.append(th2)
            return steps

        def emit_pass2(hp, qt, nxt):
            c = hp
            t0 = qt * 512
            gsl = qt % 2

            def emit_pv(kc):
                par = kc % 3
                for h in range(2):
                    psv, bpv = (psPVa, bPVa) if h == 0 else (psPVb, bPVb)
                    p.op("tensor", lambda t, kc=kc, h=h, par=par, psv=psv: t.matmul(
                        psv[:, :], V_hp[kc][:, h * 65:h * 65 + 65],
                        PB.t[:, PTof(par) + h * 512:
                             PTof(par) + (h + 1) * 512],
                        start=(kc == 0), stop=(kc == NKC - 1)),
                        reads=[b_v[kc], b_pt[par]], writes=[bpv],
                        war_free=(kc > 0), tag="pvMM")

            g0 = o_grow + gsl * 512
            for kc in range(NKC):
                pst = psST0 if kc % 2 == 0 else psST1
                bsth = [bST[2 * (kc % 2)], bST[2 * (kc % 2) + 1]]
                for h in range(2):
                    p.op("tensor", lambda t, kc=kc, h=h, pst=pst: t.matmul(
                        pst[:, h * 512:(h + 1) * 512],
                        K_hp[h * 64:h * 64 + 64, kc * 128:(kc + 1) * 128],
                        q_s[c][h * 64:h * 64 + 64, t0:t0 + 512],
                        start=True, stop=False, tile_position=(h * 64, 0)),
                        reads=[b_k, b_qs[c]], writes=[bsth[h]], tag="stMM")
                for h in range(2):
                    p.op("tensor", lambda t, h=h, pst=pst: t.matmul(
                        pst[:, h * 512:(h + 1) * 512], neg_b[h * 64:h * 64 + 1, :],
                        PB.t[h * 64:h * 64 + 1, g0:g0 + 512],
                        start=False, stop=True, tile_position=(h * 64, 0)),
                        reads=[bnegb, b_grow[gsl]], writes=[bsth[h]], war_free=True,
                        tag="subMM")
                for _ in range(2):
                    if nxt:
                        nxt.pop(0)()
                if kc > 1:
                    emit_pv(kc - 2)
                par3 = kc % 3
                p.op("scalar", lambda s, par3=par3, pst=pst: s.activation(
                    PB.t[:, PTof(par3):PTof(par3) + 1024],
                    pst[:, :], AF.Exp), reads=bsth, writes=[b_pt[par3]], tag="exp")
            emit_pv(NKC - 2)
            emit_pv(NKC - 1)
            rd = PF.view(o_rd, 1024, 64, 65)
            p.op("vector", lambda v: v.reciprocal(rd[:, 0:512], psPVa[64:65, :]),
                 reads=[bPVa], writes=[b_rd, b_big1], tag="recipA")
            p.op("vector", lambda v: v.reciprocal(rd[:, 512:1024], psPVb[64:65, :]),
                 reads=[bPVb], writes=[b_rd], war_free=True, tag="recipB")
            p.op("tensor", lambda t: t.matmul(
                psP1a[0:64, :], onesf_t[64:65, 0:64], rd[:, 0:512],
                start=True, stop=True), reads=[bonesf, b_rd], writes=[bP1a], tag="rbA")
            p.op("tensor", lambda t: t.matmul(
                psP1b[64:128, :], onesf_t[64:65, 0:64], rd[:, 512:1024],
                start=True, stop=True, tile_position=(64, 64)),
                reads=[bonesf, b_rd], writes=[bP1b], tag="rbB")
            p.op("vector", lambda v: v.tensor_copy(PF.view(o_rb, 512, 0, 64),
                 psP1a[0:64, :]), reads=[bP1a], writes=[b_rb, b_big1])
            p.op("vector", lambda v: v.tensor_copy(PF.view(o_rb, 512, 64, 128),
                 psP1b[64:128, :]), reads=[bP1b], writes=[b_rb], war_free=True)
            for h in range(2):
                psv, bpv = (psPVa, bPVa) if h == 0 else (psPVb, bPVb)
                p.op("vector", lambda v, h=h, psv=psv: v.scalar_tensor_tensor(
                    PF.view(o_st + 1024, 512, h * 64, (h + 1) * 64), psv[0:64, :], 1.0,
                    PF.view(o_rb, 512, h * 64, (h + 1) * 64),
                    op0=ALU.mult, op1=ALU.mult),
                    reads=[bpv, b_rb], writes=[b_stats],
                    war_free=(h == 1), tag="numdiv")
                p.op("vector", lambda v, h=h: v.tensor_add(
                    ao[c][h * 64:(h + 1) * 64, t0:t0 + 512],
                    PF.view(o_st + 1024, 512, h * 64, (h + 1) * 64),
                    q_s[c][h * 64:(h + 1) * 64, t0:t0 + 512]),
                    reads=[b_stats, b_qs[c]], writes=[b_ao[c]],
                    war_free=(h == 1), tag="aoadd")

        for hp in range(NP if not cfg.get("skip_attn") else 0):
            emit_kv(hp)
            for th in pass1_steps(hp, 0):
                th()
            for qt in range(NQT):
                nxt = pass1_steps(hp, qt + 1) if qt + 1 < NQT else []
                emit_pass2(hp, qt, nxt)
                for th in nxt:
                    th()

        p.dma("gpsimd", lambda g: g.dma_start(
            wsl_v, wo_d.rearrange("(n p) o -> p n o", p=128)),
            sem_w[0], writes=[b_ws, b_wsV] + b_pt + b_grow, tag="wo")
        for tt in range(NTC_O if not cfg.get("skip_attn") else 0):
            t0 = tt * 512
            p.dma("sync", lambda s, t0=t0: s.dma_start(
                PF.view(o_big2, ND * 512).rearrange("p (n t) -> p n t", t=512),
                fm(xT, t0, t0 + 512)), sem_big, writes=[b_big2], tag="xown")
            for o in range(ND):
                ps, bps = slot()
                for i in range(ND):
                    p.op("tensor", lambda t, i=i, o=o, t0=t0, ps=ps: t.matmul(
                        ps, wq_sl[i][:, o * 128:(o + 1) * 128], ao[i][:, t0:t0 + 512],
                        start=(i == 0), stop=(i == ND - 1)),
                        reads=[b_ws, b_ao[i]], writes=[bps], war_free=(i > 0), tag="woMM")
                p.op("scalar", lambda s, o=o, ps=ps: s.activation(
                    big1[o], ps, AF.Identity, bias=cc("bo", o), scale=1.0),
                    reads=[bps, bccol], writes=[b_big1], tag="woevac")
            emit_stats(cast_f32([big1[o] for o in range(ND)], [b_big1]))
            for o in range(ND):
                a, ab = asc[o % 2], b_asc[o % 2]
                p.op("vector", lambda v, o=o, a=a: v.tensor_sub(a, big1[o], mean_b),
                     reads=[b_big1, b_stats], writes=[ab])
                p.op("vector", lambda v, a=a: v.tensor_mul(a, a, rstd_b),
                     reads=[ab, b_stats], writes=[ab])
                p.op("vector", lambda v, o=o, a=a: v.tensor_add(big2[o], big2[o], a),
                     reads=[ab, b_big2], writes=[b_big2], tag="x1add")
            p.dma("sync", lambda s, t0=t0: s.dma_start(
                fm(x1d, t0, t0 + 512),
                PF.view(o_big2, ND * 512).rearrange("p (n t) -> p n t", t=512)),
                sem_big, reads=[b_big2], writes=[b_x1d], tag="x1st")

        if cfg.get("stop_after_A"):
            p.barrier("sync", [Tok("dma", sem=sem_big, idx=sem_big.count)])
            p.finalize(block)
            nc._layout = dict(o_xln=o_xln, o_qs=o_qs, o_ao=o_ao, o_k=o_k, o_v=o_v,
                              o_sc=o_sc, o_ws=o_ws,
                              o_st=o_st, o_big1=o_big1, o_big2=o_big2)
            return nc

        a_bufs = (b_xln + b_qs + b_ao + [b_k, b_kb1] + b_v + b_pt + b_grow +
                  [b_ws, b_wsV, b_xbs[0], b_xbs[1]])
        fence = p.op("vector", lambda v: v.engine_nop(), writes=a_bufs, tag="fence")

        h0 = [PB.view(o_h0 + j * TB, TB) for j in range(NJ)]
        h1v = [PB.view(o_h1 + j * TB, TB) for j in range(NJ)]
        xln2 = [PB.view(o_xln2 + i * TB, TB) for i in range(ND)]
        b_h0 = [p.buf(f"h0_{j}") for j in range(NJ)]
        b_h1 = [p.buf(f"h1_{j}") for j in range(NJ)]
        b_xln2 = [p.buf(f"xln2_{i}") for i in range(ND)]
        fsl = [PB.view(o_ws + s * 2048, 2048) for s in range(4)]
        b_fsl = [p.buf(f"fsl{s}") for s in range(4)]
        for b in b_h0 + b_h1 + b_xln2 + b_fsl:
            b.last_writer = fence
        _slot_pool[0] = [2, 3, 4, 5]

        def _stop(bufs):
            t = p.op("vector", lambda v: v.engine_nop(), reads=bufs, tag="stopper")
            p.barrier("sync", [t, Tok("dma", sem=sem_big, idx=sem_big.count)])
            p.finalize(block)
            nc._layout = dict(o_xln=o_xln, o_qs=o_qs, o_ao=o_ao, o_k=o_k, o_v=o_v,
                              o_sc=o_sc, o_ws=o_ws,
                              o_h0=o_h0, o_h1=o_h1, o_xln2=o_xln2,
                              o_st=o_st, o_big1=o_big1, o_big2=o_big2)
            return nc

        def emit_ln2(ft):
            tb0 = ft * TB
            for sb in range(NSB if not cfg.get("skip_ln2") else 0):
                t0 = tb0 + sb * 512
                p.dma("sync", lambda s, t0=t0: s.dma_start(
                    PF.view(o_big2, ND * 512).rearrange("p (n t) -> p n t", t=512),
                    fm(x1d, t0, t0 + 512)), sem_big, reads=[b_x1d],
                    writes=[b_big2], tag="x1ld")
                emit_stats(cast_f32([big2[i] for i in range(ND)], [b_big2]))
                for i in range(ND):
                    a, ab = asc[i % 2], b_asc[i % 2]
                    p.op("vector", lambda v, i=i, a=a: v.tensor_sub(a, big2[i], mean_b),
                         reads=[b_big2, b_stats], writes=[ab])
                    p.op("vector", lambda v, i=i, sb=sb, a=a: v.tensor_mul(
                        xln2[i][:, sb * 512:(sb + 1) * 512], a, rstd_b),
                        reads=[ab, b_stats], writes=[b_xln2[i]], tag="ln2ap")

        emit_ln2(0)
        for ft in range(NFT):
            tb0 = ft * TB
            if cfg.get("stop_after_LN2"):
                return _stop(b_xln2)
            for jg in range(DH // 256 if not cfg.get("s1_one_jg") else 1):
                wi = jg % 2
                p.dma("gpsimd", lambda g, jg=jg, wi=wi: g.dma_start(
                    fsl[wi].rearrange("p (n o) -> p n o", o=256),
                    w1t_d.rearrange("(n p) o -> p n o", p=128)[:, :, jg * 256:(jg + 1) * 256]),
                    sem_w[wi], writes=[b_fsl[wi]], tag="w1l")
                p.dma("gpsimd", lambda g, jg=jg, wi=wi: g.dma_start(
                    fsl[2 + wi].rearrange("p (n o) -> p n o", o=256),
                    w2t_d.rearrange("(n p) o -> p n o", p=128)[:, :, jg * 256:(jg + 1) * 256]),
                    sem_w[2 + wi], writes=[b_fsl[2 + wi]], tag="w2l")
                for jj in range(2):
                    j = jg * 2 + jj
                    for sb in range(NSB):
                        psa, bpsa = slot()
                        psb2, bpsb2 = slot()
                        for i in range(ND):
                            p.op("tensor", lambda t, i=i, jj=jj, sb=sb, psa=psa, wi=wi: t.matmul(
                                psa, fsl[wi][:, i * 256 + jj * 128:i * 256 + (jj + 1) * 128],
                                xln2[i][:, sb * 512:(sb + 1) * 512],
                                start=(i == 0), stop=(i == ND - 1)),
                                reads=[b_fsl[wi], b_xln2[i]], writes=[bpsa],
                                war_free=(i > 0), tag="w1MM")
                        for i in range(ND):
                            p.op("tensor", lambda t, i=i, jj=jj, sb=sb, psb2=psb2, wi=wi: t.matmul(
                                psb2, fsl[2 + wi][:, i * 256 + jj * 128:i * 256 + (jj + 1) * 128],
                                xln2[i][:, sb * 512:(sb + 1) * 512],
                                start=(i == 0), stop=(i == ND - 1)),
                                reads=[b_fsl[2 + wi], b_xln2[i]], writes=[bpsb2],
                                war_free=(i > 0), tag="w2MM")
                        _af1 = AF.Identity if cfg.get("no_silu") else (AF.Gelu if cfg.get("silu_as_gelu") else (AF.Sigmoid if cfg.get("silu_as_sigmoid") else AF.Silu))
                        if cfg.get("one_silu") and (jg, jj, sb) != (0, 0, 0):
                            _af1 = AF.Identity
                        _sc1 = 0.5 if cfg.get("silu_const_scale") else cc("w1s", j)
                        if cfg.get("silu_from_sbuf"):
                            p.op("scalar", lambda s, psa=psa: s.activation(
                                sc16[2][:, :], psa, AF.Identity, bias=0.0, scale=1.0),
                                reads=[bpsa], writes=[b_sc16[2], b_kb1], tag="pre")
                            p.op("scalar", lambda s, j=j, _af1=_af1, _sc1=_sc1: s.activation(
                                sc16[0][:, :], sc16[2][:, :], _af1, bias=0.0, scale=_sc1),
                                reads=[b_sc16[2], bccol], writes=[b_sc16[0], b_kb1], tag="silu")
                        else:
                            p.op("scalar", lambda s, j=j, psa=psa, _af1=_af1, _sc1=_sc1: s.activation(
                                sc16[0][:, :], psa, _af1, bias=0.0, scale=_sc1),
                                reads=[bpsa, bccol], writes=[b_sc16[0], b_kb1], tag="silu")
                        _af2 = AF.Silu if cfg.get("w2_as_silu") else AF.Identity
                        p.op("scalar", lambda s, j=j, psb2=psb2, _af2=_af2: s.activation(
                            sc16[1][:, :], psb2, _af2, bias=0.0, scale=cc("w2s", j)),
                            reads=[bpsb2, bccol], writes=[b_sc16[1], b_kb1], tag="w2sc")
                        p.op("vector", lambda v, j=j, sb=sb: v.tensor_mul(
                            h0[j][:, sb * 512:(sb + 1) * 512], sc16[0][:, :], sc16[1][:, :]),
                            reads=[b_sc16[0], b_sc16[1]], writes=[b_h0[j]], tag="h0mul")
            if cfg.get("stop_after_S1"):
                return _stop(b_h0)
            if ft + 1 < NFT:
                emit_ln2(ft + 1)
            for (wt_d, scn, src, bsrc, dst, bdst) in (
                    (h1t_d, "h1s", h0, b_h0, h1v, b_h1),
                    (h2t_d, "h2s", h1v, b_h1, h0, b_h0)):
                for j2 in range(NJ):
                    wi = j2 % 2
                    for qd in range(4):
                        jq0 = qd * NJ // 4
                        p.dma("gpsimd", lambda g, j2=j2, wi=wi, wt_d=wt_d, jq0=jq0: g.dma_start(
                            PB.view(o_ws + wi * 4096 + jq0 * 128, NJ * 32).rearrange(
                                "p (n o) -> p n o", o=128),
                            wt_d.rearrange("(n p) o -> p n o", p=128)[
                                :, jq0:jq0 + NJ // 4, j2 * 128:(j2 + 1) * 128]),
                            sem_w[wi], writes=[b_fsl[2 * wi], b_fsl[2 * wi + 1]],
                            war_free=(qd > 0), tag="hwl")
                    for sb in range(NSB):
                        ps, bps = slot()
                        for j in range(NJ):
                            p.op("tensor", lambda t, j=j, sb=sb, ps=ps, wi=wi, src=src: t.matmul(
                                ps, PB.view(o_ws + wi * 4096 + j * 128, 128),
                                src[j][:, sb * 512:(sb + 1) * 512],
                                start=(j == 0), stop=(j == NJ - 1)),
                                reads=[b_fsl[2 * wi], b_fsl[2 * wi + 1], bsrc[j]],
                                writes=[bps], war_free=(j > 0), tag="hMM")
                        p.op("scalar", lambda s, j2=j2, sb=sb, ps=ps, scn=scn, dst=dst,
                             bdst=bdst: s.activation(
                            dst[j2][:, sb * 512:(sb + 1) * 512], ps, AF.Gelu,
                            bias=0.0, scale=cc(scn, j2)),
                            reads=[bps, bccol], writes=[bdst[j2]], tag="gelu")
            if cfg.get("stop_after_S3"):
                return _stop(b_h0 + b_h1)
            for sb in range(NSB):
                t0 = tb0 + sb * 512
                for o in range(ND):
                    wi = o % 2
                    for qd in range(4):
                        jq0 = qd * NJ // 4
                        p.dma("gpsimd", lambda g, o=o, wi=wi, jq0=jq0: g.dma_start(
                            PB.view(o_ws + wi * 4096 + jq0 * 128, NJ * 32).rearrange(
                                "p (n o2) -> p n o2", o2=128),
                            w3t_d.rearrange("(n p) o2 -> p n o2", p=128)[
                                :, jq0:jq0 + NJ // 4, o * 128:(o + 1) * 128]),
                            sem_w[wi], writes=[b_fsl[2 * wi], b_fsl[2 * wi + 1]],
                            war_free=(qd > 0), tag="w3l")
                    ps, bps = slot()
                    for j in range(NJ):
                        p.op("tensor", lambda t, j=j, sb=sb, ps=ps, wi=wi: t.matmul(
                            ps, PB.view(o_ws + wi * 4096 + j * 128, 128),
                            h0[j][:, sb * 512:(sb + 1) * 512],
                            start=(j == 0), stop=(j == NJ - 1)),
                            reads=[b_fsl[2 * wi], b_fsl[2 * wi + 1], b_h0[j]],
                            writes=[bps], war_free=(j > 0), tag="w3MM")
                    p.op("scalar", lambda s, o=o, ps=ps: s.activation(
                        big1[o], ps, AF.Identity, bias=0.0, scale=cc("w3s", o)),
                        reads=[bps, bccol], writes=[b_big1], tag="w3evac")
                p.dma("sync", lambda s, t0=t0: s.dma_start(
                    PF.view(o_big2, ND * 512).rearrange("p (n t) -> p n t", t=512),
                    fm(x1d, t0, t0 + 512)), sem_big, reads=[b_x1d],
                    writes=[b_big2], tag="x1ld2")
                emit_stats(cast_f32([big1[o] for o in range(ND)], [b_big1]))
                for o in range(ND):
                    a, ab = asc[o % 2], b_asc[o % 2]
                    p.op("vector", lambda v, o=o, a=a: v.tensor_sub(a, big1[o], mean_b),
                         reads=[b_big1, b_stats], writes=[ab])
                    p.op("vector", lambda v, a=a: v.tensor_mul(a, a, rstd_b),
                         reads=[ab, b_stats], writes=[ab])
                    p.op("vector", lambda v, o=o, a=a: v.tensor_add(big2[o], big2[o], a),
                         reads=[ab, b_big2], writes=[b_big2], tag="x2add")
                p.dma("sync", lambda s, t0=t0: s.dma_start(
                    fm(outT, t0, t0 + 512),
                    PF.view(o_big2, ND * 512).rearrange("p (n t) -> p n t", t=512)),
                    sem_st, reads=[b_big2], tag="outst")

        p.barrier("sync", [Tok("dma", sem=sem_st, idx=sem_st.count)])
        p.finalize(block)
    nc._layout = dict(o_xln=o_xln, o_qs=o_qs, o_ao=o_ao, o_k=o_k, o_v=o_v,
                      o_sc=o_sc, o_ws=o_ws,
                      o_h0=o_h0, o_h1=o_h1, o_xln2=o_xln2,
                      o_st=o_st, o_big1=o_big1, o_big2=o_big2)
    return nc


_CACHE = {}


def _get_nc(cfg_key, cfg):
    if cfg_key not in _CACHE:
        _CACHE[cfg_key] = build_program(cfg)
    return _CACHE[cfg_key]


def make_host_inputs(cfg, x, Wq, bq, Wk, bk, Wv, bv, Wo, bo, ln_g, ln_b,
                     w1_w, w1_s, w2_w, w2_s, h1_w, h1_s, h2_w, h2_s, w3_w, w3_s):
    D, To, Tf = cfg["D"], cfg["To"], cfg["Tf"]
    B = x.shape[0]
    f = np.float32
    ac = np.ascontiguousarray

    def col8(vec, n):
        return vec.reshape(n, 128).T.astype(f)

    ccol = np.zeros((128, NCC), f)
    ccol[:, CC["bq"]:CC["bq"] + 8] = col8(bq, 8)
    ccol[:, CC["bk"]:CC["bk"] + 8] = col8(bk, 8)
    ccol[:, CC["bo"]:CC["bo"] + 8] = col8(bo, 8)
    ccol[:, CC["lng"]:CC["lng"] + 8] = col8(ln_g, 8)
    ccol[:, CC["lnb"]:CC["lnb"] + 8] = col8(ln_b, 8)
    ccol[:, CC["w1s"]:CC["w1s"] + 32] = col8(w1_s, 32)
    ccol[:, CC["w2s"]:CC["w2s"] + 32] = col8(w2_s, 32)
    ccol[:, CC["h1s"]:CC["h1s"] + 32] = col8(h1_s, 32)
    ccol[:, CC["h2s"]:CC["h2s"] + 32] = col8(h2_s, 32)
    ccol[:, CC["w3s"]:CC["w3s"] + 8] = col8(w3_s, 8)
    ccol[:, CC["eps"]] = 1e-5
    crow = np.concatenate([bv.astype(f), np.full(128, -1.0, f)])[None, :]
    shared = dict(
        wq=ac(Wq.astype(f)), wk=ac(Wk.astype(f)), wv=ac(Wv.astype(f)),
        wo=ac(Wo.astype(f)),
        w1t=ac(w1_w.T.astype(f)), w2t=ac(w2_w.T.astype(f)),
        h1t=ac(h1_w.T.astype(f)), h2t=ac(h2_w.T.astype(f)),
        w3t=ac(w3_w.T.astype(f)),
        ccol=ccol, crow=ac(crow.astype(f)),
        cones=np.ones((128, 128), f),
    )
    in_maps = []
    for b in range(B):
        xt = ac(x[b].T.astype(f))
        for hh in range(2):
            if hh == 0:
                xcore = xt
            else:
                xcore = ac(np.concatenate([xt[:, To:], xt[:, :To]], axis=1))
            in_maps.append(dict(shared, xT=xcore))
    return in_maps


def kernel(L0_attn_x, L0_attn_Wq, L0_attn_bq, L0_attn_Wk, L0_attn_bk,
           L0_attn_Wv, L0_attn_bv, L0_attn_Wo, L0_attn_bo,
           L0_attn_ln_g, L0_attn_ln_b,
           L0_ffn_w1_w, L0_ffn_w1_s, L0_ffn_w2_w, L0_ffn_w2_s,
           L0_ffn_h1_w, L0_ffn_h1_s, L0_ffn_h2_w, L0_ffn_h2_s,
           L0_ffn_w3_w, L0_ffn_w3_s):
    cfg = default_cfg()
    x = np.asarray(L0_attn_x)
    B, S, D = x.shape
    To = cfg["To"]
    nc = _get_nc("full", cfg)
    in_maps = make_host_inputs(
        cfg, x, L0_attn_Wq, L0_attn_bq, L0_attn_Wk, L0_attn_bk,
        L0_attn_Wv, L0_attn_bv, L0_attn_Wo, L0_attn_bo,
        L0_attn_ln_g, L0_attn_ln_b,
        L0_ffn_w1_w, L0_ffn_w1_s, L0_ffn_w2_w, L0_ffn_w2_s,
        L0_ffn_h1_w, L0_ffn_h1_s, L0_ffn_h2_w, L0_ffn_h2_s,
        L0_ffn_w3_w, L0_ffn_w3_s)
    res = run_bass_kernel_spmd(nc, in_maps, core_ids=list(range(8)))
    out = np.empty((B, S, D), np.float32)
    for b in range(B):
        out[b, :To, :] = res.results[2 * b]["outT"].T
        out[b, To:, :] = res.results[2 * b + 1]["outT"].T
    return out


# version 16
# speedup vs baseline: 1.4647x; 1.0044x over previous
import sys
sys.path.insert(0, '/opt/trn_rl_repo')
from contextlib import ExitStack
import numpy as np
import concourse.bass as bass
import concourse.mybir as mybir
from concourse.bass_utils import run_bass_kernel_spmd

F32 = mybir.dt.float32
BF16 = mybir.dt.bfloat16
AF = mybir.ActivationFunctionType
ALU = mybir.AluOpType
AX = mybir.AxisListType

ENGINES = ("tensor", "vector", "scalar", "gpsimd", "sync")


class Buf:
    __slots__ = ("name", "last_writer", "readers", "psum")

    def __init__(self, name, psum=False):
        self.name = name
        self.last_writer = None
        self.readers = []
        self.psum = psum


class DmaSem:
    __slots__ = ("handle", "count", "name")

    def __init__(self, handle, name):
        self.handle = handle
        self.count = 0
        self.name = name


class Op:
    __slots__ = ("engine", "fn", "seq", "inc", "sem_val", "deps", "dma_sem", "tag")

    def __init__(self, engine, fn, seq, dma_sem, tag):
        self.engine = engine
        self.fn = fn
        self.seq = seq
        self.inc = dma_sem is not None
        self.sem_val = None
        self.deps = []
        self.dma_sem = dma_sem
        self.tag = tag


class Tok:
    __slots__ = ("kind", "engine", "idx", "sem")

    def __init__(self, kind, engine=None, idx=None, sem=None):
        self.kind = kind
        self.engine = engine
        self.idx = idx
        self.sem = sem


class Prog:
    def __init__(self, nc, stack):
        self.nc = nc
        self.stack = stack
        self.ops = {e: [] for e in ENGINES}
        self.sems = {e: stack.enter_context(nc.semaphore(f"pg_{e}")) for e in ENGINES}
        self.n_bufs = 0
        self.n_dsems = 0

    def buf(self, name=None, psum=False):
        self.n_bufs += 1
        return Buf(name or f"b{self.n_bufs}", psum)

    def dma_sem(self, name=None):
        self.n_dsems += 1
        name = name or f"ds{self.n_dsems}"
        return DmaSem(self.stack.enter_context(self.nc.semaphore(name)), name)

    def _collect_deps(self, o, tok, reads, writes, war_free):
        deps = []

        def add(d):
            if d is tok:
                return
            if d.kind == "eng":
                deps.append(("eng", d.engine, d.idx))
            else:
                deps.append(("dma", d.sem, 16 * d.sem.count))

        for b in reads:
            if b.last_writer is not None:
                add(b.last_writer)
            if b.psum:
                for r in b.readers:
                    if r.kind == "eng" and r.engine != tok.engine:
                        add(r)
            b.readers.append(tok)
        for b in writes:
            if not war_free:
                if b.last_writer is not None:
                    add(b.last_writer)
                for r in b.readers:
                    add(r)
            b.readers = []
            b.last_writer = tok
        o.deps = deps

    def op(self, engine, fn, reads=(), writes=(), war_free=False, tag=""):
        seq = len(self.ops[engine])
        tok = Tok("eng", engine=engine, idx=seq)
        o = Op(engine, fn, seq, None, tag)
        self._collect_deps(o, tok, reads, writes, war_free)
        self.ops[engine].append(o)
        return tok

    def dma(self, qengine, fn, sem, reads=(), writes=(), war_free=False, tag=""):
        seq = len(self.ops[qengine])
        tok = Tok("dma", engine=qengine, idx=None, sem=sem)
        o = Op(qengine, fn, seq, sem, tag)
        self._collect_deps(o, tok, reads, writes, war_free)
        sem.count += 1
        tok.idx = sem.count
        self.ops[qengine].append(o)
        return tok

    def barrier(self, engine, toks, tag="barrier"):
        seq = len(self.ops[engine])
        o = Op(engine, lambda eng: eng.nop(), seq, None, tag)
        deps = []
        for d in toks:
            if d.kind == "eng":
                deps.append(("eng", d.engine, d.idx))
            else:
                deps.append(("dma", d.sem, 16 * d.sem.count))
        o.deps = deps
        self.ops[engine].append(o)
        return Tok("eng", engine=engine, idx=seq)

    def _resolve(self):
        for e in ENGINES:
            for o in self.ops[e]:
                kept = []
                for d in o.deps:
                    if d[0] == "eng":
                        if d[1] == "tensor" and e == "tensor":
                            continue
                        self.ops[d[1]][d[2]].inc = True
                    kept.append(d)
                o.deps = kept
        for e in ENGINES:
            v = 0
            for o in self.ops[e]:
                if o.dma_sem is not None:
                    continue
                if o.inc:
                    v += 1
                    o.sem_val = v

    def finalize(self, block):
        self._resolve()
        prog = self

        def section(e):
            def body(eng):
                watermark = {}
                for o in prog.ops[e]:
                    for d in o.deps:
                        if d[0] == "eng":
                            sem = prog.sems[d[1]]
                            val = prog.ops[d[1]][d[2]].sem_val
                            assert val is not None, (d, o.tag)
                        else:
                            sem = d[1].handle
                            val = d[2]
                        key = id(sem)
                        if watermark.get(key, -1) >= val:
                            continue
                        watermark[key] = val
                        eng.wait_ge(sem, val)
                    inst = o.fn(eng)
                    if o.dma_sem is not None:
                        inst.then_inc(o.dma_sem.handle, 16)
                    elif o.inc:
                        inst.then_inc(prog.sems[e], 1)
            return body

        for e in ENGINES:
            if not self.ops[e]:
                continue
            getattr(block, e)(section(e))


class Pool:

    def __init__(self, nc, stack, name, dtype, elems):
        self.t = stack.enter_context(nc.sbuf_tensor(name, [128, elems], dtype))
        self.elems = elems
        self.off = 0

    def alloc(self, n, align=64):
        o = -(-self.off // align) * align
        assert o + n <= self.elems, f"pool overflow: {o}+{n} > {self.elems}"
        self.off = o + n
        return o

    def view(self, off, n, p0=0, p1=128):
        return self.t[p0:p1, off:off + n]


CC = {}
_o = 0
for _nm, _n in [("bq", 8), ("bk", 8), ("bo", 8), ("lng", 8), ("lnb", 8),
                ("w1s", 32), ("w2s", 32), ("h1s", 32), ("h2s", 32), ("w3s", 8),
                ("eps", 1)]:
    CC[_nm] = _o
    _o += _n
NCC = _o


def default_cfg():
    return dict(D=1024, H=16, DH=4096, To=2048, Tf=4096)


def build_program(cfg):
    D, H, DH, To, Tf = cfg["D"], cfg["H"], cfg["DH"], cfg["To"], cfg["Tf"]
    ND = D // 128
    NP = H // 2
    NJ = DH // 128
    NKC = Tf // 128
    NSEG = Tf // 512
    NQT = To // 512
    NTC_F = Tf // 512
    NTC_O = To // 512
    TB = min(1024, To)
    NFT = To // TB
    NSB = TB // 512
    assert NP == ND and D == 1024

    nc = bass.Bass("TRN2", target_bir_lowering=False, debug=False,
                   dynamic_dma_scratch_size=16384)
    di = lambda n, s: nc.dram_tensor(n, s, F32, kind="ExternalInput").ap()
    xT = di("xT", [D, Tf])
    wq_d, wk_d, wv_d, wo_d = di("wq", [D, D]), di("wk", [D, D]), di("wv", [D, D]), di("wo", [D, D])
    w1t_d, w2t_d = di("w1t", [D, DH]), di("w2t", [D, DH])
    h1t_d, h2t_d = di("h1t", [DH, DH]), di("h2t", [DH, DH])
    w3t_d = di("w3t", [DH, D])
    ccol_d = di("ccol", [128, NCC])
    crow_d = di("crow", [1, D + 128])
    cones_d = di("cones", [128, 128])
    x1d = nc.dram_tensor("x1d", [D, To], F32, kind="Internal").ap()
    outT = nc.dram_tensor("outT", [D, To], F32, kind="ExternalOutput").ap()

    def fm(dram_ap, t0, t1, n_last=None):
        r = dram_ap.rearrange("(n p) t -> p n t", p=128)
        return r[:, :, t0:t1]

    stack = ExitStack()
    with stack:
        E = stack.enter_context
        p = Prog(nc, stack)

        _o_sc = 0
        _o_ws = _o_sc + 4 * 512
        _main = _o_ws + 2 * ND * 512
        _a_end = _main + ND * Tf + 2 * ND * To + Tf + NKC * 130
        _b_end = _main + 2 * NJ * TB + ND * TB
        pb_elems = max(_a_end, _b_end) + 64
        pf_elems = (3 * 512 + 2 * ND * 512)
        PB = Pool(nc, stack, "PB", BF16, pb_elems)
        PF = Pool(nc, stack, "PF", F32, pf_elems)

        ccol_t = E(nc.sbuf_tensor("ccol_t", [128, NCC], F32))
        onesf_t = E(nc.sbuf_tensor("onesf_t", [128, 128], F32))
        ones_b = E(nc.sbuf_tensor("ones_b", [128, 128], BF16))
        neg_b = E(nc.sbuf_tensor("neg_b", [128, 128], BF16))
        bks_t = E(nc.sbuf_tensor("bks_t", [128, 8], F32))

        psP1a = E(nc.psum_tensor("psP1a", [128, 512], F32))
        psP1b = E(nc.psum_tensor("psP1b", [128, 512], F32))
        psST0 = E(nc.psum_tensor("psST0", [128, 1024], F32))
        psST1 = E(nc.psum_tensor("psST1", [128, 1024], F32))
        psPVa = E(nc.psum_tensor("psPVa", [65, 512], F32))
        psPVb = E(nc.psum_tensor("psPVb", [65, 512], F32))
        block = E(nc.Block())

        bP1a, bP1b = p.buf("psP1a", True), p.buf("psP1b", True)
        bST = [p.buf(f"psST{i}", True) for i in range(4)]
        bPVa, bPVb = p.buf("psPVa", True), p.buf("psPVb", True)
        _slots = [(psP1a, 0, bP1a), (psP1b, 0, bP1b),
                  (psST0, 0, bST[0]), (psST0, 512, bST[1]),
                  (psST1, 0, bST[2]), (psST1, 512, bST[3])]
        _si = [0]

        _slot_pool = [list(range(6))]

        def slot():
            pool = _slot_pool[0]
            h, lo, b = _slots[pool[_si[0] % len(pool)]]
            _si[0] += 1
            return h[:, lo:lo + 512], b

        bccol, bonesf, bonesb, bnegb, bbks = (p.buf(x) for x in
            ["ccol", "onesf", "onesb", "negb", "bks"])

        sem_c = p.dma_sem("c")
        sem_cg = p.dma_sem("cg")
        sem_x = [p.dma_sem("x0"), p.dma_sem("x1")]
        sem_w = [p.dma_sem(f"w{i}") for i in range(4)]
        sem_bv = p.dma_sem("bv")
        sem_qg = [p.dma_sem("qg0"), p.dma_sem("qg1")]
        sem_gg = [p.dma_sem("gg0"), p.dma_sem("gg1")]
        sem_big = p.dma_sem("big")
        sem_st = p.dma_sem("st")

        p.dma("sync", lambda s: s.dma_start(ccol_t[:, :], ccol_d), sem_c, writes=[bccol])
        p.dma("sync", lambda s: s.dma_start(onesf_t[:, :], cones_d), sem_c, writes=[bonesf])
        p.dma("gpsimd", lambda g: g.dma_start(ones_b[:, :], cones_d), sem_cg, writes=[bonesb])
        p.dma("gpsimd", lambda g: g.dma_start(neg_b[0:1, :], crow_d[0:1, D:D + 128]),
              sem_cg, writes=[bnegb])
        p.dma("gpsimd", lambda g: g.dma_start(neg_b[64:65, :], crow_d[0:1, D:D + 128]),
              sem_cg, writes=[bnegb], war_free=True)
        p.op("vector", lambda v: v.tensor_scalar_mul(
            bks_t[:, :], ccol_t[:, CC["bk"]:CC["bk"] + 8], 0.125),
            reads=[bccol], writes=[bbks])

        def cc(nm, i=0):
            return ccol_t[:, CC[nm] + i:CC[nm] + i + 1]

        eps_ap = ccol_t[:, CC["eps"]:CC["eps"] + 1]

        o_sc = _o_sc
        o_ws = _o_ws
        o_xln = _main
        o_qs = o_xln + ND * Tf
        o_ao = o_qs + ND * To
        o_k = o_ao + ND * To
        o_v = o_k + Tf
        o_pt_m = _o_ws + 5120
        o_scpy = _o_ws + 2048
        o_pt0 = _o_ws + 3072
        o_grow = _o_ws + 4096
        assert Tf <= 4096
        o_h0 = _main
        o_h1 = o_h0 + NJ * TB
        o_xln2 = o_h1 + NJ * TB

        xln = [PB.view(o_xln + i * Tf, Tf) for i in range(ND)]
        q_s = [PB.view(o_qs + i * To, To) for i in range(ND)]
        ao = [PB.view(o_ao + i * To, To) for i in range(ND)]
        K_hp = PB.view(o_k, Tf)
        V_hp = [PB.view(o_v + k * 130, 130) for k in range(NKC)]
        sc16 = [PB.view(o_sc + i * 512, 512) for i in range(4)]
        b_xln = [p.buf(f"xln{i}") for i in range(ND)]
        b_qs = [p.buf(f"qs{i}") for i in range(ND)]
        b_ao = [p.buf(f"ao{i}") for i in range(ND)]
        b_k = p.buf("K_hp")
        b_v = [p.buf(f"V{k}") for k in range(NKC)]
        b_pt = [p.buf("pt0"), p.buf("pt1"), p.buf("pt2")]

        def PTof(par):
            return o_pt0 if par == 0 else o_pt_m + (par - 1) * 1024
        scpy = [PB.view(o_scpy + i * 512, 512) for i in range(2)]
        b_scpy = [p.buf("scpy0"), p.buf("scpy1")]
        b_grow = [p.buf("grow0"), p.buf("grow1")]
        b_kb1 = p.buf("kb1")
        b_sc16 = [p.buf(f"sc16_{i}") for i in range(4)]
        b_ws = p.buf("wslab")
        b_wsV = p.buf("wslabV")
        b_xbs = [p.buf("xb0"), p.buf("xb1")]

        o_st = PF.alloc(3 * 512)
        mean_b = PF.view(o_st, 512)
        rstd_b = PF.view(o_st + 512, 512)
        s1f = PF.view(o_st + 1024, 512)
        b_stats = p.buf("stats")
        asc = [s1f, s1f]
        b_asc = [b_stats, b_stats]

        o_big1 = PF.alloc(ND * 512)
        o_rd = o_big1
        o_rb = o_big1 + 1024
        o_bv = o_big1 + 1536
        o_brow = o_big1 + 1664
        o_cm = o_big1 + 1792
        o_gc = o_big1 + 1920
        o_big2 = PF.alloc(ND * 512)
        big1 = [PF.view(o_big1 + i * 512, 512) for i in range(ND)]
        big2 = [PF.view(o_big2 + i * 512, 512) for i in range(ND)]
        b_stats = p.buf("stats")
        b_cm = [p.buf("cm0"), p.buf("cm1")]
        b_gc = [p.buf("gc0"), p.buf("gc1")]
        b_rd, b_rb, b_bv, b_brow = p.buf("rd"), p.buf("rb"), p.buf("bv"), p.buf("brow")
        b_big1, b_big2 = p.buf("big1"), p.buf("big2")
        b_x1d = p.buf("x1d_dram")

        def emit_stats(bf_chunks):
            n = len(bf_chunks)
            for i, (kind, xap, xbufs) in enumerate(bf_chunks):
                if kind == "f32":
                    dst, db = sc16[i % 2], b_sc16[i % 2]
                    p.op("vector", lambda v, xap=xap, dst=dst: v.tensor_copy(
                        dst[:, :], xap), reads=list(xbufs), writes=[db, b_kb1], tag="cast")
                    xap, xbufs = dst[:, :], [db]
                sq, bsq = sc16[2 + (i % 2)], b_sc16[2 + (i % 2)]
                p.op("vector", lambda v, xap=xap, sq=sq: v.tensor_mul(sq[:, :], xap, xap),
                     reads=list(xbufs), writes=[bsq, b_kb1], tag="sq")
                p.op("tensor", lambda t, xap=xap, i=i, n=n: t.matmul(
                    psP1a[:, :], ones_b[:, :], xap, start=(i == 0), stop=(i == n - 1)),
                    reads=[bonesb] + list(xbufs), writes=[bP1a], war_free=(i > 0), tag="sMM")
                p.op("tensor", lambda t, sq=sq, i=i, n=n: t.matmul(
                    psP1b[:, :], ones_b[:, :], sq[:, :], start=(i == 0), stop=(i == n - 1)),
                    reads=[bonesb, bsq], writes=[bP1b], war_free=(i > 0), tag="qMM")
            inv = 1.0 / (n * 128)
            p.op("vector", lambda v: v.tensor_scalar_mul(mean_b, psP1a[:, :], inv),
                 reads=[bP1a], writes=[b_stats])
            p.op("vector", lambda v: v.tensor_scalar_mul(s1f, psP1b[:, :], inv),
                 reads=[bP1b], writes=[b_stats])
            p.op("vector", lambda v: v.scalar_tensor_tensor(
                rstd_b, mean_b, 1.0, mean_b, op0=ALU.mult, op1=ALU.mult),
                reads=[b_stats], writes=[b_stats])
            p.op("vector", lambda v: v.tensor_sub(s1f, s1f, rstd_b),
                 reads=[b_stats], writes=[b_stats])
            p.op("scalar", lambda s: s.activation(s1f, s1f, AF.Sqrt, bias=eps_ap, scale=1.0),
                 reads=[b_stats, bccol], writes=[b_stats])
            p.op("vector", lambda v: v.reciprocal(rstd_b, s1f),
                 reads=[b_stats], writes=[b_stats])

        def cast_f32(srcs, bufs_read):
            return [("f32", ap, list(bufs_read)) for ap in srcs]

        xb = [[PB.view(o_ws + sl * ND * 512 + i * 512, 512) for i in range(ND)]
              for sl in range(2)]
        for tc in range(NTC_F if not cfg.get("skip_a1") else 0):
            sl = tc % 2
            dst = PB.view(o_ws + sl * ND * 512, ND * 512).rearrange(
                "p (n t) -> p n t", t=512)
            p.dma("gpsimd", lambda g, tc=tc, dst=dst: g.dma_start(
                dst, fm(xT, tc * 512, (tc + 1) * 512)),
                sem_x[sl], writes=[b_xbs[sl]], tag="xload")
            emit_stats([("bf", xb[sl][i][:, :], [b_xbs[sl]]) for i in range(ND)])
            for i in range(ND):
                a, ab = asc[i % 2], b_asc[i % 2]
                p.op("vector", lambda v, i=i, sl=sl, a=a: v.tensor_sub(
                    a, xb[sl][i][:, :], mean_b), reads=[b_xbs[sl], b_stats], writes=[ab])
                p.op("vector", lambda v, i=i, tc=tc, a=a: v.tensor_mul(
                    xln[i][:, tc * 512:(tc + 1) * 512], a, rstd_b),
                    reads=[ab, b_stats], writes=[b_xln[i]], tag="ln1ap")

        wsl_v = PB.view(o_ws, ND * 1024).rearrange("p (n o) -> p n o", o=1024)
        wq_sl = [PB.view(o_ws + i * 1024, 1024) for i in range(ND)]
        p.dma("gpsimd", lambda g: g.dma_start(
            wsl_v, wq_d.rearrange("(n p) o -> p n o", p=128)),
            sem_w[0], writes=[b_ws, b_wsV, b_xbs[0], b_xbs[1]] + b_pt + b_grow, tag="wq")
        for tt in range(NQT if not cfg.get("skip_a2") else 0):
            t0 = tt * 512
            for o in range(ND):
                ps, bps = slot()
                for i in range(ND):
                    p.op("tensor", lambda t, i=i, o=o, t0=t0, ps=ps: t.matmul(
                        ps, wq_sl[i][:, o * 128:(o + 1) * 128],
                        xln[i][:, t0:t0 + 512], start=(i == 0), stop=(i == ND - 1)),
                        reads=[b_ws, b_xln[i]], writes=[bps], war_free=(i > 0), tag="qpMM")
                p.op("scalar", lambda s, o=o, ps=ps: s.activation(
                    big1[o], ps, AF.Identity, bias=cc("bq", o), scale=1.0),
                    reads=[bps, bccol], writes=[b_big1], tag="qevac")
            emit_stats(cast_f32([big1[o] for o in range(ND)], [b_big1]))
            for o in range(ND):
                a, ab = asc[o % 2], b_asc[o % 2]
                p.op("vector", lambda v, o=o, a=a: v.tensor_sub(a, big1[o], mean_b),
                     reads=[b_big1, b_stats], writes=[ab])
                p.op("vector", lambda v, a=a: v.tensor_mul(a, a, rstd_b),
                     reads=[ab, b_stats], writes=[ab])
                p.op("scalar", lambda s, o=o, t0=t0, a=a: s.activation(
                    q_s[o][:, t0:t0 + 512], a, AF.Identity,
                    bias=cc("lnb", o), scale=cc("lng", o)),
                    reads=[ab, bccol], writes=[b_qs[o]], tag="qln")

        if cfg.get("stop_after_A2"):
            p.barrier("sync", [Tok("dma", sem=sem_c, idx=sem_c.count)])
            p.finalize(block)
            nc._layout = dict(o_xln=o_xln, o_qs=o_qs, o_ao=o_ao, o_k=o_k, o_v=o_v,
                              o_sc=o_sc, o_ws=o_ws,
                              o_st=o_st, o_big1=o_big1, o_big2=o_big2)
            return nc

        vvíew = PB.t[:, o_v:o_v + NKC * 130].rearrange("p (k c) -> p k c", c=130)
        p.op("vector", lambda v: v.memset(vvíew[:, :, 64:65], 1.0),
             writes=list(b_v), tag="vones")
        p.op("vector", lambda v: v.memset(vvíew[:, :, 129:130], 1.0),
             writes=list(b_v), war_free=True, tag="vones")

        p.op("vector", lambda v: v.memset(PF.view(o_gc, 256), 0.0),
             writes=[b_gc[0], b_gc[1], b_big1], tag="gcinit")
        kvK = [PB.view(o_ws + i * 128, 128) for i in range(ND)]
        kvV = [PB.view(o_ws + 1024 + i * 128, 128) for i in range(ND)]
        bvrow = PF.view(o_brow, 128, 0, 1)

        def emit_kv(hp):
            c0 = hp * 128
            p.dma("gpsimd", lambda g, c0=c0: g.dma_start(
                PB.view(o_ws, ND * 128).rearrange("p (n o) -> p n o", o=128),
                wk_d.rearrange("(n p) o -> p n o", p=128)[:, :, c0:c0 + 128]),
                sem_w[0], writes=[b_ws], tag="wk")
            p.dma("gpsimd", lambda g, c0=c0: g.dma_start(
                PB.view(o_ws + 1024, ND * 128).rearrange("p (n o) -> p n o", o=128),
                wv_d.rearrange("(n p) o -> p n o", p=128)[:, :, c0:c0 + 128]),
                sem_w[1], writes=[b_wsV], tag="wv")
            p.dma("sync", lambda s, c0=c0: s.dma_start(
                bvrow, crow_d[0:1, c0:c0 + 128]), sem_bv, writes=[b_brow, b_big1], tag="bvrow")
            p.op("tensor", lambda t: t.matmul(
                psP1a[:, 0:128], onesf_t[0:1, :], bvrow,
                start=True, stop=True), reads=[bonesf, b_brow], writes=[bP1a], tag="bvMM")
            p.op("vector", lambda v: v.tensor_copy(PF.view(o_bv, 128), psP1a[:, 0:128]),
                 reads=[bP1a], writes=[b_bv, b_big1])
            for tc in range(NTC_F):
                ps, bps = slot()
                for i in range(ND):
                    p.op("tensor", lambda t, i=i, tc=tc, ps=ps: t.matmul(
                        ps, kvK[i][:, :], xln[i][:, tc * 512:(tc + 1) * 512],
                        start=(i == 0), stop=(i == ND - 1)),
                        reads=[b_ws, b_xln[i]], writes=[bps], war_free=(i > 0),
                        tag="kMM")
                p.op("scalar", lambda s, hp=hp, tc=tc, ps=ps: s.activation(
                    K_hp[:, tc * 512:(tc + 1) * 512], ps, AF.Identity,
                    bias=bks_t[:, hp:hp + 1], scale=0.125),
                    reads=[bps, bbks], writes=[b_k], tag="kevac")
            for kc in range(NKC):
                psh, lo, bps = _slots[_si[0] % len(_slots)]
                _si[0] += 1
                for i in range(ND):
                    p.op("tensor", lambda t, i=i, kc=kc, psh=psh, lo=lo: t.matmul(
                        psh[:, lo:lo + 128], xln[i][:, kc * 128:(kc + 1) * 128],
                        kvV[i][:, :], start=(i == 0), stop=(i == ND - 1)),
                        reads=[b_wsV, b_xln[i]], writes=[bps], war_free=(i > 0), tag="vMM")
                for h in range(2):
                    p.op("vector", lambda v, kc=kc, h=h, psh=psh, lo=lo: v.scalar_tensor_tensor(
                        V_hp[kc][:, h * 65:h * 65 + 64], psh[:, lo + h * 64:lo + h * 64 + 64],
                        1.0, PF.view(o_bv, 128)[:, h * 64:(h + 1) * 64],
                        op0=ALU.mult, op1=ALU.add),
                        reads=[bps, b_bv], writes=[b_v[kc]], tag="vevac")

        def pass1_steps(hp, qt):
            c = hp
            t0 = qt * 512
            gsl = qt % 2
            steps = []

            for qs in range(4):
                for seg in range(NSEG):
                    for h in range(2):
                        def th(qs=qs, h=h, seg=seg):
                            k = (qs * NSEG * 2 + seg * 2 + h) % 2
                            pa, pb = (psP1a, bP1a) if k == 0 else (psP1b, bP1b)
                            sc_, bsc_ = scpy[k], b_scpy[k]
                            p.op("tensor", lambda t, qs=qs, h=h, seg=seg: t.matmul(
                                pa[:, :],
                                q_s[c][h * 64:h * 64 + 64,
                                       t0 + qs * 128:t0 + (qs + 1) * 128],
                                K_hp[h * 64:h * 64 + 64, seg * 512:(seg + 1) * 512],
                                start=True, stop=True),
                                reads=[b_qs[c], b_k], writes=[pb], tag="s1MM")
                            col = o_cm + gsl * 64 + qs * 16 + h * 8 + seg
                            p.op("vector", lambda v, col=col: v.reduce_max(
                                PF.view(col, 1), pa[:, :], axis=AX.X, negate=True),
                                reads=[pb], writes=[b_cm[gsl], b_big1], tag="s1max")
                        steps.append(th)

                def th2(qs=qs):
                    gc0 = o_gc + gsl * 128
                    base = o_cm + gsl * 64 + qs * 16
                    for h in range(2):
                        col = gc0 + qs * 2 + h
                        p.op("vector", lambda v, h=h, col=col: v.tensor_reduce(
                            PF.view(col, 1), PF.view(base + h * 8, NSEG),
                            axis=AX.X, op=ALU.min),
                            reads=[b_cm[gsl]], writes=[b_gc[gsl], b_big1], tag="gmin")
                        p.dma("gpsimd", lambda g, h=h, col=col: g.dma_start(
                            PB.t[h * 64:h * 64 + 1,
                                 o_grow + gsl * 512 + qs * 128:
                                 o_grow + gsl * 512 + qs * 128 + 128],
                            PF.t[0:128, col:col + 1]),
                            sem_gg[gsl], reads=[b_gc[gsl]], writes=[b_grow[gsl]],
                            war_free=(qs != 0 or h != 0), tag="grow")
                steps.append(th2)
            return steps

        def emit_pass2(hp, qt, nxt):
            c = hp
            t0 = qt * 512
            gsl = qt % 2

            def emit_pv(kc):
                par = kc % 3
                for h in range(2):
                    psv, bpv = (psPVa, bPVa) if h == 0 else (psPVb, bPVb)
                    p.op("tensor", lambda t, kc=kc, h=h, par=par, psv=psv: t.matmul(
                        psv[:, :], V_hp[kc][:, h * 65:h * 65 + 65],
                        PB.t[:, PTof(par) + h * 512:
                             PTof(par) + (h + 1) * 512],
                        start=(kc == 0), stop=(kc == NKC - 1)),
                        reads=[b_v[kc], b_pt[par]], writes=[bpv],
                        war_free=(kc > 0), tag="pvMM")

            g0 = o_grow + gsl * 512
            for kc in range(NKC):
                pst = psST0 if kc % 2 == 0 else psST1
                bsth = [bST[2 * (kc % 2)], bST[2 * (kc % 2) + 1]]
                for h in range(2):
                    p.op("tensor", lambda t, kc=kc, h=h, pst=pst: t.matmul(
                        pst[:, h * 512:(h + 1) * 512],
                        K_hp[h * 64:h * 64 + 64, kc * 128:(kc + 1) * 128],
                        q_s[c][h * 64:h * 64 + 64, t0:t0 + 512],
                        start=True, stop=False, tile_position=(h * 64, 0)),
                        reads=[b_k, b_qs[c]], writes=[bsth[h]], tag="stMM")
                for h in range(2):
                    p.op("tensor", lambda t, h=h, pst=pst: t.matmul(
                        pst[:, h * 512:(h + 1) * 512], neg_b[h * 64:h * 64 + 1, :],
                        PB.t[h * 64:h * 64 + 1, g0:g0 + 512],
                        start=False, stop=True, tile_position=(h * 64, 0)),
                        reads=[bnegb, b_grow[gsl]], writes=[bsth[h]], war_free=True,
                        tag="subMM")
                for _ in range(2):
                    if nxt:
                        nxt.pop(0)()
                if kc > 1:
                    emit_pv(kc - 2)
                par3 = kc % 3
                p.op("scalar", lambda s, par3=par3, pst=pst: s.activation(
                    PB.t[:, PTof(par3):PTof(par3) + 1024],
                    pst[:, :], AF.Exp), reads=bsth, writes=[b_pt[par3]], tag="exp")
            emit_pv(NKC - 2)
            emit_pv(NKC - 1)
            rd = PF.view(o_rd, 1024, 64, 65)
            p.op("vector", lambda v: v.reciprocal(rd[:, 0:512], psPVa[64:65, :]),
                 reads=[bPVa], writes=[b_rd, b_big1], tag="recipA")
            p.op("vector", lambda v: v.reciprocal(rd[:, 512:1024], psPVb[64:65, :]),
                 reads=[bPVb], writes=[b_rd], war_free=True, tag="recipB")
            p.op("tensor", lambda t: t.matmul(
                psP1a[0:64, :], onesf_t[64:65, 0:64], rd[:, 0:512],
                start=True, stop=True), reads=[bonesf, b_rd], writes=[bP1a], tag="rbA")
            p.op("tensor", lambda t: t.matmul(
                psP1b[64:128, :], onesf_t[64:65, 0:64], rd[:, 512:1024],
                start=True, stop=True, tile_position=(64, 64)),
                reads=[bonesf, b_rd], writes=[bP1b], tag="rbB")
            p.op("vector", lambda v: v.tensor_copy(PF.view(o_rb, 512, 0, 64),
                 psP1a[0:64, :]), reads=[bP1a], writes=[b_rb, b_big1])
            p.op("vector", lambda v: v.tensor_copy(PF.view(o_rb, 512, 64, 128),
                 psP1b[64:128, :]), reads=[bP1b], writes=[b_rb], war_free=True)
            for h in range(2):
                psv, bpv = (psPVa, bPVa) if h == 0 else (psPVb, bPVb)
                p.op("vector", lambda v, h=h, psv=psv: v.scalar_tensor_tensor(
                    PF.view(o_st + 1024, 512, h * 64, (h + 1) * 64), psv[0:64, :], 1.0,
                    PF.view(o_rb, 512, h * 64, (h + 1) * 64),
                    op0=ALU.mult, op1=ALU.mult),
                    reads=[bpv, b_rb], writes=[b_stats],
                    war_free=(h == 1), tag="numdiv")
                p.op("vector", lambda v, h=h: v.tensor_add(
                    ao[c][h * 64:(h + 1) * 64, t0:t0 + 512],
                    PF.view(o_st + 1024, 512, h * 64, (h + 1) * 64),
                    q_s[c][h * 64:(h + 1) * 64, t0:t0 + 512]),
                    reads=[b_stats, b_qs[c]], writes=[b_ao[c]],
                    war_free=(h == 1), tag="aoadd")

        for hp in range(NP if not cfg.get("skip_attn") else 0):
            emit_kv(hp)
            for th in pass1_steps(hp, 0):
                th()
            for qt in range(NQT):
                nxt = pass1_steps(hp, qt + 1) if qt + 1 < NQT else []
                emit_pass2(hp, qt, nxt)
                for th in nxt:
                    th()

        p.dma("gpsimd", lambda g: g.dma_start(
            wsl_v, wo_d.rearrange("(n p) o -> p n o", p=128)),
            sem_w[0], writes=[b_ws, b_wsV] + b_pt + b_grow, tag="wo")
        for tt in range(NTC_O if not cfg.get("skip_attn") else 0):
            t0 = tt * 512
            p.dma("sync", lambda s, t0=t0: s.dma_start(
                PF.view(o_big2, ND * 512).rearrange("p (n t) -> p n t", t=512),
                fm(xT, t0, t0 + 512)), sem_big, writes=[b_big2], tag="xown")
            for o in range(ND):
                ps, bps = slot()
                for i in range(ND):
                    p.op("tensor", lambda t, i=i, o=o, t0=t0, ps=ps: t.matmul(
                        ps, wq_sl[i][:, o * 128:(o + 1) * 128], ao[i][:, t0:t0 + 512],
                        start=(i == 0), stop=(i == ND - 1)),
                        reads=[b_ws, b_ao[i]], writes=[bps], war_free=(i > 0), tag="woMM")
                p.op("scalar", lambda s, o=o, ps=ps: s.activation(
                    big1[o], ps, AF.Identity, bias=cc("bo", o), scale=1.0),
                    reads=[bps, bccol], writes=[b_big1], tag="woevac")
            emit_stats(cast_f32([big1[o] for o in range(ND)], [b_big1]))
            for o in range(ND):
                a, ab = asc[o % 2], b_asc[o % 2]
                p.op("vector", lambda v, o=o, a=a: v.tensor_sub(a, big1[o], mean_b),
                     reads=[b_big1, b_stats], writes=[ab])
                p.op("vector", lambda v, a=a: v.tensor_mul(a, a, rstd_b),
                     reads=[ab, b_stats], writes=[ab])
                p.op("vector", lambda v, o=o, a=a: v.tensor_add(big2[o], big2[o], a),
                     reads=[ab, b_big2], writes=[b_big2], tag="x1add")
            p.dma("sync", lambda s, t0=t0: s.dma_start(
                fm(x1d, t0, t0 + 512),
                PF.view(o_big2, ND * 512).rearrange("p (n t) -> p n t", t=512)),
                sem_big, reads=[b_big2], writes=[b_x1d], tag="x1st")

        if cfg.get("stop_after_A"):
            p.barrier("sync", [Tok("dma", sem=sem_big, idx=sem_big.count)])
            p.finalize(block)
            nc._layout = dict(o_xln=o_xln, o_qs=o_qs, o_ao=o_ao, o_k=o_k, o_v=o_v,
                              o_sc=o_sc, o_ws=o_ws,
                              o_st=o_st, o_big1=o_big1, o_big2=o_big2)
            return nc

        a_bufs = (b_xln + b_qs + b_ao + [b_k, b_kb1] + b_v + b_pt + b_grow +
                  [b_ws, b_wsV, b_xbs[0], b_xbs[1]])
        fence = p.op("vector", lambda v: v.engine_nop(), writes=a_bufs, tag="fence")

        h0 = [PB.view(o_h0 + j * TB, TB) for j in range(NJ)]
        h1v = [PB.view(o_h1 + j * TB, TB) for j in range(NJ)]
        xln2 = [PB.view(o_xln2 + i * TB, TB) for i in range(ND)]
        b_h0 = [p.buf(f"h0_{j}") for j in range(NJ)]
        b_h1 = [p.buf(f"h1_{j}") for j in range(NJ)]
        b_xln2 = [p.buf(f"xln2_{i}") for i in range(ND)]
        fsl = [PB.view(o_ws + s * 2048, 2048) for s in range(4)]
        b_fsl = [p.buf(f"fsl{s}") for s in range(4)]
        for b in b_h0 + b_h1 + b_xln2 + b_fsl:
            b.last_writer = fence
        _slot_pool[0] = [2, 3, 4, 5]

        def _stop(bufs):
            t = p.op("vector", lambda v: v.engine_nop(), reads=bufs, tag="stopper")
            p.barrier("sync", [t, Tok("dma", sem=sem_big, idx=sem_big.count)])
            p.finalize(block)
            nc._layout = dict(o_xln=o_xln, o_qs=o_qs, o_ao=o_ao, o_k=o_k, o_v=o_v,
                              o_sc=o_sc, o_ws=o_ws,
                              o_h0=o_h0, o_h1=o_h1, o_xln2=o_xln2,
                              o_st=o_st, o_big1=o_big1, o_big2=o_big2)
            return nc

        def emit_ln2(ft):
            tb0 = ft * TB
            for sb in range(NSB if not cfg.get("skip_ln2") else 0):
                t0 = tb0 + sb * 512
                p.dma("sync", lambda s, t0=t0: s.dma_start(
                    PF.view(o_big2, ND * 512).rearrange("p (n t) -> p n t", t=512),
                    fm(x1d, t0, t0 + 512)), sem_big, reads=[b_x1d],
                    writes=[b_big2], tag="x1ld")
                emit_stats(cast_f32([big2[i] for i in range(ND)], [b_big2]))
                for i in range(ND):
                    a, ab = asc[i % 2], b_asc[i % 2]
                    p.op("vector", lambda v, i=i, a=a: v.tensor_sub(a, big2[i], mean_b),
                         reads=[b_big2, b_stats], writes=[ab])
                    p.op("vector", lambda v, i=i, sb=sb, a=a: v.tensor_mul(
                        xln2[i][:, sb * 512:(sb + 1) * 512], a, rstd_b),
                        reads=[ab, b_stats], writes=[b_xln2[i]], tag="ln2ap")

        emit_ln2(0)
        for ft in range(NFT):
            tb0 = ft * TB
            if cfg.get("stop_after_LN2"):
                return _stop(b_xln2)
            for jg in range(DH // 256 if not cfg.get("s1_one_jg") else 1):
                wi = jg % 2
                p.dma("gpsimd", lambda g, jg=jg, wi=wi: g.dma_start(
                    fsl[wi].rearrange("p (n o) -> p n o", o=256),
                    w1t_d.rearrange("(n p) o -> p n o", p=128)[:, :, jg * 256:(jg + 1) * 256]),
                    sem_w[wi], writes=[b_fsl[wi]], tag="w1l")
                p.dma("gpsimd", lambda g, jg=jg, wi=wi: g.dma_start(
                    fsl[2 + wi].rearrange("p (n o) -> p n o", o=256),
                    w2t_d.rearrange("(n p) o -> p n o", p=128)[:, :, jg * 256:(jg + 1) * 256]),
                    sem_w[2 + wi], writes=[b_fsl[2 + wi]], tag="w2l")
                for jj in range(2):
                    j = jg * 2 + jj
                    for sb in range(NSB):
                        psa, bpsa = slot()
                        psb2, bpsb2 = slot()
                        for i in range(ND):
                            p.op("tensor", lambda t, i=i, jj=jj, sb=sb, psa=psa, wi=wi: t.matmul(
                                psa, fsl[wi][:, i * 256 + jj * 128:i * 256 + (jj + 1) * 128],
                                xln2[i][:, sb * 512:(sb + 1) * 512],
                                start=(i == 0), stop=(i == ND - 1)),
                                reads=[b_fsl[wi], b_xln2[i]], writes=[bpsa],
                                war_free=(i > 0), tag="w1MM")
                        for i in range(ND):
                            p.op("tensor", lambda t, i=i, jj=jj, sb=sb, psb2=psb2, wi=wi: t.matmul(
                                psb2, fsl[2 + wi][:, i * 256 + jj * 128:i * 256 + (jj + 1) * 128],
                                xln2[i][:, sb * 512:(sb + 1) * 512],
                                start=(i == 0), stop=(i == ND - 1)),
                                reads=[b_fsl[2 + wi], b_xln2[i]], writes=[bpsb2],
                                war_free=(i > 0), tag="w2MM")
                        _af1 = AF.Identity if cfg.get("no_silu") else (AF.Gelu if cfg.get("silu_as_gelu") else (AF.Sigmoid if cfg.get("silu_as_sigmoid") else AF.Silu))
                        if cfg.get("one_silu") and (jg, jj, sb) != (0, 0, 0):
                            _af1 = AF.Identity
                        _sc1 = 0.5 if cfg.get("silu_const_scale") else cc("w1s", j)
                        if cfg.get("silu_from_sbuf"):
                            p.op("scalar", lambda s, psa=psa: s.activation(
                                sc16[2][:, :], psa, AF.Identity, bias=0.0, scale=1.0),
                                reads=[bpsa], writes=[b_sc16[2], b_kb1], tag="pre")
                            p.op("scalar", lambda s, j=j, _af1=_af1, _sc1=_sc1: s.activation(
                                sc16[0][:, :], sc16[2][:, :], _af1, bias=0.0, scale=_sc1),
                                reads=[b_sc16[2], bccol], writes=[b_sc16[0], b_kb1], tag="silu")
                        else:
                            p.op("scalar", lambda s, j=j, psa=psa, _af1=_af1, _sc1=_sc1: s.activation(
                                sc16[(j % 2) * 2][:, :], psa, _af1, bias=0.0, scale=_sc1),
                                reads=[bpsa, bccol], writes=[b_sc16[(j % 2) * 2], b_kb1], tag="silu")
                        _af2 = AF.Silu if cfg.get("w2_as_silu") else AF.Identity
                        p.op("scalar", lambda s, j=j, psb2=psb2, _af2=_af2: s.activation(
                            sc16[(j % 2) * 2 + 1][:, :], psb2, _af2, bias=0.0, scale=cc("w2s", j)),
                            reads=[bpsb2, bccol], writes=[b_sc16[(j % 2) * 2 + 1], b_kb1], tag="w2sc")
                        p.op("vector", lambda v, j=j, sb=sb: v.tensor_mul(
                            h0[j][:, sb * 512:(sb + 1) * 512], sc16[(j % 2) * 2][:, :], sc16[(j % 2) * 2 + 1][:, :]),
                            reads=[b_sc16[(j % 2) * 2], b_sc16[(j % 2) * 2 + 1]], writes=[b_h0[j]], tag="h0mul")
            if cfg.get("stop_after_S1"):
                return _stop(b_h0)
            if ft + 1 < NFT:
                emit_ln2(ft + 1)
            for (wt_d, scn, src, bsrc, dst, bdst) in (
                    (h1t_d, "h1s", h0, b_h0, h1v, b_h1),
                    (h2t_d, "h2s", h1v, b_h1, h0, b_h0)):
                for j2 in range(NJ):
                    wi = j2 % 2
                    for qd in range(4):
                        jq0 = qd * NJ // 4
                        p.dma("gpsimd", lambda g, j2=j2, wi=wi, wt_d=wt_d, jq0=jq0: g.dma_start(
                            PB.view(o_ws + wi * 4096 + jq0 * 128, NJ * 32).rearrange(
                                "p (n o) -> p n o", o=128),
                            wt_d.rearrange("(n p) o -> p n o", p=128)[
                                :, jq0:jq0 + NJ // 4, j2 * 128:(j2 + 1) * 128]),
                            sem_w[wi], writes=[b_fsl[2 * wi], b_fsl[2 * wi + 1]],
                            war_free=(qd > 0), tag="hwl")
                    for sb in range(NSB):
                        ps, bps = slot()
                        for j in range(NJ):
                            p.op("tensor", lambda t, j=j, sb=sb, ps=ps, wi=wi, src=src: t.matmul(
                                ps, PB.view(o_ws + wi * 4096 + j * 128, 128),
                                src[j][:, sb * 512:(sb + 1) * 512],
                                start=(j == 0), stop=(j == NJ - 1)),
                                reads=[b_fsl[2 * wi], b_fsl[2 * wi + 1], bsrc[j]],
                                writes=[bps], war_free=(j > 0), tag="hMM")
                        p.op("scalar", lambda s, j2=j2, sb=sb, ps=ps, scn=scn, dst=dst,
                             bdst=bdst: s.activation(
                            dst[j2][:, sb * 512:(sb + 1) * 512], ps, AF.Gelu,
                            bias=0.0, scale=cc(scn, j2)),
                            reads=[bps, bccol], writes=[bdst[j2]], tag="gelu")
            if cfg.get("stop_after_S3"):
                return _stop(b_h0 + b_h1)
            for sb in range(NSB):
                t0 = tb0 + sb * 512
                p.dma("sync", lambda s, t0=t0: s.dma_start(
                    PF.view(o_big2, ND * 512).rearrange("p (n t) -> p n t", t=512),
                    fm(x1d, t0, t0 + 512)), sem_big, reads=[b_x1d],
                    writes=[b_big2], tag="x1ld2")
                for o in range(ND):
                    wi = o % 2
                    for qd in range(4):
                        jq0 = qd * NJ // 4
                        p.dma("gpsimd", lambda g, o=o, wi=wi, jq0=jq0: g.dma_start(
                            PB.view(o_ws + wi * 4096 + jq0 * 128, NJ * 32).rearrange(
                                "p (n o2) -> p n o2", o2=128),
                            w3t_d.rearrange("(n p) o2 -> p n o2", p=128)[
                                :, jq0:jq0 + NJ // 4, o * 128:(o + 1) * 128]),
                            sem_w[wi], writes=[b_fsl[2 * wi], b_fsl[2 * wi + 1]],
                            war_free=(qd > 0), tag="w3l")
                    ps, bps = slot()
                    for j in range(NJ):
                        p.op("tensor", lambda t, j=j, sb=sb, ps=ps, wi=wi: t.matmul(
                            ps, PB.view(o_ws + wi * 4096 + j * 128, 128),
                            h0[j][:, sb * 512:(sb + 1) * 512],
                            start=(j == 0), stop=(j == NJ - 1)),
                            reads=[b_fsl[2 * wi], b_fsl[2 * wi + 1], b_h0[j]],
                            writes=[bps], war_free=(j > 0), tag="w3MM")
                    p.op("scalar", lambda s, o=o, ps=ps: s.activation(
                        big1[o], ps, AF.Identity, bias=0.0, scale=cc("w3s", o)),
                        reads=[bps, bccol], writes=[b_big1], tag="w3evac")
                emit_stats(cast_f32([big1[o] for o in range(ND)], [b_big1]))
                for o in range(ND):
                    a, ab = asc[o % 2], b_asc[o % 2]
                    p.op("vector", lambda v, o=o, a=a: v.tensor_sub(a, big1[o], mean_b),
                         reads=[b_big1, b_stats], writes=[ab])
                    p.op("vector", lambda v, a=a: v.tensor_mul(a, a, rstd_b),
                         reads=[ab, b_stats], writes=[ab])
                    p.op("vector", lambda v, o=o, a=a: v.tensor_add(big2[o], big2[o], a),
                         reads=[ab, b_big2], writes=[b_big2], tag="x2add")
                p.dma("sync", lambda s, t0=t0: s.dma_start(
                    fm(outT, t0, t0 + 512),
                    PF.view(o_big2, ND * 512).rearrange("p (n t) -> p n t", t=512)),
                    sem_st, reads=[b_big2], tag="outst")

        p.barrier("sync", [Tok("dma", sem=sem_st, idx=sem_st.count)])
        p.finalize(block)
    nc._layout = dict(o_xln=o_xln, o_qs=o_qs, o_ao=o_ao, o_k=o_k, o_v=o_v,
                      o_sc=o_sc, o_ws=o_ws,
                      o_h0=o_h0, o_h1=o_h1, o_xln2=o_xln2,
                      o_st=o_st, o_big1=o_big1, o_big2=o_big2)
    return nc


_CACHE = {}


def _get_nc(cfg_key, cfg):
    if cfg_key not in _CACHE:
        _CACHE[cfg_key] = build_program(cfg)
    return _CACHE[cfg_key]


def make_host_inputs(cfg, x, Wq, bq, Wk, bk, Wv, bv, Wo, bo, ln_g, ln_b,
                     w1_w, w1_s, w2_w, w2_s, h1_w, h1_s, h2_w, h2_s, w3_w, w3_s):
    D, To, Tf = cfg["D"], cfg["To"], cfg["Tf"]
    B = x.shape[0]
    f = np.float32
    ac = np.ascontiguousarray

    def col8(vec, n):
        return vec.reshape(n, 128).T.astype(f)

    ccol = np.zeros((128, NCC), f)
    ccol[:, CC["bq"]:CC["bq"] + 8] = col8(bq, 8)
    ccol[:, CC["bk"]:CC["bk"] + 8] = col8(bk, 8)
    ccol[:, CC["bo"]:CC["bo"] + 8] = col8(bo, 8)
    ccol[:, CC["lng"]:CC["lng"] + 8] = col8(ln_g, 8)
    ccol[:, CC["lnb"]:CC["lnb"] + 8] = col8(ln_b, 8)
    ccol[:, CC["w1s"]:CC["w1s"] + 32] = col8(w1_s, 32)
    ccol[:, CC["w2s"]:CC["w2s"] + 32] = col8(w2_s, 32)
    ccol[:, CC["h1s"]:CC["h1s"] + 32] = col8(h1_s, 32)
    ccol[:, CC["h2s"]:CC["h2s"] + 32] = col8(h2_s, 32)
    ccol[:, CC["w3s"]:CC["w3s"] + 8] = col8(w3_s, 8)
    ccol[:, CC["eps"]] = 1e-5
    crow = np.concatenate([bv.astype(f), np.full(128, -1.0, f)])[None, :]
    shared = dict(
        wq=ac(Wq.astype(f)), wk=ac(Wk.astype(f)), wv=ac(Wv.astype(f)),
        wo=ac(Wo.astype(f)),
        w1t=ac(w1_w.T.astype(f)), w2t=ac(w2_w.T.astype(f)),
        h1t=ac(h1_w.T.astype(f)), h2t=ac(h2_w.T.astype(f)),
        w3t=ac(w3_w.T.astype(f)),
        ccol=ccol, crow=ac(crow.astype(f)),
        cones=np.ones((128, 128), f),
    )
    in_maps = []
    for b in range(B):
        xt = ac(x[b].T.astype(f))
        for hh in range(2):
            if hh == 0:
                xcore = xt
            else:
                xcore = ac(np.concatenate([xt[:, To:], xt[:, :To]], axis=1))
            in_maps.append(dict(shared, xT=xcore))
    return in_maps


def kernel(L0_attn_x, L0_attn_Wq, L0_attn_bq, L0_attn_Wk, L0_attn_bk,
           L0_attn_Wv, L0_attn_bv, L0_attn_Wo, L0_attn_bo,
           L0_attn_ln_g, L0_attn_ln_b,
           L0_ffn_w1_w, L0_ffn_w1_s, L0_ffn_w2_w, L0_ffn_w2_s,
           L0_ffn_h1_w, L0_ffn_h1_s, L0_ffn_h2_w, L0_ffn_h2_s,
           L0_ffn_w3_w, L0_ffn_w3_s):
    cfg = default_cfg()
    x = np.asarray(L0_attn_x)
    B, S, D = x.shape
    To = cfg["To"]
    nc = _get_nc("full", cfg)
    in_maps = make_host_inputs(
        cfg, x, L0_attn_Wq, L0_attn_bq, L0_attn_Wk, L0_attn_bk,
        L0_attn_Wv, L0_attn_bv, L0_attn_Wo, L0_attn_bo,
        L0_attn_ln_g, L0_attn_ln_b,
        L0_ffn_w1_w, L0_ffn_w1_s, L0_ffn_w2_w, L0_ffn_w2_s,
        L0_ffn_h1_w, L0_ffn_h1_s, L0_ffn_h2_w, L0_ffn_h2_s,
        L0_ffn_w3_w, L0_ffn_w3_s)
    res = run_bass_kernel_spmd(nc, in_maps, core_ids=list(range(8)))
    out = np.empty((B, S, D), np.float32)
    for b in range(B):
        out[b, :To, :] = res.results[2 * b]["outT"].T
        out[b, To:, :] = res.results[2 * b + 1]["outT"].T
    return out
